# version 06
# baseline (speedup 1.0000x reference)
import contextlib
import numpy as np
import ml_dtypes
import concourse.bass as bass
import concourse.mybir as mybir
from concourse.bass_utils import run_bass_kernel_spmd

F32 = mybir.dt.float32
BF16 = mybir.dt.bfloat16
I32 = mybir.dt.int32
U8 = mybir.dt.uint8
AF = mybir.ActivationFunctionType
ALU = mybir.AluOpType
AX = mybir.AxisListType

BF = ml_dtypes.bfloat16

NCORE = 8
B, SEQ, D = 4, 4096, 2048
TOK = 2048
NCH = D // 128

LN_EPS = 1e-5
F32_EPS = float(np.finfo(np.float32).eps)

ENGS = ("T", "V", "S", "G", "Y")


class KB:

    def __init__(self):
        self.nc = bass.Bass("TRN2", target_bir_lowering=False, debug=False)
        self.es = contextlib.ExitStack()
        self.es.__enter__()
        self.sems = {}
        dma_keys = tuple(f"GD{i}" for i in range(8)) + tuple(f"YD{i}" for i in range(8))
        for e in ENGS + dma_keys + ("CC",):
            self.sems[e] = self.es.enter_context(self.nc.semaphore("sp_" + e))
        self.cnt = {kk: 0 for kk in self.sems}
        self.dma_rr = {"G": 0, "Y": 0}
        self.hw = {e: {} for e in ENGS}
        self.writer = {}
        self.readers = {}
        self.queue = {e: [] for e in ENGS}

    def sb(self, name, shape, dt=F32):
        return self.es.enter_context(self.nc.sbuf_tensor(name, list(shape), dt)).ap()

    def psum(self, name, shape=(128, 512), dt=F32):
        return self.es.enter_context(self.nc.psum_tensor(name, list(shape), dt)).ap()

    def dram_in(self, name, shape, dt=F32):
        return self.nc.dram_tensor(name, list(shape), dt, kind="ExternalInput").ap()

    def dram_out(self, name, shape, dt=F32):
        return self.nc.dram_tensor(name, list(shape), dt, kind="ExternalOutput").ap()

    def dram_tmp(self, name, shape, dt=F32):
        return self.nc.dram_tensor(name, list(shape), dt, kind="Internal").ap()

    def op(self, eng, fn, reads=(), writes=(), dma=False):
        assert eng in ENGS
        if dma:
            pfx = {"G": "GD", "Y": "YD"}[eng]
            semkey = f"{pfx}{self.dma_rr[eng] % 8}"
            self.dma_rr[eng] += 1
        else:
            semkey = eng
        waits = {}

        def need(wkey, val, wdma):
            if (not dma) and (not wdma) and wkey == eng == "T":
                return
            if dma and (not wdma) and wkey == eng:
                return
            if waits.get(wkey, 0) < val:
                waits[wkey] = val

        for r in reads:
            w = self.writer.get(r)
            if w is not None:
                need(*w)
        for r in writes:
            w = self.writer.get(r)
            if w is not None:
                need(*w)
            for rk, (rv, rdma) in self.readers.get(r, {}).items():
                need(rk, rv, rdma)

        hw = self.hw[eng]
        final = []
        for kk, v in sorted(waits.items()):
            if hw.get(kk, 0) >= v:
                continue
            hw[kk] = v
            final.append((self.sems[kk], v))

        inc = 16 if dma else 1
        self.cnt[semkey] += inc
        val = self.cnt[semkey]
        semh = self.sems[semkey]

        def run(e, final=final, fn=fn, semh=semh, inc=inc):
            for s, v in final:
                e.wait_ge(s, v)
            fn(e).then_inc(semh, inc)

        self.queue[eng].append(run)

        for r in writes:
            self.writer[r] = (semkey, val, dma)
            self.readers[r] = {}
        for r in reads:
            self.readers.setdefault(r, {})[semkey] = (val, dma)
        return (semkey, val)

    def fence(self):
        snap = {kk: v for kk, v in self.cnt.items() if v and kk != "CC"}
        for e in ENGS:
            hw = self.hw[e]
            waits = []
            for kk, v in sorted(snap.items()):
                if kk == e:
                    continue
                if hw.get(kk, 0) >= v:
                    continue
                hw[kk] = v
                waits.append((self.sems[kk], v))
            if waits:
                def runf(eng, waits=waits):
                    for sh, v in waits:
                        eng.wait_ge(sh, v)
                self.queue[e].append(runf)

    def cc(self, kind, op_, rgroups, ins_aps, outs_aps, reads=(), writes=()):
        waits = {}
        for r in reads:
            w = self.writer.get(r)
            if w is not None and waits.get(w[0], 0) < w[1]:
                waits[w[0]] = w[1]
        for r in writes:
            w = self.writer.get(r)
            if w is not None and waits.get(w[0], 0) < w[1]:
                waits[w[0]] = w[1]
            for rk, (rv, rdma) in self.readers.get(r, {}).items():
                if waits.get(rk, 0) < rv:
                    waits[rk] = rv
        hw = self.hw["G"]
        final = []
        for kk, v in sorted(waits.items()):
            if hw.get(kk, 0) >= v:
                continue
            hw[kk] = v
            final.append((self.sems[kk], v))
        self.cnt["CC"] += 1
        val = self.cnt["CC"]
        semh = self.sems["CC"]

        def run(e, final=final):
            for sh, v in final:
                e.wait_ge(sh, v)
            e.collective_compute(kind, op_, replica_groups=rgroups,
                                 ins=[a.opt() for a in ins_aps],
                                 outs=[a.opt() for a in outs_aps],
                                 ).then_inc(semh, 1)
        self.queue["G"].append(run)
        for r in writes:
            self.writer[r] = ("CC", val, False)
            self.readers[r] = {}
        for r in reads:
            self.readers.setdefault(r, {})["CC"] = (val, False)
        return val

    def finish(self):
        block = self.es.enter_context(self.nc.Block())
        engmap = {"T": block.tensor, "V": block.vector, "S": block.scalar,
                  "G": block.gpsimd, "Y": block.sync}
        sems = self.sems
        dma_finals = [(sems[kk], self.cnt[kk]) for kk in self.cnt
                      if (kk.startswith("GD") or kk.startswith("YD") or kk == "CC")
                      and self.cnt[kk]]

        for e in ENGS:
            q = self.queue[e]
            if e == "G":
                def runall(eng, q=q, dma_finals=dma_finals):
                    for fn in q:
                        fn(eng)
                    for sh, v in dma_finals:
                        eng.wait_ge(sh, v)
                engmap[e](runall)
            elif q:
                def runall(eng, q=q):
                    for fn in q:
                        fn(eng)
                engmap[e](runall)
        self.es.__exit__(None, None, None)
        return self.nc


_ESZ = {F32: 4, BF16: 2, I32: 4, U8: 1, mybir.dt.float16: 2}


class Arena:

    def __init__(self, k, name, nbytes):
        self.nbytes = nbytes
        self.th = k.es.enter_context(k.nc.sbuf_tensor(name, [128, nbytes // 2], BF16))
        self.views = {BF16: self.th}
        self.off = 0
        self.peak = 0

    def get(self, name, shape, dt=F32):
        p, n = shape
        esz = _ESZ[dt]
        off = (self.off + 31) & ~31
        self.off = off + n * esz
        self.peak = max(self.peak, self.off)
        assert self.off <= self.nbytes, f"arena overflow at {name}: {self.off}"
        if dt not in self.views:
            self.views[dt] = self.th.bitcast(dt)
        v = self.views[dt]
        e0 = off // esz
        return v.ap()[0:p, e0:e0 + n]

    def reset(self):
        self.off = 0

    def get2(self, name, shape, dt_a, dt_b):
        p, n = shape
        off = (self.off + 31) & ~31
        self.off = off + n * 4
        self.peak = max(self.peak, self.off)
        assert self.off <= self.nbytes, f"arena overflow at {name}"
        outs = []
        for dt in (dt_a, dt_b):
            if dt not in self.views:
                self.views[dt] = self.th.bitcast(dt)
            outs.append(self.views[dt].ap()[0:p, off // 4: off // 4 + n])
        return outs


class Pacer:

    def __init__(self, chunks, nslots):
        self.chunks = chunks
        self.n = max(1, nslots)
        self.i = 0
        self.done = 0

    def __call__(self):
        self.i += 1
        tgt = min((self.i * len(self.chunks) + self.n - 1) // self.n,
                  len(self.chunks))
        while self.done < tgt:
            self.chunks[self.done]()
            self.done += 1

    def flush(self):
        while self.done < len(self.chunks):
            self.chunks[self.done]()
            self.done += 1


class _SbAlloc:

    def __init__(self, k, ar=None):
        self.k, self.ar = k, ar

    def __call__(self, name, shape, dt=F32):
        if self.ar is not None:
            return self.ar.get(name, shape, dt)
        return self.k.sb(name, shape, dt)



def alloc_common(k):
    c = {}
    c["ones16"] = k.sb("c_ones16", [128, 1], BF16)
    c["ones1_32"] = k.sb("c_ones1_32", [1, 128], F32)
    c["ones_col32"] = k.sb("c_ones_col32", [128, 1], F32)
    c["idn32"] = k.sb("c_idn32", [128, 128], F32)
    c["idn16"] = k.sb("c_idn16", [128, 128], BF16)
    c["kv"] = k.sb("c_kv", [128, 8], F32)
    pbig = k.psum("pbig", (128, 4096))
    c["pbig"] = pbig
    c["pb"] = [pbig[:, i * 512:(i + 1) * 512] for i in range(8)]
    return c


def load_consts(k, c, dr):
    for nm in ("ones16", "ones1_32", "idn32", "idn16", "kv"):
        k.op("Y", lambda y, nm=nm: y.dma_start(out=c[nm], in_=dr[nm]),
             writes=[nm], dma=True)
    k.op("V", lambda v: v.memset(c["ones_col32"], 1.0), writes=["ones_col32"])


KVALS = [LN_EPS, F32_EPS, 1e-6, 0.0, 0.0, 0.0, 0.0, 0.0]


def const_host():
    return {
        "ones16": np.ones((128, 1), dtype=BF),
        "ones1_32": np.ones((1, 128), dtype=np.float32),
        "idn32": np.eye(128, dtype=np.float32),
        "idn16": np.eye(128).astype(BF),
        "kv": np.tile(np.asarray(KVALS, np.float32), (128, 1)),
    }



def emit_ln_half(k, c, f, xr, th, with_gate=False):
    pb = c["pb"]
    for ch in range(NCH):
        q = ch % 2
        k.op("Y", lambda y, ch=ch, q=q: y.dma_start(
            out=f["xc"][q], in_=xr[:, ch, th:th + 512]),
            writes=[f"xc{q}"], dma=True)
        k.op("S", lambda s, ch=ch, q=q: s.activation(
            f["sq16c"][q], f["xc"][q], AF.Square),
            reads=[f"xc{q}"], writes=[f"sq16c{q}"])
        k.op("V", lambda v, ch=ch, q=q: v.tensor_copy(
            f["x16c"][q], f["xc"][q]),
            reads=[f"xc{q}"], writes=[f"x16c{q}"])

        def fmm(t, ch=ch, q=q):
            t.matmul(pb[0][0:1, :], c["ones16"], f["x16c"][q],
                     start=(ch == 0), stop=(ch == NCH - 1))
            last = t.matmul(pb[1][0:1, :], c["ones16"], f["sq16c"][q],
                            start=(ch == 0), stop=(ch == NCH - 1))
            if with_gate:
                last = t.matmul(pb[2][0:8, :], f["gw_sb"][:, ch * 8:(ch + 1) * 8],
                                f["xc"][q],
                                start=(ch == 0), stop=(ch == NCH - 1))
            return last
        wr = ["pb0", "pb1"] + (["pb2"] if with_gate else [])
        k.op("T", fmm, reads=[f"x16c{q}", f"sq16c{q}", f"xc{q}", "ones16"]
             + (["gw_sb"] if with_gate else []), writes=wr)

    sr = f["statrow"]

    def fstat_a(v):
        v.tensor_scalar(sr[0:1, 0:512], pb[0][0:1, :], 1.0 / D, None, ALU.mult)
        return v.tensor_scalar(sr[0:1, 512:1024], pb[1][0:1, :], 1.0 / D, None, ALU.mult)
    k.op("V", fstat_a, reads=["pb0", "pb1"], writes=["statrow"])
    k.op("V", lambda v: v.tensor_tensor(sr[0:1, 1024:1536], sr[0:1, 0:512],
                                        sr[0:1, 0:512], ALU.mult),
         reads=["statrow"], writes=["statrow"])
    k.op("V", lambda v: v.tensor_tensor(sr[0:1, 512:1024], sr[0:1, 512:1024],
                                        sr[0:1, 1024:1536], ALU.subtract),
         reads=["statrow"], writes=["statrow"])

    k.op("S", lambda s: s.activation(sr[0:1, 1024:1536], sr[0:1, 512:1024],
                                     AF.Sqrt, bias=c["kv"][0:1, 0:1]),
         reads=["statrow", "kv"], writes=["statrow"])

    k.op("V", lambda v: v.reciprocal(sr[0:1, 1024:1536], sr[0:1, 1024:1536]),
         reads=["statrow"], writes=["statrow"])
    k.op("V", lambda v: v.tensor_tensor(sr[0:1, 1536:2048], sr[0:1, 0:512],
                                        sr[0:1, 1024:1536], ALU.mult),
         reads=["statrow"], writes=["statrow"])
    k.op("V", lambda v: v.tensor_scalar_mul(sr[0:1, 1536:2048],
                                            sr[0:1, 1536:2048], -1.0),
         reads=["statrow"], writes=["statrow"])

    def fbc(t):
        t.matmul(pb[3][:, :], c["ones1_32"], sr[0:1, 1024:1536], start=True, stop=True)
        return t.matmul(pb[4][:, :], c["ones1_32"], sr[0:1, 1536:2048],
                        start=True, stop=True)
    k.op("T", fbc, reads=["statrow", "ones1_32"], writes=["pb3", "pb4"])



NE = 8
TT = 1024
NTILE = TOK // TT


def alloc_ffn0(k, ar=None):
    sb = _SbAlloc(k, ar)
    f = {}
    f["xc"] = [sb(f"f0_xc{i}", [128, 512], F32) for i in range(2)]
    f["xn16"] = sb("f0_xn16", [128, NCH * TT], BF16)
    f["acc"] = sb("f0_acc", [128, NCH * TT], F32)
    f["hbuf"] = sb("f0_hbuf", [128, 11 * TT], BF16)
    f["wslot"] = [sb(f"f0_ws{i}", [128, 2048], BF16) for i in range(4)]
    f["sg"] = [sb(f"f0_sg{i}", [128, 512], F32) for i in range(4)]
    f["htmp"] = [sb(f"f0_ht{i}", [128, 512], F32) for i in range(2)]
    f["x16c"] = [sb(f"f0_x16c{i}", [128, 512], BF16) for i in range(2)]
    f["sq16c"] = [sb(f"f0_sq16c{i}", [128, 512], BF16) for i in range(2)]
    f["statrow"] = sb("f0_stat", [1, 2048], F32)
    f["outc"] = [sb(f"f0_outc{i}", [128, 512], F32) for i in range(2)]
    f["gw_sb"] = sb("f0_gw", [128, NCH * 8], F32)
    f["gsum"] = sb("f0_gsum", [8, 1], F32)
    f["gbias8"] = sb("f0_gbias8", [128, 8], F32)
    f["grow"] = sb("f0_grow", [8, 512], F32)
    f["rt"] = sb("f0_rt", [128, 256], F32)
    f["weT"] = sb("f0_weT", [8, TOK], F32)
    f["weRow"] = sb("f0_weRow", [1, TT], F32)
    f["Be"] = sb("f0_Be", [128, TT], F32)
    return f


def emit_ffn0_init(k, c, f, io):
    pb = c["pb"]
    k.op("Y", lambda y: y.dma_start(out=f["gw_sb"], in_=io["gw_l"]),
         writes=["gw_sb"], dma=True)
    k.op("Y", lambda y: y.dma_start(out=f["rt"][0:1, 0:8], in_=io["gbias_row"]),
         writes=["rt"], dma=True)

    def fgs(t):
        for ch in range(NCH):
            last = t.matmul(pb[7][0:8, 0:1], f["gw_sb"][:, ch * 8:(ch + 1) * 8],
                            c["ones_col32"], start=(ch == 0), stop=(ch == NCH - 1))
        return last
    k.op("T", fgs, reads=["gw_sb", "ones_col32"], writes=["pb7"])
    k.op("V", lambda v: v.tensor_copy(f["gsum"], pb[7][0:8, 0:1]),
         reads=["pb7"], writes=["gsum"])

    k.op("T", lambda t: t.matmul(pb[7][:, 0:8], c["ones1_32"], f["rt"][0:1, 0:8],
                                 start=True, stop=True),
         reads=["rt", "ones1_32"], writes=["pb7"])
    k.op("V", lambda v: v.tensor_copy(f["gbias8"], pb[7][:, 0:8]),
         reads=["pb7"], writes=["gbias8"])


def emit_routing_ffn0(k, c, f, toff):
    pb = c["pb"]
    rt = f["rt"]

    def ftr(t):
        for sbk in range(4):
            last = t.transpose(pb[5][:, sbk * 8:(sbk + 1) * 8],
                               f["grow"][0:8, sbk * 128:(sbk + 1) * 128],
                               c["idn32"][0:8, 0:8])
        return last
    k.op("T", ftr, reads=["grow", "idn32"], writes=["pb5"])

    O = 64
    SL = {"s": 0, "tb": 8, "m1": 16, "m2": 17, "dn": 18, "mk1": 24, "mk2": 32,
          "t2": 40, "mkf": 48, "w": 56}

    def sl(nm, sbk, wid=8):
        b = sbk * O + SL[nm]
        return rt[:, b:b + wid]

    def step(fn):
        def g(v):
            for sbk in range(4):
                last = fn(v, sbk)
            return last
        k.op("V", g, reads=["pb5", "gbias8", "rt"], writes=["rt"])

    step(lambda v, i: v.tensor_copy(sl("s", i), pb[5][:, i * 8:(i + 1) * 8]))
    step(lambda v, i: v.tensor_tensor(sl("tb", i), sl("s", i), f["gbias8"], ALU.add))
    step(lambda v, i: v.tensor_reduce(sl("m1", i, 1), sl("tb", i), AX.X, ALU.max))
    step(lambda v, i: v.tensor_scalar(sl("mk1", i), sl("tb", i), sl("m1", i, 1), None, ALU.is_equal))
    step(lambda v, i: v.scalar_tensor_tensor(sl("t2", i), sl("mk1", i), -1e30, sl("tb", i), ALU.mult, ALU.add))
    step(lambda v, i: v.tensor_reduce(sl("m2", i, 1), sl("t2", i), AX.X, ALU.max))
    step(lambda v, i: v.tensor_scalar(sl("mk2", i), sl("t2", i), sl("m2", i, 1), None, ALU.is_equal))
    step(lambda v, i: v.tensor_tensor(sl("mkf", i), sl("mk1", i), sl("mk2", i), ALU.add))
    step(lambda v, i: v.tensor_tensor(sl("w", i), sl("s", i), sl("mkf", i), ALU.mult))
    step(lambda v, i: v.tensor_reduce(sl("dn", i, 1), sl("w", i), AX.X, ALU.add))
    step(lambda v, i: v.reciprocal(sl("dn", i, 1), sl("dn", i, 1)))
    step(lambda v, i: v.tensor_scalar(sl("w", i), sl("w", i), sl("dn", i, 1), None, ALU.mult))

    def ftb(t):
        for sbk in range(4):
            last = t.transpose(pb[6][0:8, sbk * 128:(sbk + 1) * 128],
                               rt[:, sbk * 64 + 56:sbk * 64 + 64], c["idn32"])
        return last
    k.op("T", ftb, reads=["rt", "idn32"], writes=["pb6"])

    k.op("V", lambda v: v.tensor_copy(f["weT"][:, toff:toff + 512], pb[6][0:8, 0:512]),
         reads=["pb6"], writes=["weT"])


def emit_ffn0(k, c, f, io, xdram, ydram, uid, filler=None):
    pb = c["pb"]
    sgw, sdw, upw, dww = io["sgw_l"], io["sdw_l"], io["upw_l"], io["dww_l"]
    xr = xdram.rearrange("(c p) t -> p c t", p=128)

    for it in range(NTILE):
        t0 = it * TT
        for hf in range(2):
            th = t0 + hf * 512
            if filler:
                filler()
            emit_ln_half(k, c, f, xr, th, with_gate=True)

            for ch in range(NCH):
                q = ch % 2
                k.op("Y", lambda y, ch=ch, q=q, th=th: y.dma_start(
                    out=f["xc"][q], in_=xr[:, ch, th:th + 512]),
                    writes=[f"xc{q}"], dma=True)

                k.op("V", lambda v, ch=ch, q=q: v.tensor_tensor(
                    f["htmp"][q], f["xc"][q], pb[3][:, :], ALU.mult),
                    reads=[f"xc{q}", "pb3"], writes=[f"htmp{q}"])

                def fxn(v, ch=ch, q=q, hf=hf):
                    dst = f["xn16"][:, ch * TT + hf * 512: ch * TT + hf * 512 + 512]
                    return v.tensor_tensor(dst, f["htmp"][q], pb[4][:, :], ALU.add)
                k.op("V", fxn, reads=[f"htmp{q}", "pb4"],
                     writes=[f"xn16h{hf}c{ch}"])

            k.op("V", lambda v: v.tensor_copy(f["grow"], pb[2][0:8, :]),
                 reads=["pb2"], writes=["grow"])
            k.op("V", lambda v: v.tensor_tensor(f["grow"], f["grow"],
                                                pb[3][0:8, :], ALU.mult),
                 reads=["grow", "pb3"], writes=["grow"])
            k.op("V", lambda v: v.scalar_tensor_tensor(
                f["grow"], pb[4][0:8, :], f["gsum"], f["grow"], ALU.mult, ALU.add),
                 reads=["pb4", "gsum", "grow"], writes=["grow"])
            k.op("S", lambda s: s.activation(f["grow"], f["grow"], AF.Sigmoid),
                 reads=["grow"], writes=["grow"])

            emit_routing_ffn0(k, c, f, toff=t0 + hf * 512)

        groups = [("shA", None), ("shB", None)] + [("ex", e) for e in range(NE)]
        for gi, (kind, e) in enumerate(groups):
            if kind == "ex":
                def fwr(g, e=e, t0=t0):
                    return g.dma_start(out=f["weRow"],
                                       in_=f["weT"][e:e + 1, t0:t0 + TT])
                k.op("G", fwr, reads=["weT"], writes=["weRow"], dma=True)

                def fbe(t):
                    t.matmul(pb[5][:, :], c["ones1_32"],
                             f["weRow"][0:1, 0:512], start=True, stop=True)
                    return t.matmul(pb[6][:, :], c["ones1_32"],
                                    f["weRow"][0:1, 512:TT],
                                    start=True, stop=True)
                k.op("T", fbe, reads=["weRow", "ones1_32"], writes=["pb5", "pb6"])

                def fbe2(v):
                    v.tensor_copy(f["Be"][:, 0:512], pb[5][:, :])
                    return v.tensor_copy(f["Be"][:, 512:TT], pb[6][:, :])
                k.op("V", fbe2, reads=["pb5", "pb6"], writes=["Be"])

            for j in range(11):
                if filler:
                    filler()
                q = j % 2
                bb = 4 * q
                for half, slot in ((0, 2 * q), (1, 2 * q + 1)):
                    if kind == "shA":
                        src = sgw[j + 22 * half]
                    elif kind == "shB":
                        src = sgw[11 + j + 22 * half]
                    else:
                        src = upw[e, j + 11 * half]
                    k.op("G", lambda g, src=src, slot=slot: g.dma_start(
                        out=f["wslot"][slot], in_=src),
                        writes=[f"ws{slot}"], dma=True)

                def fup(t, q=q, bb=bb):
                    for ts in range(2):
                        for ch in range(NCH):
                            t.matmul(pb[bb + ts][:, :],
                                     f["wslot"][2 * q][:, ch * 128:(ch + 1) * 128],
                                     f["xn16"][:, ch * TT + ts * 512:ch * TT + (ts + 1) * 512],
                                     start=(ch == 0), stop=(ch == NCH - 1))
                    for ts in range(2):
                        for ch in range(NCH):
                            last = t.matmul(pb[bb + 2 + ts][:, :],
                                            f["wslot"][2 * q + 1][:, ch * 128:(ch + 1) * 128],
                                            f["xn16"][:, ch * TT + ts * 512:ch * TT + (ts + 1) * 512],
                                            start=(ch == 0), stop=(ch == NCH - 1))
                    return last
                xn_deps = [f"xn16h{h}c{ch}" for h in range(2) for ch in range(NCH)]
                k.op("T", fup, reads=[f"ws{2*q}", f"ws{2*q+1}"] + xn_deps,
                     writes=[f"pb{bb}", f"pb{bb+1}", f"pb{bb+2}", f"pb{bb+3}"])

                silu_off = 2 if kind in ("shA", "shB") else 0
                lin_off = 2 - silu_off
                for ts in range(2):
                    k.op("S", lambda s, ts=ts, q=q, bb=bb, so=silu_off: s.activation(
                        f["sg"][2 * q + ts], pb[bb + so + ts][:, :], AF.Silu),
                        reads=[f"pb{bb + silu_off + ts}"], writes=[f"sg{2*q+ts}"])

                for ts in range(2):
                    if kind == "ex":
                        k.op("V", lambda v, ts=ts, q=q, bb=bb, lo=lin_off: v.tensor_tensor(
                            f["htmp"][ts], f["sg"][2 * q + ts],
                            pb[bb + lo + ts][:, :], ALU.mult),
                            reads=[f"sg{2*q+ts}", f"pb{bb + lin_off + ts}"],
                            writes=[f"htmp{ts}"])

                        def fh2(v, ts=ts, j=j):
                            dst = f["hbuf"][:, j * TT + ts * 512: j * TT + (ts + 1) * 512]
                            return v.tensor_tensor(dst, f["htmp"][ts],
                                                   f["Be"][:, ts * 512:(ts + 1) * 512],
                                                   ALU.mult)
                        k.op("V", fh2, reads=[f"htmp{ts}", "Be"], writes=[f"hb{j}t{ts}"])
                    else:
                        def fh(v, ts=ts, j=j, q=q, bb=bb, lo=lin_off):
                            dst = f["hbuf"][:, j * TT + ts * 512: j * TT + (ts + 1) * 512]
                            return v.tensor_tensor(dst, f["sg"][2 * q + ts],
                                                   pb[bb + lo + ts][:, :], ALU.mult)
                        k.op("V", fh, reads=[f"sg{2*q+ts}", f"pb{bb + lin_off + ts}"],
                             writes=[f"hb{j}t{ts}"])

            for o2 in range(16):
                if filler:
                    filler()
                slot = o2 % 4
                if kind == "shA":
                    src = sdw[o2][:, 0:1408]
                elif kind == "shB":
                    src = sdw[o2][:, 1408:2816]
                else:
                    src = dww[e, o2]
                k.op("G", lambda g, src=src, slot=slot: g.dma_start(
                    out=f["wslot"][slot][:, 0:1408], in_=src),
                    writes=[f"ws{slot}"], dma=True)

                for ts in range(2):
                    bank = 5 + (o2 * 2 + ts) % 2

                    def fdn(t, ts=ts, slot=slot, bank=bank):
                        for ch in range(11):
                            last = t.matmul(pb[bank][:, :],
                                            f["wslot"][slot][:, ch * 128:(ch + 1) * 128],
                                            f["hbuf"][:, ch * TT + ts * 512:ch * TT + (ts + 1) * 512],
                                            start=(ch == 0), stop=(ch == 10))
                        return last
                    hdeps = [f"hb{jj}t{ts}" for jj in range(11)]
                    k.op("T", fdn, reads=[f"ws{slot}"] + hdeps, writes=[f"pb{bank}"])

                    def facc(v, o2=o2, ts=ts, bank=bank, gi=gi):
                        dst = f["acc"][:, o2 * TT + ts * 512: o2 * TT + (ts + 1) * 512]
                        if gi == 0:
                            return v.tensor_copy(dst, pb[bank][:, :])
                        return v.tensor_tensor(dst, dst, pb[bank][:, :], ALU.add)
                    k.op("V", facc, reads=[f"pb{bank}"] + ([f"acc{o2}t{ts}"] if gi else []),
                         writes=[f"acc{o2}t{ts}"])

        for hf in range(2):
            th = t0 + hf * 512
            for ch in range(NCH):
                q = ch % 2
                k.op("Y", lambda y, ch=ch, q=q, th=th: y.dma_start(
                    out=f["xc"][q], in_=xr[:, ch, th:th + 512]),
                    writes=[f"xc{q}"], dma=True)

                def fo(v, ch=ch, q=q, hf=hf):
                    src = f["acc"][:, ch * TT + hf * 512: ch * TT + (hf + 1) * 512]
                    return v.scalar_tensor_tensor(f["outc"][q], src, 0.5,
                                                  f["xc"][q],
                                                  ALU.mult, ALU.add)
                k.op("V", fo, reads=[f"xc{q}", f"acc{ch}t{hf}"], writes=[f"outc{q}"])

                k.op("G", lambda g, ch=ch, q=q, th=th: g.dma_start(
                    out=ydram[ch * 128:(ch + 1) * 128, th:th + 512], in_=f["outc"][q]),
                    reads=[f"outc{q}"], writes=[f"{uid}y{ch}_{th}"], dma=True)



def prep_ffn0_host(inputs):
    gw = np.asarray(inputs["L0_ffn_gate_w"], dtype=np.float32)
    sgwt = np.ascontiguousarray(np.asarray(inputs["L0_ffn_shared_gw"], np.float32).T)
    sdwt = np.ascontiguousarray(np.asarray(inputs["L0_ffn_shared_dw"], np.float32).T)
    upw = np.asarray(inputs["L0_ffn_up_w"], np.float32)
    dww = np.asarray(inputs["L0_ffn_down_w"], np.float32)

    gw_l = np.ascontiguousarray(
        gw.T.reshape(NCH, 128, 8).transpose(1, 0, 2).reshape(128, NCH * 8))
    sgw_l = np.ascontiguousarray(
        sgwt.reshape(NCH, 128, 44, 128).transpose(2, 1, 0, 3).reshape(44, 128, 2048))
    sdw_l = np.ascontiguousarray(
        sdwt.reshape(22, 128, 16, 128).transpose(2, 1, 0, 3).reshape(16, 128, 22 * 128))
    upw_l = np.ascontiguousarray(
        upw.transpose(0, 2, 1).reshape(8, NCH, 128, 22, 128)
        .transpose(0, 3, 2, 1, 4).reshape(8, 22, 128, 2048))
    dww_l = np.ascontiguousarray(
        dww.transpose(0, 2, 1).reshape(8, 11, 128, 16, 128)
        .transpose(0, 3, 2, 1, 4).reshape(8, 16, 128, 11 * 128))
    gbias_row = np.asarray(inputs["L0_ffn_gate_bias"], np.float32).reshape(1, 8)
    return {"gw_l": gw_l, "sgw_l": sgw_l, "sdw_l": sdw_l, "upw_l": upw_l,
            "dww_l": dww_l, "gbias_row": gbias_row}


def decl_ffn0_inputs(k):
    io = {}
    io["gw_l"] = k.dram_in("gw_l", [128, NCH * 8])
    io["gbias_row"] = k.dram_in("gbias_row", [1, 8])
    io["sgw_l"] = k.dram_in("sgw_l", [44, 128, 2048])
    io["sdw_l"] = k.dram_in("sdw_l", [16, 128, 22 * 128])
    io["upw_l"] = k.dram_in("upw_l", [8, 22, 128, 2048])
    io["dww_l"] = k.dram_in("dww_l", [8, 16, 128, 11 * 128])
    return io



def _const_dram_decl(k):
    dr = {}
    dr["ones16"] = k.dram_in("ones16", [128, 1], BF16)
    dr["ones1_32"] = k.dram_in("ones1_32", [1, 128], F32)
    dr["idn32"] = k.dram_in("idn32", [128, 128], F32)
    dr["idn16"] = k.dram_in("idn16", [128, 128], BF16)
    dr["kv"] = k.dram_in("kv", [128, 8], F32)
    return dr


def build_ffn0_launch():
    k = KB()
    xin = k.dram_in("xT", [D, TOK])
    yout = k.dram_out("yT", [D, TOK])
    dr = _const_dram_decl(k)
    c = alloc_common(k)
    load_consts(k, c, dr)
    io = decl_ffn0_inputs(k)
    f = alloc_ffn0(k)
    emit_ffn0_init(k, c, f, io)
    emit_ffn0(k, c, f, io, xin, yout, uid="a")
    return k.finish()





def alloc_attn0pre(k, ar=None):
    sb = _SbAlloc(k, ar)
    f = {}
    f["xc"] = [sb(f"ap_xc{i}", [128, 512], F32) for i in range(2)]
    f["x16c"] = [sb(f"ap_x16c{i}", [128, 512], BF16) for i in range(2)]
    f["sq16c"] = [sb(f"ap_sq16c{i}", [128, 512], BF16) for i in range(2)]
    f["statrow"] = sb("ap_stat", [1, 2048], F32)
    f["htmp"] = [sb(f"ap_ht{i}", [128, 512], F32) for i in range(2)]
    f["xn16"] = sb("ap_xn16", [128, NCH * 512], BF16)
    f["wq16"] = sb("ap_wq16", [128, NCH * 2048], BF16)
    f["wstg"] = [sb(f"ap_wstg{i}", [128, 512], BF16) for i in range(2)]
    return f


def emit_attn0pre(k, c, f, io, xdram, wdram, filler=None):
    pb = c["pb"]
    xr = xdram.rearrange("(c p) t -> p c t", p=128)
    for oc in range(NCH):
        k.op("G", lambda g, oc=oc: g.dma_start(
            out=f["wq16"][:, oc * 2048:(oc + 1) * 2048], in_=io["wql"][oc]),
            writes=[f"wq{oc}"], dma=True)

    for hf in range(4):
        th = hf * 512
        if filler:
            filler()
        emit_ln_half(k, c, f, xr, th, with_gate=False)
        for ch in range(NCH):
            q = ch % 2
            k.op("Y", lambda y, ch=ch, q=q, th=th: y.dma_start(
                out=f["xc"][q], in_=xr[:, ch, th:th + 512]),
                writes=[f"xc{q}"], dma=True)
            k.op("V", lambda v, ch=ch, q=q: v.tensor_tensor(
                f["htmp"][q], f["xc"][q], pb[3][:, :], ALU.mult),
                reads=[f"xc{q}", "pb3"], writes=[f"htmp{q}"])

            def fxn(v, ch=ch, q=q):
                dst = f["xn16"][:, ch * 512:(ch + 1) * 512]
                return v.tensor_tensor(dst, f["htmp"][q], pb[4][:, :], ALU.add)
            k.op("V", fxn, reads=[f"htmp{q}", "pb4"], writes=[f"apxn{ch}"])
            if filler:
                filler()

        for oc in range(NCH):
            q = oc % 2
            if filler:
                filler()

            def fw(t, oc=oc, q=q):
                for ch in range(NCH):
                    last = t.matmul(pb[q][:, :],
                                    f["wq16"][:, oc * 2048 + ch * 128:oc * 2048 + (ch + 1) * 128],
                                    f["xn16"][:, ch * 512:(ch + 1) * 512],
                                    start=(ch == 0), stop=(ch == NCH - 1))
                return last
            k.op("T", fw, reads=[f"wq{oc}"] + [f"apxn{ch}" for ch in range(NCH)],
                 writes=[f"pb{q}"])
            k.op("V", lambda v, q=q: v.tensor_copy(f["wstg"][q], pb[q][:, :]),
                 reads=[f"pb{q}"], writes=[f"wstg{q}"])
            k.op("G", lambda g, oc=oc, q=q, th=th: g.dma_start(
                out=wdram[oc * 128:(oc + 1) * 128, th:th + 512], in_=f["wstg"][q]),
                reads=[f"wstg{q}"], writes=[f"wout{oc}_{th}"], dma=True)


def prep_attn0_host(inputs):
    W = np.asarray(inputs["L0_attn_W"], np.float32)
    WT = np.ascontiguousarray(W.T)
    wql = np.ascontiguousarray(
        WT.reshape(NCH, 128, NCH, 128).transpose(2, 1, 0, 3).reshape(NCH, 128, 2048))
    wpl = np.ascontiguousarray(
        W.reshape(NCH, 128, NCH, 128).transpose(2, 1, 0, 3).reshape(NCH, 128, 2048))
    g1 = np.asarray(inputs["L0_attn_g1"], np.float32).reshape(128, 1)
    g2 = np.asarray(inputs["L0_attn_g2"], np.float32).reshape(128, 1)
    g3 = np.asarray(inputs["L0_attn_g3"], np.float32).reshape(128, 1)
    return {"wql": wql, "wpl": wpl, "g1": g1, "g2": g2, "g3": g3}


def build_attn0pre_launch():
    k = KB()
    xin = k.dram_in("xT", [D, TOK])
    wout = k.dram_out("wT", [D, TOK], BF16)
    dr = _const_dram_decl(k)
    c = alloc_common(k)
    load_consts(k, c, dr)
    io = {"wql": k.dram_in("wql", [NCH, 128, 2048])}
    f = alloc_attn0pre(k)
    emit_attn0pre(k, c, f, io, xin, wout)
    return k.finish()



SCL = float(1.0 / np.sqrt(128.0))


def alloc_attn0core(k, ar=None):
    sb = _SbAlloc(k, ar)
    f = {}
    f["wh"] = sb("ac_wh", [128, SEQ], BF16)
    f["R"] = sb("ac_R", [128, SEQ], BF16)
    f["qh"] = sb("ac_qh", [128, SEQ], BF16)
    f["vh"] = sb("ac_vh", [128, SEQ], BF16)
    f["vtm"] = sb("ac_vtm", [128, SEQ], BF16)
    f["Estrip"] = [sb(f"ac_Es{i}", [128, 1024], BF16) for i in range(2)]
    f["Es2"] = [sb(f"ac_Es2{i}", [128, 1024], BF16) for i in range(2)]
    f["Eu"] = [sb(f"ac_Eu{i}", [128, 1024], BF16) for i in range(2)]
    f["Ew"] = [sb(f"ac_Ew{i}", [128, 1024], BF16) for i in range(2)]
    f["Ub"] = sb("ac_Ub", [128, 512], BF16)
    f["csb"] = sb("ac_csb", [128, 128], F32)
    f["cs"] = sb("ac_cs", [128, 32], F32)
    f["colinv"] = sb("ac_colinv", [128, 32], F32)
    f["s2r"] = sb("ac_s2r", [1, SEQ], BF16)
    f["rstdr"] = sb("ac_rstdr", [1, SEQ], BF16)
    f["rsr"] = sb("ac_rsr", [1, SEQ], F32)
    f["srow"] = sb("ac_srow", [1, 64], F32)
    f["negM"] = sb("ac_negM", [128, 1], F32)
    f["g12"] = sb("ac_g12", [128, 1], F32)
    f["g3c"] = sb("ac_g3c", [128, 1], F32)
    f["gtmp"] = sb("ac_gtmp", [128, 3], F32)
    f["fb"] = sb("ac_fb", [128, 1], F32)
    f["flagrow"] = sb("ac_flagrow", [1, 1], F32)
    f["rowsel"] = sb("ac_rowsel", [1, 512], F32)
    f["qsel"] = [sb(f"ac_qsel{i}", [128, 512], BF16) for i in range(2)]
    f["qst"] = [sb(f"ac_qst{i}", [128, 512], BF16) for i in range(2)]
    f["yatt"] = sb("ac_yatt", [128, NCH * TOK], BF16)
    f["wslot"] = [sb(f"ac_ws{i}", [128, 2048], BF16) for i in range(2)]
    f["xc"] = [sb(f"ac_xc{i}", [128, 512], F32) for i in range(2)]
    f["outc"] = [sb(f"ac_outc{i}", [128, 512], F32) for i in range(2)]
    f["ones1_16"] = sb("ac_ones1_16", [1, 128], BF16)
    return f


def emit_attn0core(k, c, f, io, w0, w1, xdram, ydram, filler=None):
    pb = c["pb"]
    k.op("Y", lambda y: y.dma_start(out=f["ones1_16"], in_=io["ones1_16d"]),
         writes=["ones1_16"], dma=True)
    for nm, dn in (("g12", "g1"), ("g3c", "g3")):
        pass
    k.op("Y", lambda y: y.dma_start(out=f["gtmp"][:, 0:1], in_=io["g1"]),
         writes=["gtmp"], dma=True)
    k.op("Y", lambda y: y.dma_start(out=f["gtmp"][:, 1:2], in_=io["g2"]),
         writes=["gtmp"], dma=True)
    k.op("Y", lambda y: y.dma_start(out=f["gtmp"][:, 2:3], in_=io["g3"]),
         writes=["gtmp"], dma=True)
    k.op("V", lambda v: v.tensor_tensor(f["g12"], f["gtmp"][:, 0:1],
                                        f["gtmp"][:, 1:2], ALU.mult),
         reads=["gtmp"], writes=["g12"])
    k.op("V", lambda v: v.tensor_copy(f["g3c"], f["gtmp"][:, 2:3]),
         reads=["gtmp"], writes=["g3c"])
    k.op("Y", lambda y: y.dma_start(out=f["flagrow"], in_=io["flag_d"]),
         writes=["flagrow"], dma=True)

    def ffb(t):
        return t.matmul(pb[0][:, 0:1], c["ones1_32"], f["flagrow"], start=True, stop=True)
    k.op("T", ffb, reads=["flagrow", "ones1_32"], writes=["pb0"])
    k.op("V", lambda v: v.tensor_copy(f["fb"], pb[0][:, 0:1]), reads=["pb0"],
         writes=["fb"])

    xr = xdram.rearrange("(c p) t -> p c t", p=128)

    for h in range(NCH):
        if filler:
            filler()
        k.op("Y", lambda y, h=h: y.dma_start(
            out=f["wh"][:, 0:2048], in_=w0[h * 128:(h + 1) * 128, :]),
            reads=list(io.get("wdeps", ())), writes=["wh"], dma=True)
        k.op("Y", lambda y, h=h: y.dma_start(
            out=f["wh"][:, 2048:4096], in_=w1[h * 128:(h + 1) * 128, :]),
            reads=list(io.get("wdeps", ())), writes=["wh"], dma=True)
        k.op("S", lambda s: s.activation(f["R"], f["wh"], AF.Square),
             reads=["wh"], writes=["R"])
        for ib in range(8):
            q = ib % 2

            def fs2(t, ib=ib, q=q):
                return t.matmul(pb[2 + q][0:1, :], c["ones16"],
                                f["R"][:, ib * 512:(ib + 1) * 512], start=True, stop=True)
            k.op("T", fs2, reads=["R", "ones16"], writes=[f"pb{2+q}"])
            k.op("V", lambda v, ib=ib, q=q: v.tensor_copy(
                f["s2r"][0:1, ib * 512:(ib + 1) * 512], pb[2 + q][0:1, :]),
                reads=[f"pb{2+q}"], writes=["s2r"])
        k.op("S", lambda s: s.activation(f["rstdr"], f["s2r"], AF.Sqrt,
                                         bias=c["kv"][0:1, 1:2], scale=1.0 / 128.0),
             reads=["s2r", "kv"], writes=["rstdr"])
        def frcp(v):
            with k.nc.allow_low_precision(reason="rms row kept in bf16"):
                return v.reciprocal(f["rstdr"], f["rstdr"])
        k.op("V", frcp, reads=["rstdr"], writes=["rstdr"])

        for ib in range(8):
            q = ib % 2

            def fR(t, ib=ib, q=q):
                return t.matmul(pb[2 + q][:, :], f["ones1_16"],
                                f["rstdr"][0:1, ib * 512:(ib + 1) * 512], start=True, stop=True)
            k.op("T", fR, reads=["rstdr", "ones1_16"], writes=[f"pb{2+q}"])
            k.op("V", lambda v, ib=ib, q=q: v.tensor_copy(
                f["R"][:, ib * 512:(ib + 1) * 512], pb[2 + q][:, :]),
                reads=[f"pb{2+q}"], writes=["R"])
        k.op("V", lambda v: v.tensor_tensor(f["wh"], f["wh"], f["R"], ALU.mult),
             reads=["wh", "R"], writes=["wh"])
        k.op("V", lambda v: v.tensor_scalar(f["qh"], f["wh"], f["g12"], None, ALU.mult),
             reads=["wh", "g12"], writes=["qh"])
        k.op("V", lambda v: v.tensor_scalar(f["vh"], f["wh"], f["g3c"], None, ALU.mult),
             reads=["wh", "g3c"], writes=["vh"])
        for tc in range(32):
            k.op("Y", lambda y, tc=tc: y.dma_start(
                out=f["vtm"][:, tc * 128:(tc + 1) * 128],
                in_=f["vh"][:, tc * 128:(tc + 1) * 128], transpose=True),
                reads=["vh"], writes=["vtm"], dma=True)
        k.op("S", lambda s: s.activation(f["R"], f["qh"], AF.Square),
             reads=["qh"], writes=["R"])
        k.op("V", lambda v: v.memset(f["srow"][0:1, 0:1], 0.0), writes=["srow"])
        for ib in range(8):
            q = ib % 2

            def fnq(t, ib=ib, q=q):
                return t.matmul(pb[2 + q][0:1, :], c["ones16"],
                                f["R"][:, ib * 512:(ib + 1) * 512], start=True, stop=True)
            k.op("T", fnq, reads=["R", "ones16"], writes=[f"pb{2+q}"])
            k.op("V", lambda v, ib=ib, q=q: v.tensor_reduce(
                f["srow"][0:1, 1:2], pb[2 + q][0:1, :], AX.X, ALU.max),
                reads=[f"pb{2+q}", "srow"], writes=["srow"])
            k.op("V", lambda v: v.tensor_tensor(
                f["srow"][0:1, 0:1], f["srow"][0:1, 0:1], f["srow"][0:1, 1:2], ALU.max),
                reads=["srow"], writes=["srow"])
        k.op("S", lambda s: s.activation(f["srow"][0:1, 3:4], f["srow"][0:1, 0:1], AF.Sqrt),
             reads=["srow"], writes=["srow"])
        k.op("V", lambda v: v.tensor_scalar_mul(f["srow"][0:1, 4:5], f["srow"][0:1, 3:4],
                                                -SCL * float(np.sqrt(128.0))),
             reads=["srow"], writes=["srow"])

        def fnM(t):
            return t.matmul(pb[6][:, 0:1], c["ones1_32"], f["srow"][0:1, 4:5],
                            start=True, stop=True)
        k.op("T", fnM, reads=["srow", "ones1_32"], writes=["pb6"])
        k.op("V", lambda v: v.tensor_copy(f["negM"], pb[6][:, 0:1]),
             reads=["pb6"], writes=["negM"])

        for ibp in range(4):
            if filler:
                filler()

            def mk_frs(jt):
                q = jt % 2

                def frs(t, jt=jt, q=q):
                    for d in range(2):
                        last = t.matmul(pb[4 + d][0:1, :], c["ones16"],
                                        f["Estrip"][q][:, d * 512:(d + 1) * 512],
                                        start=(jt == 0), stop=(jt == 31))
                    return last
                k.op("T", frs, reads=[f"E{q}", "ones16"], writes=["pb4", "pb5"])

            for jt in range(32):
                q = jt % 2

                def fat(t, jt=jt, ibp=ibp, q=q):
                    for d in range(2):
                        ib = ibp * 2 + d
                        last = t.matmul(pb[2 * q + d][:, :],
                                        f["wh"][:, jt * 128:(jt + 1) * 128],
                                        f["qh"][:, ib * 512:(ib + 1) * 512],
                                        start=True, stop=True)
                    return last
                k.op("T", fat, reads=["wh", "qh"], writes=[f"pb{2*q}", f"pb{2*q+1}"])

                def fex(s, jt=jt, ibp=ibp, q=q):
                    return s.activation(f["Estrip"][q],
                                        c["pbig"][:, q * 1024:(q + 1) * 1024], AF.Exp,
                                        bias=f["negM"], scale=SCL,
                                        accum_out=f["csb"][:, jt * 4 + ibp:jt * 4 + ibp + 1])
                csl = ["csb"] if (jt == 0 and ibp == 0) or (jt == 31 and ibp == 3) else []
                k.op("S", fex, reads=[f"pb{2*q}", f"pb{2*q+1}", "negM"],
                     writes=[f"E{q}"] + csl)
                if jt > 0:
                    mk_frs(jt - 1)
            mk_frs(31)

            def frse(v, ibp=ibp):
                for d in range(2):
                    ib = ibp * 2 + d
                    last = v.tensor_copy(f["rsr"][0:1, ib * 512:(ib + 1) * 512],
                                         pb[4 + d][0:1, :])
                return last
            k.op("V", frse, reads=["pb4", "pb5"], writes=["rsr"])

        k.op("V", lambda v: v.tensor_reduce(
            f["cs"], f["csb"].rearrange("p (jt g) -> p jt g", g=4), AX.X, ALU.add),
            reads=["csb"], writes=["cs"])
        k.op("V", lambda v: v.reciprocal(f["colinv"], f["cs"]),
             reads=["cs"], writes=["colinv"])

        for q4 in range(4):
            if filler:
                filler()
            qs = q4 % 2

            def fqs1(v, q4=q4, qs=qs):
                a = f["qh"][:, q4 * 512:(q4 + 1) * 512]
                b = f["qh"][:, (4 + q4) * 512:(5 + q4) * 512]
                return v.tensor_tensor(f["qst"][qs], b, a, ALU.subtract)
            k.op("V", fqs1, reads=["qh"], writes=[f"qst{qs}"])
            k.op("V", lambda v, qs=qs: v.tensor_scalar(
                f["qst"][qs], f["qst"][qs], f["fb"], None, ALU.mult),
                reads=[f"qst{qs}", "fb"], writes=[f"qst{qs}"])

            def fqs2(v, q4=q4, qs=qs):
                a = f["qh"][:, q4 * 512:(q4 + 1) * 512]
                return v.tensor_tensor(f["qsel"][qs], f["qst"][qs], a, ALU.add)
            k.op("V", fqs2, reads=[f"qst{qs}", "qh"], writes=[f"qsel{qs}"])

            rs = f["rowsel"]

            def frsel1(v, q4=q4):
                a = f["rsr"][0:1, q4 * 512:(q4 + 1) * 512]
                b = f["rsr"][0:1, (4 + q4) * 512:(5 + q4) * 512]
                return v.tensor_tensor(rs, b, a, ALU.subtract)
            k.op("V", frsel1, reads=["rsr"], writes=["rowsel"])
            k.op("V", lambda v: v.tensor_scalar(rs, rs, f["flagrow"], None, ALU.mult),
                 reads=["rowsel", "flagrow"], writes=["rowsel"])

            def frsel2(v, q4=q4):
                a = f["rsr"][0:1, q4 * 512:(q4 + 1) * 512]
                return v.tensor_tensor(rs, rs, a, ALU.add)
            k.op("V", frsel2, reads=["rowsel", "rsr"], writes=["rowsel"])
            k.op("V", lambda v: v.reciprocal(rs, rs), reads=["rowsel"],
                 writes=["rowsel"])
            k.op("T", lambda t: t.matmul(pb[6][:, :], c["ones1_32"], f["rowsel"],
                                         start=True, stop=True),
                 reads=["rowsel", "ones1_32"], writes=["pb6"])
            k.op("V", lambda v: v.tensor_copy(f["Ub"], pb[6][:, :]),
                 reads=["pb6"], writes=["Ub"])

            def mk_fy(sp):
                q = sp % 2
                jts = (2 * sp, 2 * sp + 1)

                def fy(t, jts=jts, q=q, sp=sp):
                    for d, jt in enumerate(jts):
                        t.matmul(pb[7][:, :], f["vtm"][:, jt * 128:(jt + 1) * 128],
                                 f["Eu"][q][:, d * 512:(d + 1) * 512],
                                 start=(sp == 0 and d == 0), stop=False)
                        last = t.matmul(pb[7][:, :], f["vtm"][:, jt * 128:(jt + 1) * 128],
                                        f["Ew"][q][:, d * 512:(d + 1) * 512], start=False,
                                        stop=(sp == 15 and d == 1))
                    return last
                k.op("T", fy, reads=[f"Eu{q}", f"Ew{q}", "vtm"], writes=["pb7"])

            for sp in range(16):
                q = sp % 2
                jts = (2 * sp, 2 * sp + 1)

                def fat(t, jts=jts, qs=qs, q=q):
                    for d, jt in enumerate(jts):
                        last = t.matmul(pb[2 * q + d][:, :],
                                        f["wh"][:, jt * 128:(jt + 1) * 128],
                                        f["qsel"][qs], start=True, stop=True)
                    return last
                k.op("T", fat, reads=["wh", f"qsel{qs}"], writes=[f"pb{2*q}", f"pb{2*q+1}"])

                def fex2(s, q=q):
                    return s.activation(f["Es2"][q],
                                        c["pbig"][:, q * 1024:(q + 1) * 1024], AF.Exp,
                                        bias=f["negM"], scale=SCL)
                k.op("S", fex2, reads=[f"pb{2*q}", f"pb{2*q+1}", "negM"],
                     writes=[f"E2{q}"])

                def fuw(v, jts=jts, q=q):
                    for d, jt in enumerate(jts):
                        v.tensor_tensor(f["Eu"][q][:, d * 512:(d + 1) * 512],
                                        f["Es2"][q][:, d * 512:(d + 1) * 512],
                                        f["Ub"], ALU.mult)
                        last = v.tensor_scalar(f["Ew"][q][:, d * 512:(d + 1) * 512],
                                               f["Es2"][q][:, d * 512:(d + 1) * 512],
                                               f["colinv"][:, jt:jt + 1], None, ALU.mult)
                    return last
                k.op("V", fuw, reads=[f"E2{q}", "Ub", "colinv"],
                     writes=[f"Eu{q}", f"Ew{q}"])
                if sp > 0:
                    mk_fy(sp - 1)
            mk_fy(15)

            def fye(v, h=h, q4=q4):
                dst = f["yatt"][:, h * TOK + q4 * 512: h * TOK + (q4 + 1) * 512]
                return v.tensor_copy(dst, pb[7][:, :])
            k.op("V", fye, reads=["pb7"], writes=[f"ya{h}"])

    for oc in range(NCH):
        ws = oc % 2
        k.op("G", lambda g, oc=oc, ws=ws: g.dma_start(
            out=f["wslot"][ws], in_=io["wpl"][oc]),
            writes=[f"aws{ws}"], dma=True)
        for tb in range(4):
            q = tb % 2

            def fpj(t, oc=oc, ws=ws, tb=tb, q=q):
                for hh in range(NCH):
                    last = t.matmul(pb[q][:, :], f["wslot"][ws][:, hh * 128:(hh + 1) * 128],
                                    f["yatt"][:, hh * TOK + tb * 512:hh * TOK + (tb + 1) * 512],
                                    start=(hh == 0), stop=(hh == NCH - 1))
                return last
            k.op("T", fpj, reads=[f"aws{ws}"] + [f"ya{hh}" for hh in range(NCH)],
                 writes=[f"pb{q}"])
            k.op("Y", lambda y, oc=oc, tb=tb, q=q: y.dma_start(
                out=f["xc"][q], in_=xr[:, oc, tb * 512:(tb + 1) * 512]),
                writes=[f"xc{q}"], dma=True)

            def fout(v, q=q):
                return v.scalar_tensor_tensor(f["outc"][q], pb[q][:, :], 1.0,
                                              f["xc"][q], ALU.mult, ALU.add)
            k.op("V", fout, reads=[f"pb{q}", f"xc{q}"], writes=[f"outc{q}"])
            k.op("G", lambda g, oc=oc, tb=tb, q=q: g.dma_start(
                out=ydram[oc * 128:(oc + 1) * 128, tb * 512:(tb + 1) * 512],
                in_=f["outc"][q]),
                reads=[f"outc{q}"], writes=[f"acy{oc}_{tb}"], dma=True)


def build_attn0core_launch():
    k = KB()
    w0in = k.dram_in("w0", [D, TOK], BF16)
    w1in = k.dram_in("w1", [D, TOK], BF16)
    xin = k.dram_in("xT", [D, TOK])
    yout = k.dram_out("yT", [D, TOK])
    dr = _const_dram_decl(k)
    c = alloc_common(k)
    load_consts(k, c, dr)
    io = {
        "wpl": k.dram_in("wpl", [NCH, 128, 2048]),
        "g1": k.dram_in("g1", [128, 1]),
        "g2": k.dram_in("g2", [128, 1]),
        "g3": k.dram_in("g3", [128, 1]),
        "ones1_16d": k.dram_in("ones1_16d", [1, 128], BF16),
        "flag_d": k.dram_in("flag_d", [1, 1]),
    }
    f = alloc_attn0core(k)
    emit_attn0core(k, c, f, io, w0in, w1in, xin, yout)
    return k.finish()





def alloc_dequant(k, ar=None):
    sb = _SbAlloc(k, ar)
    f = {}
    f["bt"] = [sb(f"dq_bt{i}", [128, 1408], F32) for i in range(2)]
    f["h1"] = sb("dq_h1", [128, 1408], F32)
    f["hi"] = sb("dq_hi", [128, 1408], I32)
    f["hif"] = sb("dq_hif", [128, 1408], F32)
    f["lo"] = sb("dq_lo", [128, 1408], F32)
    f["s1"] = sb("dq_s1", [128, 1408], F32)
    f["m"] = sb("dq_m", [128, 1408], F32)
    f["r4"] = sb("dq_r4", [128, 1408], F32)
    f["t2"] = sb("dq_t2", [128, 1408], F32)
    f["sgn"] = sb("dq_sgn", [128, 1408], F32)
    f["out"] = [sb(f"dq_out{i}", [128, 2816], F32) for i in range(2)]
    f["o16"] = [sb(f"dq_o16{i}", [128, 2816], BF16) for i in range(2)]
    f["sct"] = sb("dq_sct", [128, 44], F32)
    f["sc16"] = sb("dq_sc16", [128, 44], mybir.dt.float16)
    f["trtmp"] = sb("dq_trtmp", [128, 256], F32)
    f["tstage"] = [sb(f"dq_tst{i}", [128, 2816], BF16) for i in range(2)]
    return f


def emit_dequant_tile(k, f, uid, pk_u8, sct_f16, w16_out, nby, ngrp):
    q = uid % 2
    k.op("G", lambda g: g.dma_start(out=f["bt"][q][:, 0:nby], in_=pk_u8),
         writes=[f"dqbt{q}"], dma=True)
    k.op("G", lambda g: g.dma_start(out=f["sc16"][:, 0:ngrp], in_=sct_f16),
         writes=["dqsc16"], dma=True)
    k.op("V", lambda v: v.tensor_copy(f["sct"][:, 0:ngrp], f["sc16"][:, 0:ngrp]),
         reads=["dqsc16"], writes=["dqsct"])

    bt = f["bt"][q][:, 0:nby]
    B_ = [f[nm] for nm in ("h1", "hi", "hif", "lo", "s1", "m", "r4", "t2", "sgn")]
    h1, hi, hif, lo, s1, m, r4, t2, sgn = [b[:, 0:nby] for b in B_]
    out = f["out"][q][:, 0:2 * nby]
    o16 = f["o16"][q][:, 0:2 * nby]

    def vo(fn, rd, wr):
        k.op("V", fn, reads=rd, writes=wr)

    R = f"dqbt{q}"
    vo(lambda v: v.tensor_scalar(h1, bt, 1.0 / 16.0, -0.46875, ALU.mult, ALU.add), [R], ["dqh1"])
    vo(lambda v: v.tensor_copy(hi, h1), ["dqh1"], ["dqhi"])
    vo(lambda v: v.tensor_copy(hif, hi), ["dqhi"], ["dqhif"])
    vo(lambda v: v.scalar_tensor_tensor(lo, hif, -16.0, bt, ALU.mult, ALU.add),
       ["dqhif", R], ["dqlo"])
    for par, n in ((0, lo), (1, hif)):
        dst = out.rearrange("p (t two) -> p t two", two=2)[:, :, par]
        vo(lambda v, n=n: v.tensor_scalar(s1, n, 8.0, None, ALU.is_ge),
           ["dqlo" if par == 0 else "dqhif"], ["dqs1"])
        vo(lambda v, n=n: v.scalar_tensor_tensor(m, s1, -8.0, n, ALU.mult, ALU.add),
           ["dqs1", "dqlo" if par == 0 else "dqhif"], ["dqm"])
        vo(lambda v: v.tensor_scalar(r4, m, -4.0, 0.0, ALU.add, ALU.max), ["dqm"], ["dqr4"])
        vo(lambda v: v.tensor_scalar(t2, m, -6.0, 0.0, ALU.add, ALU.max), ["dqm"], ["dqt2"])
        vo(lambda v: v.scalar_tensor_tensor(t2, r4, 0.5, t2, ALU.mult, ALU.add),
           ["dqr4", "dqt2"], ["dqt2"])
        vo(lambda v: v.scalar_tensor_tensor(t2, m, 0.5, t2, ALU.mult, ALU.add),
           ["dqm", "dqt2"], ["dqt2"])
        vo(lambda v: v.tensor_scalar(sgn, s1, -2.0, 1.0, ALU.mult, ALU.add), ["dqs1"], ["dqsgn"])
        vo(lambda v, dst=dst: v.tensor_tensor(dst, t2, sgn, ALU.mult),
           ["dqt2", "dqsgn"], [f"dqoutp{par}"])
    def fsc(v):
        for g in range(ngrp):
            last = v.tensor_scalar(o16[:, g * 128:(g + 1) * 128],
                                   out[:, g * 128:(g + 1) * 128],
                                   f["sct"][:, g:g + 1], None, ALU.mult)
        return last
    k.op("V", fsc, reads=["dqoutp0", "dqoutp1", "dqsct"], writes=[f"dqo16{q}"])
    if w16_out is not None:
        k.op("G", lambda g: g.dma_start(out=w16_out, in_=o16), reads=[f"dqo16{q}"],
             writes=[f"dqw{uid}"], dma=True)
        return
    slab, c, trtmp, tstage = f["_slab"], f["_c"], f["trtmp"], f["tstage"][q]
    nblk = ngrp
    for blk in range(nblk):
        bq = blk % 2
        k.op("V", lambda v, blk=blk, bq=bq: v.tensor_copy(
            trtmp[:, bq * 128:(bq + 1) * 128], o16[:, blk * 128:(blk + 1) * 128]),
            reads=[f"dqo16{q}"], writes=[f"dqtr{bq}"])
        k.op("T", lambda t, blk=blk, bq=bq: t.transpose(
            c["pb"][4 + bq][:, 0:128], trtmp[:, bq * 128:(bq + 1) * 128], c["idn32"]),
            reads=[f"dqtr{bq}", "idn32"], writes=[f"pb{4+bq}"])
        k.op("V", lambda v, blk=blk, bq=bq: v.tensor_copy(
            tstage[:, blk * 128:(blk + 1) * 128], c["pb"][4 + bq][:, 0:128]),
            reads=[f"pb{4+bq}"], writes=[f"dqst{q}"])
    k.op("G", lambda g: g.dma_start(out=slab, in_=tstage[:, 0:nblk * 128]),
         reads=[f"dqst{q}"], writes=[f"dqw{uid}"], dma=True)


def build_dequant_launch():
    k = KB()
    gu_pk = k.dram_in("gu_pk", [1408, 1024], U8)
    gu_sc = k.dram_in("gu_sc", [1408, 16], mybir.dt.float16)
    dn_pk = k.dram_in("dn_pk", [256, 2816], U8)
    dn_sc = k.dram_in("dn_sc", [256, 44], mybir.dt.float16)
    gu_out = k.dram_out("gu_out", [1408, 2048], BF16)
    dn_out = k.dram_out("dn_out", [256, 5632], BF16)
    f = alloc_dequant(k)
    uid = 0
    for r in range(11):
        emit_dequant_tile(k, f, uid, gu_pk[r * 128:(r + 1) * 128, :],
                          gu_sc[r * 128:(r + 1) * 128, :],
                          gu_out[r * 128:(r + 1) * 128, :], 1024, 16)
        uid += 1
    for r in range(2):
        emit_dequant_tile(k, f, uid, dn_pk[r * 128:(r + 1) * 128, 0:1408],
                          dn_sc[r * 128:(r + 1) * 128, 0:22],
                          dn_out[r * 128:(r + 1) * 128, 0:2816], 1408, 22)
        uid += 1
        emit_dequant_tile(k, f, uid, dn_pk[r * 128:(r + 1) * 128, 1408:2816],
                          dn_sc[r * 128:(r + 1) * 128, 22:44],
                          dn_out[r * 128:(r + 1) * 128, 2816:5632], 1408, 22)
        uid += 1
    return k.finish()


def alloc_dequant2(k, ar):
    f = {}
    f["bt"] = [ar.get(f"d2_bt{i}", (128, 1408), F32) for i in range(2)]
    for nm in ("A", "C", "D", "E", "F"):
        f[nm] = ar.get("d2_" + nm, (128, 1408), F32)
    f["Bf"], f["Bi"] = ar.get2("d2_B", (128, 1408), F32, I32)
    f["out"] = [ar.get(f"d2_out{i}", (128, 2816), F32) for i in range(2)]
    f["o16"] = [ar.get(f"d2_o16{i}", (128, 2816), BF16) for i in range(2)]
    f["sct"] = [ar.get(f"d2_sct{i}", (128, 44), F32) for i in range(2)]
    f["sc16"] = [ar.get(f"d2_sc16{i}", (128, 44), mybir.dt.float16)
                 for i in range(2)]
    f["tstage"] = [ar.get(f"d2_tst{i}", (128, 2816), BF16) for i in range(2)]
    return f


def make_dequant_chunks(k, f, tiles):
    chunks = []
    res = []
    for idx, (pk, sc, slab, nby, ngrp) in enumerate(tiles):
        q = idx % 2
        bt = f["bt"][q][:, 0:nby]
        A, C = f["A"][:, 0:nby], f["C"][:, 0:nby]
        Dt, E, Ff = f["D"][:, 0:nby], f["E"][:, 0:nby], f["F"][:, 0:nby]
        Bf, Bi = f["Bf"][:, 0:nby], f["Bi"][:, 0:nby]
        out = f["out"][q][:, 0:2 * nby]
        o16 = f["o16"][q][:, 0:2 * nby]
        sct = f["sct"][q][:, 0:ngrp]
        sc16 = f["sc16"][q][:, 0:ngrp]
        tst = f["tstage"][q]
        R = f"dqbt{q}"

        def c_load(pk=pk, sc=sc, bt=bt, A=A, C=C, Bi=Bi, sct=sct, sc16=sc16,
                   q=q, R=R):
            k.op("G", lambda g: g.dma_start(out=bt, in_=pk),
                 writes=[R], dma=True)
            k.op("G", lambda g: g.dma_start(out=sc16, in_=sc),
                 writes=[f"dqsc16{q}"], dma=True)
            k.op("V", lambda v: v.tensor_copy(sct, sc16),
                 reads=[f"dqsc16{q}"], writes=[f"dqsct{q}"])
            k.op("V", lambda v: v.tensor_scalar(A, bt, 1.0 / 16.0, -0.46875,
                                                ALU.mult, ALU.add),
                 reads=[R], writes=["dqA"])
            k.op("V", lambda v: v.tensor_copy(Bi, A), reads=["dqA"], writes=["dqB"])
            k.op("V", lambda v: v.tensor_copy(C, Bi), reads=["dqB"], writes=["dqC"])
            k.op("V", lambda v: v.scalar_tensor_tensor(A, C, -16.0, bt,
                                                       ALU.mult, ALU.add),
                 reads=["dqC", R], writes=["dqA"])
        chunks.append(c_load)

        for par in range(2):
            def c_par(par=par, A=A, C=C, Bf=Bf, Dt=Dt, E=E, Ff=Ff, out=out, q=q):
                n = A if par == 0 else C
                nn = "dqA" if par == 0 else "dqC"
                k.op("V", lambda v: v.tensor_scalar(Bf, n, 8.0, None, ALU.is_ge),
                     reads=[nn], writes=["dqB"])
                k.op("V", lambda v: v.scalar_tensor_tensor(Dt, Bf, -8.0, n,
                                                           ALU.mult, ALU.add),
                     reads=["dqB", nn], writes=["dqD"])
                k.op("V", lambda v: v.tensor_scalar(E, Dt, -4.0, 0.0,
                                                    ALU.add, ALU.max),
                     reads=["dqD"], writes=["dqE"])
                k.op("V", lambda v: v.tensor_scalar(Ff, Dt, -6.0, 0.0,
                                                    ALU.add, ALU.max),
                     reads=["dqD"], writes=["dqF"])
                k.op("V", lambda v: v.scalar_tensor_tensor(Ff, E, 0.5, Ff,
                                                           ALU.mult, ALU.add),
                     reads=["dqE", "dqF"], writes=["dqF"])
                k.op("V", lambda v: v.scalar_tensor_tensor(Ff, Dt, 0.5, Ff,
                                                           ALU.mult, ALU.add),
                     reads=["dqD", "dqF"], writes=["dqF"])
                k.op("V", lambda v: v.tensor_scalar(Bf, Bf, -2.0, 1.0,
                                                    ALU.mult, ALU.add),
                     reads=["dqB"], writes=["dqB"])
                dst = out.rearrange("p (t two) -> p t two", two=2)[:, :, par]
                k.op("V", lambda v: v.tensor_tensor(dst, Ff, Bf, ALU.mult),
                     reads=["dqF", "dqB"], writes=[f"dqoutp{par}"])
            chunks.append(c_par)

        def c_scale(out=out, o16=o16, sct=sct, ngrp=ngrp, q=q):
            def fsc(v):
                for g in range(ngrp):
                    last = v.tensor_scalar(o16[:, g * 128:(g + 1) * 128],
                                           out[:, g * 128:(g + 1) * 128],
                                           sct[:, g:g + 1], None, ALU.mult)
                return last
            k.op("V", fsc, reads=["dqoutp0", "dqoutp1", f"dqsct{q}"],
                 writes=[f"dqo16{q}"])
        chunks.append(c_scale)

        def c_tr(o16=o16, tst=tst, slab=slab, nby=nby, q=q, idx=idx):
            for blk in range(2 * nby // 128):
                k.op("Y", lambda y, blk=blk: y.dma_start(
                    out=tst[:, blk * 128:(blk + 1) * 128],
                    in_=o16[:, blk * 128:(blk + 1) * 128], transpose=True),
                    reads=[f"dqo16{q}"], writes=[f"dqts{q}_{blk % 4}"], dma=True)
            k.op("G", lambda g: g.dma_start(out=slab, in_=tst[:, 0:2 * nby]),
                 reads=[f"dqts{q}_{i}" for i in range(4)],
                 writes=[f"dqw{idx}"], dma=True)
        chunks.append(c_tr)
        res.append(f"dqw{idx}")
    return chunks, res


def alloc_tern(k, ar):
    ft = {}
    ft["wbuf"] = [ar.get(f"tn_wb{i}", (128, 1024), F32) for i in range(2)]
    ft["wt16"] = [ar.get(f"tn_wt{i}", (128, 1024), BF16) for i in range(2)]
    ft["absacc"] = ar.get("tn_absacc", (128, 2), F32)
    ft["sv"] = ar.get("tn_sv", (1, 4), F32)
    ft["s1b"] = ar.get("tn_s1b", (128, 1), F32)
    return ft


def make_tern_chunks(k, c, ft, wsrc_l, n_oc, w16dst, uid, mean_out):
    pb = c["pb"]
    nel = float(n_oc * 128 * 2048)
    chunks = []

    def c_init():
        k.op("V", lambda v: v.memset(ft["absacc"][:, 0:1], 0.0),
             writes=["tnaa"])
    chunks.append(c_init)

    for oc in range(n_oc):
        def c_mean(oc=oc):
            for hh in range(2):
                q = hh
                k.op("G", lambda g, oc=oc, hh=hh, q=q: g.dma_start(
                    out=ft["wbuf"][q], in_=wsrc_l[oc][:, hh * 1024:(hh + 1) * 1024]),
                    writes=[f"tnwb{q}"], dma=True)
                k.op("V", lambda v, q=q: v.tensor_reduce(
                    ft["absacc"][:, 1:2], ft["wbuf"][q], AX.X, ALU.add,
                    apply_absolute_value=True),
                    reads=[f"tnwb{q}"], writes=["tnap"])
                k.op("V", lambda v: v.tensor_tensor(
                    ft["absacc"][:, 0:1], ft["absacc"][:, 0:1],
                    ft["absacc"][:, 1:2], ALU.add),
                    reads=["tnap", "tnaa"], writes=["tnaa"])
        chunks.append(c_mean)

    def c_mid():
        def ftot(t):
            return t.matmul(pb[6][0:1, 0:1], ft["absacc"][:, 0:1],
                            c["ones_col32"], start=True, stop=True)
        k.op("T", ftot, reads=["tnaa", "ones_col32"], writes=["pb6"])
        k.op("V", lambda v: v.tensor_scalar(ft["sv"][0:1, 0:1], pb[6][0:1, 0:1],
                                            1.0 / nel, 1e-5, ALU.mult, ALU.max),
             reads=["pb6"], writes=["tnsv"])
        k.op("V", lambda v: v.reciprocal(ft["sv"][0:1, 1:2], ft["sv"][0:1, 0:1]),
             reads=["tnsv"], writes=["tnsv"])
        k.op("G", lambda g: g.dma_start(out=mean_out, in_=ft["sv"][0:1, 0:1]),
             reads=["tnsv"], writes=[uid + "mo"], dma=True)

        def fbc(t):
            return t.matmul(pb[6][:, 0:1], c["ones1_32"], ft["sv"][0:1, 1:2],
                            start=True, stop=True)
        k.op("T", fbc, reads=["tnsv", "ones1_32"], writes=["pb6"])
        k.op("V", lambda v: v.tensor_copy(ft["s1b"], pb[6][:, 0:1]),
             reads=["pb6"], writes=["tns1b"])
    chunks.append(c_mid)

    for oc in range(n_oc):
        def c_quant(oc=oc):
            for hh in range(2):
                q = hh
                dst = w16dst[oc][:, hh * 1024:(hh + 1) * 1024]
                k.op("G", lambda g, oc=oc, hh=hh, q=q: g.dma_start(
                    out=ft["wbuf"][q], in_=wsrc_l[oc][:, hh * 1024:(hh + 1) * 1024]),
                    writes=[f"tnwb{q}"], dma=True)
                k.op("V", lambda v, q=q: v.tensor_scalar(
                    ft["wbuf"][q], ft["wbuf"][q], ft["s1b"], None, ALU.mult),
                    reads=[f"tnwb{q}", "tns1b"], writes=[f"tnwb{q}"])
                k.op("V", lambda v, q=q: v.tensor_scalar(
                    ft["wt16"][q], ft["wbuf"][q], 0.5, None, ALU.is_gt),
                    reads=[f"tnwb{q}"], writes=[f"tnwa{q}"])
                k.op("V", lambda v, q=q: v.tensor_scalar(
                    ft["wbuf"][q], ft["wbuf"][q], -0.5, None, ALU.is_lt),
                    reads=[f"tnwb{q}"], writes=[f"tnwb{q}"])
                k.op("V", lambda v, q=q: v.tensor_tensor(
                    ft["wt16"][q], ft["wt16"][q], ft["wbuf"][q], ALU.subtract),
                    reads=[f"tnwa{q}", f"tnwb{q}"], writes=[f"tnwt{q}"])
                k.op("G", lambda g, dst=dst, q=q: g.dma_start(
                    out=dst, in_=ft["wt16"][q]),
                    reads=[f"tnwt{q}"], writes=[f"{uid}wo{oc}"], dma=True)
        chunks.append(c_quant)
    return chunks


def prep_dequant_host(inputs, core):
    gu_pk = np.ascontiguousarray(np.asarray(inputs["L1_ffn_gate_up_packed"]))
    dn_pk = np.ascontiguousarray(np.asarray(inputs["L1_ffn_down_packed"]))
    gu_b = gu_pk.view(np.uint8).reshape(11264, 1024)
    dn_b = dn_pk.view(np.uint8).reshape(2048, 2816)
    gu_sc = np.ascontiguousarray(np.asarray(inputs["L1_ffn_gate_up_scales"]).T)
    dn_sc = np.ascontiguousarray(np.asarray(inputs["L1_ffn_down_scales"]).T)
    return {
        "gu_pk": np.ascontiguousarray(gu_b[core * 1408:(core + 1) * 1408]),
        "gu_sc": np.ascontiguousarray(gu_sc[core * 1408:(core + 1) * 1408]),
        "dn_pk": np.ascontiguousarray(dn_b[core * 256:(core + 1) * 256]),
        "dn_sc": np.ascontiguousarray(dn_sc[core * 256:(core + 1) * 256]),
    }



def alloc_ffn1(k, ar=None):
    sb = _SbAlloc(k, ar)
    f = {}
    f["xc"] = [sb(f"f1_xc{i}", [128, 512], F32) for i in range(2)]
    f["x16c"] = [sb(f"f1_x16c{i}", [128, 512], BF16) for i in range(2)]
    f["sq16c"] = [sb(f"f1_sq16c{i}", [128, 512], BF16) for i in range(2)]
    f["statrow"] = sb("f1_stat", [1, 2048], F32)
    f["htmp"] = [sb(f"f1_ht{i}", [128, 512], F32) for i in range(2)]
    f["xn16"] = sb("f1_xn16", [128, NCH * 512], BF16)
    f["hbuf"] = sb("f1_hbuf", [128, 44 * 512], BF16)
    f["ybuf"] = sb("f1_ybuf", [128, NCH * 512], F32)
    f["wslot"] = [sb(f"f1_ws{i}", [128, 2048], BF16) for i in range(6)]
    f["dslot"] = [sb(f"f1_ds{i}", [128, 5632], BF16) for i in range(3)]
    f["sg"] = [sb(f"f1_sg{i}", [128, 512], F32) for i in range(4)]
    f["yrow"] = sb("f1_yrow", [1, 512], F32)
    f["outc"] = [sb(f"f1_outc{i}", [128, 512], F32) for i in range(2)]
    return f


def emit_ffn1(k, c, f, io, xdram, ydram, uid, wdeps=()):
    pb = c["pb"]
    xr = xdram.rearrange("(c p) t -> p c t", p=128)
    guT, dnT = io["guT_l"], io["dnT_l"]

    for it in range(4):
        th = it * 512
        emit_ln_half(k, c, f, xr, th, with_gate=False)
        for ch in range(NCH):
            q = ch % 2
            k.op("Y", lambda y, ch=ch, q=q, th=th: y.dma_start(
                out=f["xc"][q], in_=xr[:, ch, th:th + 512]),
                writes=[f"xc{q}"], dma=True)
            k.op("V", lambda v, ch=ch, q=q: v.tensor_tensor(
                f["htmp"][q], f["xc"][q], pb[3][:, :], ALU.mult),
                reads=[f"xc{q}", "pb3"], writes=[f"htmp{q}"])

            def fxn(v, ch=ch, q=q):
                dst = f["xn16"][:, ch * 512:(ch + 1) * 512]
                return v.tensor_tensor(dst, f["htmp"][q], pb[4][:, :], ALU.add)
            k.op("V", fxn, reads=[f"htmp{q}", "pb4"], writes=[f"f1xn{ch}"])
        xn_deps = [f"f1xn{ch}" for ch in range(NCH)]

        for j in range(44):
            q = j % 2
            q3 = j % 3
            bb = 4 * q
            for half, slot in ((0, 2 * q3), (1, 2 * q3 + 1)):
                src = guT[j + 44 * half]
                k.op("G", lambda g, src=src, slot=slot: g.dma_start(
                    out=f["wslot"][slot], in_=src),
                    reads=list(wdeps), writes=[f"f1ws{slot}"], dma=True)

            def fup(t, q3=q3, bb=bb):
                for ch in range(NCH):
                    t.matmul(pb[bb][:, :], f["wslot"][2 * q3][:, ch * 128:(ch + 1) * 128],
                             f["xn16"][:, ch * 512:(ch + 1) * 512],
                             start=(ch == 0), stop=(ch == NCH - 1))
                for ch in range(NCH):
                    last = t.matmul(pb[bb + 2][:, :],
                                    f["wslot"][2 * q3 + 1][:, ch * 128:(ch + 1) * 128],
                                    f["xn16"][:, ch * 512:(ch + 1) * 512],
                                    start=(ch == 0), stop=(ch == NCH - 1))
                return last
            k.op("T", fup, reads=[f"f1ws{2*q3}", f"f1ws{2*q3+1}"] + xn_deps,
                 writes=[f"pb{bb}", f"pb{bb+2}"])

            k.op("S", lambda s, q=q, bb=bb: s.activation(
                f["sg"][2 * q], pb[bb][:, :], AF.Silu),
                reads=[f"pb{bb}"], writes=[f"f1sg{2*q}"])

            def fh(v, j=j, q=q, bb=bb):
                dst = f["hbuf"][:, j * 512:(j + 1) * 512]
                return v.tensor_tensor(dst, f["sg"][2 * q], pb[bb + 2][:, :], ALU.mult)
            k.op("V", fh, reads=[f"f1sg{2*q}", f"pb{bb+2}"], writes=[f"f1hb{j}"])

        for oc in range(NCH):
            slot = oc % 3
            k.op("G", lambda g, oc=oc, slot=slot: g.dma_start(
                out=f["dslot"][slot], in_=dnT[oc]),
                reads=list(wdeps), writes=[f"f1ds{slot}"], dma=True)
            bank = 5 + oc % 2

            def fdn(t, oc=oc, slot=slot, bank=bank):
                for ci in range(44):
                    last = t.matmul(pb[bank][:, :],
                                    f["dslot"][slot][:, ci * 128:(ci + 1) * 128],
                                    f["hbuf"][:, ci * 512:(ci + 1) * 512],
                                    start=(ci == 0), stop=(ci == 43))
                return last
            k.op("T", fdn, reads=[f"f1ds{slot}"] + [f"f1hb{j}" for j in range(44)],
                 writes=[f"pb{bank}"])
            k.op("V", lambda v, oc=oc, bank=bank: v.tensor_copy(
                f["ybuf"][:, oc * 512:(oc + 1) * 512], pb[bank][:, :]),
                reads=[f"pb{bank}"], writes=[f"f1y{oc}"])
            k.op("S", lambda s, oc=oc: s.activation(
                f["sq16c"][oc % 2], f["ybuf"][:, oc * 512:(oc + 1) * 512], AF.Square),
                reads=[f"f1y{oc}"], writes=[f"sq16c{oc%2}"])

            def fst(t, oc=oc):
                return t.matmul(pb[7][0:1, :], c["ones16"], f["sq16c"][oc % 2],
                                start=(oc == 0), stop=(oc == NCH - 1))
            k.op("T", fst, reads=[f"sq16c{oc%2}", "ones16"], writes=["pb7"])

        k.op("S", lambda s: s.activation(f["yrow"], pb[7][0:1, :], AF.Sqrt,
                                         bias=c["kv"][0:1, 2:3], scale=1.0 / D),
             reads=["pb7", "kv"], writes=["yrow"])
        k.op("V", lambda v: v.reciprocal(f["yrow"], f["yrow"]),
             reads=["yrow"], writes=["yrow"])

        def fyb(t):
            return t.matmul(pb[7][:, :], c["ones1_32"], f["yrow"], start=True, stop=True)
        k.op("T", fyb, reads=["yrow", "ones1_32"], writes=["pb7"])

        for ch in range(NCH):
            q = ch % 2
            k.op("Y", lambda y, ch=ch, q=q, th=th: y.dma_start(
                out=f["xc"][q], in_=xr[:, ch, th:th + 512]),
                writes=[f"xc{q}"], dma=True)
            k.op("V", lambda v, ch=ch, q=q: v.tensor_tensor(
                f["htmp"][q], f["ybuf"][:, ch * 512:(ch + 1) * 512], pb[7][:, :], ALU.mult),
                reads=[f"f1y{ch}", "pb7"], writes=[f"htmp{q}"])

            def fo(v, ch=ch, q=q):
                return v.scalar_tensor_tensor(f["outc"][q], f["htmp"][q], 0.5,
                                              f["xc"][q], ALU.mult, ALU.add)
            k.op("V", fo, reads=[f"htmp{q}", f"xc{q}"], writes=[f"outc{q}"])
            k.op("G", lambda g, ch=ch, q=q, th=th: g.dma_start(
                out=ydram[ch * 128:(ch + 1) * 128, th:th + 512], in_=f["outc"][q]),
                reads=[f"outc{q}"], writes=[f"{uid}f1y{ch}_{th}"], dma=True)


def prep_ffn1_host(gu16, dn16):
    guT = np.ascontiguousarray(
        gu16.T.reshape(NCH, 128, 88, 128).transpose(2, 1, 0, 3).reshape(88, 128, 2048))
    dnT = np.ascontiguousarray(
        dn16.T.reshape(44, 128, 16, 128).transpose(2, 1, 0, 3).reshape(16, 128, 44 * 128))
    return {"guT_l": guT, "dnT_l": dnT}


def build_ffn1_launch():
    k = KB()
    xin = k.dram_in("xT", [D, TOK])
    yout = k.dram_out("yT", [D, TOK])
    dr = _const_dram_decl(k)
    c = alloc_common(k)
    load_consts(k, c, dr)
    io = {"guT_l": k.dram_in("guT_l", [88, 128, 2048], BF16),
          "dnT_l": k.dram_in("dnT_l", [16, 128, 44 * 128], BF16)}
    f = alloc_ffn1(k)
    emit_ffn1(k, c, f, io, xin, yout, uid="a")
    return k.finish()





SCL128 = float(1.0 / np.sqrt(128.0))
RMS1_EPS = 1e-6


def alloc_attn1a(k, ar=None):
    sb = _SbAlloc(k, ar)
    f = {}
    f["wbuf"] = [sb(f"a1_wb{i}", [128, 2048], F32) for i in range(2)]
    f["wt16"] = [sb(f"a1_wt{i}", [128, 2048], BF16) for i in range(2)]
    f["absacc"] = sb("a1_absacc", [128, 2], F32)
    f["sv"] = sb("a1_sv", [1, 16], F32)
    f["s1b"] = sb("a1_s1b", [128, 1], F32)
    f["m1b"] = sb("a1_m1b", [128, 1], F32)
    f["xc"] = [sb(f"a1_xc{i}", [128, 512], F32) for i in range(2)]
    f["x16c"] = [sb(f"a1_x16c{i}", [128, 512], BF16) for i in range(2)]
    f["sq16c"] = [sb(f"a1_sq16c{i}", [128, 512], BF16) for i in range(2)]
    f["statrow"] = sb("a1_stat", [1, 2048], F32)
    f["htmp"] = [sb(f"a1_ht{i}", [128, 512], F32) for i in range(2)]
    f["zn16"] = sb("a1_zn16", [128, NCH * 512], BF16)
    f["r2row"] = sb("a1_r2row", [1, 512], F32)
    f["qnw"] = sb("a1_qnw", [128, 16], F32)
    f["qkvsb"] = sb("a1_qkvsb", [128, 48 * 512], BF16)
    f["wslot"] = [sb(f"a1_ws{i}", [128, 2048], BF16) for i in range(4)]
    f["negs"] = sb("a1_negs", [128, 16], F32)
    f["lamrev"] = sb("a1_lamrev", [128, 16], F32)
    f["lamT"] = sb("a1_lamT", [128, 16], F32)
    f["D2"] = sb("a1_D2", [128, 128], F32)
    f["B2"] = sb("a1_B2", [128, 128], F32)
    f["trevcol"] = sb("a1_trev", [128, 1], F32)
    f["trow"] = sb("a1_trow", [1, 2048], F32)
    f["MT"] = sb("a1_MT", [128, 16 * 128], BF16)
    f["lambc"] = sb("a1_lambc", [128, 16 * 128], BF16)
    f["mtmp"] = sb("a1_mtmp", [128, 128], F32)
    f["rtmp"] = sb("a1_rtmp", [1, 2048], F32)
    f["Ssb"] = sb("a1_S", [128, 16 * 128], F32)
    f["S16"] = sb("a1_S16", [128, 16 * 128], BF16)
    f["vct"] = [sb(f"a1_vct{i}", [128, 128], BF16) for i in range(2)]
    f["kct"] = [sb(f"a1_kct{i}", [128, 128], BF16) for i in range(2)]
    f["kct2"] = [sb(f"a1_kct2{i}", [128, 128], BF16) for i in range(2)]
    f["sc16"] = [sb(f"a1_sc{i}", [128, 128], BF16) for i in range(2)]
    f["qlb"] = [sb(f"a1_qlb{i}", [128, 128], BF16) for i in range(2)]
    f["ost"] = sb("a1_ost", [128, 16 * 512], BF16)
    f["m1r"] = sb("a1_m1r", [1, 1], F32)
    return f


def emit_ternarize(k, c, f, wsrc_l, n_oc, w16dst, uid, mean_out=None):
    pb = c["pb"]
    nel = float(n_oc * 128 * 2048)
    k.op("V", lambda v: v.memset(f["absacc"][:, 0:1], 0.0), writes=[uid + "aa"])
    for oc in range(n_oc):
        q = oc % 2
        k.op("G", lambda g, oc=oc, q=q: g.dma_start(out=f["wbuf"][q], in_=wsrc_l[oc]),
             writes=[f"{uid}wb{q}"], dma=True)
        k.op("V", lambda v, q=q: v.tensor_reduce(
            f["absacc"][:, 1:2], f["wbuf"][q], AX.X, ALU.add, apply_absolute_value=True),
            reads=[f"{uid}wb{q}"], writes=[uid + "ap"])
        k.op("V", lambda v: v.tensor_tensor(f["absacc"][:, 0:1], f["absacc"][:, 0:1],
                                            f["absacc"][:, 1:2], ALU.add),
             reads=[uid + "ap", uid + "aa"], writes=[uid + "aa"])

    def ftot(t):
        return t.matmul(pb[0][0:1, 0:1], f["absacc"][:, 0:1], c["ones_col32"],
                        start=True, stop=True)
    k.op("T", ftot, reads=[uid + "aa", "ones_col32"], writes=["pb0"])
    k.op("V", lambda v: v.tensor_scalar(f["sv"][0:1, 0:1], pb[0][0:1, 0:1],
                                        1.0 / nel, 1e-5, ALU.mult, ALU.max),
         reads=["pb0"], writes=[uid + "sv"])
    k.op("V", lambda v: v.reciprocal(f["sv"][0:1, 1:2], f["sv"][0:1, 0:1]),
         reads=[uid + "sv"], writes=[uid + "sv"])

    def fbc(t):
        t.matmul(pb[0][:, 0:1], c["ones1_32"], f["sv"][0:1, 1:2], start=True, stop=True)
        return t.matmul(pb[1][:, 0:1], c["ones1_32"], f["sv"][0:1, 0:1],
                        start=True, stop=True)
    k.op("T", fbc, reads=[uid + "sv", "ones1_32"], writes=["pb0", "pb1"])
    k.op("V", lambda v: v.tensor_copy(f["s1b"], pb[0][:, 0:1]), reads=["pb0"],
         writes=[uid + "s1b"])
    k.op("V", lambda v: v.tensor_copy(f["m1b"], pb[1][:, 0:1]), reads=["pb1"],
         writes=[uid + "m1b"])
    if mean_out is not None:
        k.op("G", lambda g: g.dma_start(out=mean_out, in_=f["sv"][0:1, 0:1]),
             reads=[uid + "sv"], writes=[uid + "mo"], dma=True)

    for oc in range(n_oc):
        q = oc % 2
        k.op("G", lambda g, oc=oc, q=q: g.dma_start(out=f["wbuf"][q], in_=wsrc_l[oc]),
             writes=[f"{uid}wb{q}"], dma=True)
        k.op("V", lambda v, q=q: v.tensor_scalar(
            f["wbuf"][q], f["wbuf"][q], f["s1b"], None, ALU.mult),
            reads=[f"{uid}wb{q}", uid + "s1b"], writes=[f"{uid}wb{q}"])
        k.op("V", lambda v, q=q: v.tensor_scalar(
            f["htmp"][0 if q == 0 else 1][:, 0:512], f["wbuf"][q][:, 0:512], 0.0, None,
            ALU.bypass), reads=[], writes=["_dummy"]) if False else None
        k.op("V", lambda v, q=q: v.tensor_scalar(
            f["wt16"][q], f["wbuf"][q], 0.5, None, ALU.is_gt),
            reads=[f"{uid}wb{q}"], writes=[f"{uid}wa{q}"])
        k.op("V", lambda v, q=q: v.tensor_scalar(
            f["wbuf"][q], f["wbuf"][q], -0.5, None, ALU.is_lt),
            reads=[f"{uid}wb{q}"], writes=[f"{uid}wb{q}"])
        k.op("V", lambda v, q=q: v.tensor_tensor(
            f["wt16"][q], f["wt16"][q], f["wbuf"][q], ALU.subtract),
            reads=[f"{uid}wa{q}", f"{uid}wb{q}"], writes=[f"{uid}wt{q}"])
        k.op("G", lambda g, oc=oc, q=q: g.dma_start(out=w16dst[oc], in_=f["wt16"][q]),
             reads=[f"{uid}wt{q}"], writes=[f"{uid}wo{oc}"], dma=True)


def emit_attn1a(k, c, f, io, xdram, outs):
    pb = c["pb"]
    xr = xdram.rearrange("(c p) t -> p c t", p=128)
    oL, qT, lamg_d, send_d = outs["oL"], outs["qT"], outs["lamg"], outs["send"]

    for nm in ("D2", "B2", "trevcol", "trow", "qnw"):
        k.op("Y", lambda y, nm=nm: y.dma_start(out=f[nm], in_=io[nm + "_d"]),
             writes=[nm], dma=True)
    k.op("Y", lambda y: y.dma_start(out=f["sv"][0:1, 2:3], in_=io["dummy1"]),
         writes=["svd"], dma=True)
    k.op("Y", lambda y: y.dma_start(out=f["sv"][0:1, 0:16], in_=io["srow_d"]),
         writes=["svs"], dma=True)
    k.op("V", lambda v: v.tensor_scalar_mul(f["rtmp"][0:1, 0:16], f["sv"][0:1, 0:16], -1.0),
         reads=["svs"], writes=["rtmp"])

    def fnegs(t):
        return t.matmul(pb[0][:, 0:16], c["ones1_32"], f["rtmp"][0:1, 0:16],
                        start=True, stop=True)
    k.op("T", fnegs, reads=["rtmp", "ones1_32"], writes=["pb0"])
    k.op("V", lambda v: v.tensor_copy(f["negs"], pb[0][:, 0:16]), reads=["pb0"],
         writes=["negs"])

    for h in range(NCH):
        k.op("V", lambda v, h=h: v.tensor_tensor(
            f["lamrev"][:, h:h + 1], f["trevcol"], f["negs"][:, h:h + 1], ALU.mult),
            reads=["trevcol", "negs"], writes=["lamrev"])
    k.op("S", lambda s: s.activation(f["lamrev"], f["lamrev"], AF.Exp),
         reads=["lamrev"], writes=["lamrev"])
    k.op("V", lambda v: v.tensor_scalar_mul(f["lamT"], f["negs"], 128.0),
         reads=["negs"], writes=["lamT"])
    k.op("S", lambda s: s.activation(f["lamT"], f["lamT"], AF.Exp),
         reads=["lamT"], writes=["lamT"])

    for h in range(NCH):
        k.op("V", lambda v, h=h: v.tensor_scalar(
            f["mtmp"], f["D2"], f["negs"][:, h:h + 1], None, ALU.mult),
            reads=["D2", "negs"], writes=["mtmp"])
        k.op("S", lambda s: s.activation(f["mtmp"], f["mtmp"], AF.Exp),
             reads=["mtmp"], writes=["mtmp"])
        k.op("V", lambda v, h=h: v.tensor_tensor(
            f["MT"][:, h * 128:(h + 1) * 128], f["mtmp"], f["B2"], ALU.mult),
            reads=["mtmp", "B2"], writes=["MT"])
        k.op("V", lambda v, h=h: v.tensor_scalar(
            f["rtmp"][0:1, 0:128], f["D2"][0:1, :], f["negs"][0:1, h:h + 1], None, ALU.mult),
            reads=["D2", "negs"], writes=["rtmp"])
        k.op("S", lambda s: s.activation(f["rtmp"][0:1, 128:256], f["rtmp"][0:1, 0:128],
                                         AF.Exp), reads=["rtmp"], writes=["rtmp"])
        k.op("V", lambda v: v.tensor_copy(f["sq16c"][0][0:1, 0:128],
                                          f["rtmp"][0:1, 128:256]),
             reads=["rtmp"], writes=["sq16c0"])

        def flb(t):
            return t.matmul(pb[1][:, 0:128], c["ones16"][0:1, 0:1].tensor.ap()[0:1, 0:1]
                            if False else f["sq16c"][0][0:1, 0:128], f["sq16c"][0][0:1, 0:128],
                            start=True, stop=True)
        k.op("Y", lambda y: y.dma_start(out=f["x16c"][0][0:1, 0:128],
                                        in_=io["ones1_16d"]),
             writes=["x16c0"], dma=True) if h == 0 else None

        def flb2(t, h=h):
            return t.matmul(pb[1][:, 0:128], f["x16c"][0][0:1, 0:128],
                            f["sq16c"][0][0:1, 0:128], start=True, stop=True)
        k.op("T", flb2, reads=["sq16c0", "x16c0"], writes=["pb1"])
        k.op("V", lambda v, h=h: v.tensor_copy(
            f["lambc"][:, h * 128:(h + 1) * 128], pb[1][:, 0:128]),
            reads=["pb1"], writes=["lambc"])
        k.op("V", lambda v, h=h: v.tensor_scalar(
            f["rtmp"], f["trow"], f["negs"][0:1, h:h + 1], None, ALU.mult),
            reads=["trow", "negs"], writes=["rtmp"])
        k.op("S", lambda s: s.activation(f["rtmp"], f["rtmp"], AF.Exp),
             reads=["rtmp"], writes=["rtmp"])
        k.op("V", lambda v: v.tensor_scalar_mul(f["rtmp"], f["rtmp"], SCL128),
             reads=["rtmp"], writes=["rtmp"])
        k.op("G", lambda g, h=h: g.dma_start(out=lamg_d[h:h + 1, :], in_=f["rtmp"]),
             reads=["rtmp"], writes=[f"lamg{h}"], dma=True)

    if "m1v_d" in io:
        k.op("Y", lambda y: y.dma_start(out=f["m1r"], in_=io["m1v_d"]),
             reads=list(io.get("m1deps", ())), writes=["m1r"], dma=True)
        k.op("T", lambda t: t.matmul(pb[1][:, 0:1], c["ones1_32"], f["m1r"],
                                     start=True, stop=True),
             reads=["m1r", "ones1_32"], writes=["pb1"])
        k.op("V", lambda v: v.tensor_copy(f["absacc"][:, 0:1], pb[1][:, 0:1]),
             reads=["pb1"], writes=["qkvscale"])
    else:
        emit_ternarize(k, c, f, io["qkvT_l"], 48, io["wq16d"], "tq")
        k.op("V", lambda v: v.tensor_copy(f["absacc"][:, 0:1], f["m1b"]),
             reads=["tqm1b"], writes=["qkvscale"])
        emit_ternarize(k, c, f, io["projT_l"], 16, outs["wp16d"], "tp",
                       mean_out=outs["m2v"])

    k.op("V", lambda v: v.memset(f["Ssb"], 0.0), writes=["Sall"])
    k.op("V", lambda v: v.memset(f["S16"], 0.0), writes=["S16all"])

    for it in range(4):
        th = it * 512
        emit_ln_half(k, c, f, xr, th, with_gate=False)
        for ch in range(NCH):
            q = ch % 2
            k.op("Y", lambda y, ch=ch, q=q, th=th: y.dma_start(
                out=f["xc"][q], in_=xr[:, ch, th:th + 512]),
                writes=[f"xc{q}"], dma=True)
            k.op("V", lambda v, ch=ch, q=q: v.tensor_tensor(
                f["htmp"][q], f["xc"][q], pb[3][:, :], ALU.mult),
                reads=[f"xc{q}", "pb3"], writes=[f"htmp{q}"])

            def fz(v, ch=ch, q=q):
                dst = f["zn16"][:, ch * 512:(ch + 1) * 512]
                return v.tensor_tensor(dst, f["htmp"][q], pb[4][:, :], ALU.add)
            k.op("V", fz, reads=[f"htmp{q}", "pb4"], writes=[f"zn{ch}"])
            k.op("S", lambda s, ch=ch, q=q: s.activation(
                f["sq16c"][q], f["zn16"][:, ch * 512:(ch + 1) * 512], AF.Square),
                reads=[f"zn{ch}"], writes=[f"sq16c{q}"])

            def fzs(t, ch=ch, q=q):
                return t.matmul(pb[5][0:1, :], c["ones16"], f["sq16c"][q],
                                start=(ch == 0), stop=(ch == NCH - 1))
            k.op("T", fzs, reads=[f"sq16c{q}", "ones16"], writes=["pb5"])
        k.op("S", lambda s: s.activation(f["r2row"], pb[5][0:1, :], AF.Sqrt,
                                         bias=c["kv"][0:1, 1:2], scale=1.0 / D),
             reads=["pb5", "kv"], writes=["r2row"])
        k.op("V", lambda v: v.reciprocal(f["r2row"], f["r2row"]),
             reads=["r2row"], writes=["r2row"])

        def fr2b(t):
            return t.matmul(pb[6][:, :], c["ones1_32"], f["r2row"], start=True, stop=True)
        k.op("T", fr2b, reads=["r2row", "ones1_32"], writes=["pb6"])
        for ch in range(NCH):
            q = ch % 2
            k.op("V", lambda v, ch=ch, q=q: v.tensor_tensor(
                f["htmp"][q], f["zn16"][:, ch * 512:(ch + 1) * 512], pb[6][:, :], ALU.mult),
                reads=[f"zn{ch}", "pb6"], writes=[f"htmp{q}"])

            def fh2(v, ch=ch, q=q):
                dst = f["zn16"][:, ch * 512:(ch + 1) * 512]
                return v.tensor_scalar(dst, f["htmp"][q], f["qnw"][:, ch:ch + 1],
                                       None, ALU.mult)
            k.op("V", fh2, reads=[f"htmp{q}", "qnw"], writes=[f"zn{ch}"])
        zdeps = [f"zn{ch}" for ch in range(NCH)]

        for oc in range(48):
            slot = oc % 4
            q = oc % 2
            k.op("G", lambda g, oc=oc, slot=slot: g.dma_start(
                out=f["wslot"][slot], in_=io["wq16d"][oc]),
                writes=[f"a1ws{slot}"], dma=True)

            def fqkv(t, oc=oc, slot=slot, q=q):
                for ch in range(NCH):
                    last = t.matmul(pb[q][:, :],
                                    f["wslot"][slot][:, ch * 128:(ch + 1) * 128],
                                    f["zn16"][:, ch * 512:(ch + 1) * 512],
                                    start=(ch == 0), stop=(ch == NCH - 1))
                return last
            k.op("T", fqkv, reads=[f"a1ws{slot}"] + zdeps, writes=[f"pb{q}"])

            def fqe(v, oc=oc, q=q):
                dst = f["qkvsb"][:, oc * 512:(oc + 1) * 512]
                return v.tensor_scalar(dst, pb[q][:, :], f["absacc"][:, 0:1],
                                       None, ALU.mult)
            k.op("V", fqe, reads=[f"pb{q}", "qkvscale"], writes=[f"qk{oc}"])
        for h in range(NCH):
            k.op("G", lambda g, h=h, th=th: g.dma_start(
                out=qT[h * 128:(h + 1) * 128, th:th + 512],
                in_=f["qkvsb"][:, h * 512:(h + 1) * 512]),
                reads=[f"qk{h}"], writes=[f"qTo{h}_{it}"], dma=True)

        for cc in range(4):
            for h in range(NCH):
                q = (cc * NCH + h) % 2
                qsl = f["qkvsb"][:, h * 512 + cc * 128: h * 512 + (cc + 1) * 128]
                ksl = f["qkvsb"][:, (16 + h) * 512 + cc * 128: (16 + h) * 512 + (cc + 1) * 128]
                vsl = f["qkvsb"][:, (32 + h) * 512 + cc * 128: (32 + h) * 512 + (cc + 1) * 128]
                rq, rk, rv = f"qk{h}", f"qk{16+h}", f"qk{32+h}"

                k.op("Y", lambda y, vsl=vsl, q=q: y.dma_start(
                    out=f["vct"][q], in_=vsl, transpose=True),
                    reads=[rv], writes=[f"vct{q}"], dma=True)
                k.op("Y", lambda y, ksl=ksl, q=q: y.dma_start(
                    out=f["kct"][q], in_=ksl, transpose=True),
                    reads=[rk], writes=[f"kct{q}"], dma=True)
                k.op("T", lambda t, ksl=ksl, qsl=qsl, q=q: t.matmul(
                    pb[2 + q][:, 0:128], ksl, qsl, start=True, stop=True),
                    reads=[rk, rq], writes=[f"pb{2+q}"])
                k.op("V", lambda v, h=h, q=q: v.tensor_tensor(
                    f["sc16"][q], pb[2 + q][:, 0:128], f["MT"][:, h * 128:(h + 1) * 128],
                    ALU.mult), reads=[f"pb{2+q}", "MT"], writes=[f"sc16{q}"])
                k.op("V", lambda v, h=h, q=q, qsl=qsl: v.tensor_tensor(
                    f["qlb"][q], qsl, f["lambc"][:, h * 128:(h + 1) * 128], ALU.mult),
                    reads=[rq, "lambc"], writes=[f"qlb{q}"])

                def fic(t, h=h, q=q):
                    t.matmul(pb[4 + q][:, 0:128], f["vct"][q], f["sc16"][q],
                             start=True, stop=False)
                    return t.matmul(pb[4 + q][:, 0:128],
                                    f["S16"][:, h * 128:(h + 1) * 128], f["qlb"][q],
                                    start=False, stop=True)
                k.op("T", fic, reads=[f"vct{q}", f"sc16{q}", f"S16h{h}", f"qlb{q}"],
                     writes=[f"pb{4+q}"])
                k.op("V", lambda v, h=h, cc=cc, q=q: v.tensor_scalar(
                    f["ost"][:, h * 512 + cc * 128: h * 512 + (cc + 1) * 128],
                    pb[4 + q][:, 0:128], SCL128, None, ALU.mult),
                    reads=[f"pb{4+q}"], writes=[f"ost{h}"])
                k.op("V", lambda v, h=h, q=q: v.tensor_scalar(
                    f["kct2"][q], f["kct"][q], f["lamrev"][:, h:h + 1], None, ALU.mult),
                    reads=[f"kct{q}", "lamrev"], writes=[f"kct2{q}"])
                k.op("T", lambda t, q=q: t.matmul(
                    pb[6 + q][:, 0:128], f["kct2"][q], f["vct"][q], start=True, stop=True),
                    reads=[f"kct2{q}", f"vct{q}"], writes=[f"pb{6+q}"])
                k.op("V", lambda v, h=h: v.tensor_scalar(
                    f["Ssb"][:, h * 128:(h + 1) * 128],
                    f["Ssb"][:, h * 128:(h + 1) * 128],
                    f["lamT"][:, h:h + 1], None, ALU.mult),
                    reads=[f"Sh{h}", "lamT"], writes=[f"Sh{h}"])
                k.op("V", lambda v, h=h, q=q: v.tensor_tensor(
                    f["Ssb"][:, h * 128:(h + 1) * 128],
                    f["Ssb"][:, h * 128:(h + 1) * 128],
                    pb[6 + q][:, 0:128], ALU.add),
                    reads=[f"Sh{h}", f"pb{6+q}"], writes=[f"Sh{h}"])
                k.op("V", lambda v, h=h: v.tensor_copy(
                    f["S16"][:, h * 128:(h + 1) * 128],
                    f["Ssb"][:, h * 128:(h + 1) * 128]),
                    reads=[f"Sh{h}"], writes=[f"S16h{h}"])
        for h in range(NCH):
            k.op("G", lambda g, h=h, th=th: g.dma_start(
                out=oL[h * 128:(h + 1) * 128, th:th + 512],
                in_=f["ost"][:, h * 512:(h + 1) * 512]),
                reads=[f"ost{h}"], writes=[f"oLo{h}_{it}"], dma=True)

    k.op("G", lambda g: g.dma_start(out=send_d, in_=f["Ssb"]),
         reads=[f"Sh{h}" for h in range(NCH)] + ["Sall"], writes=["sendo"], dma=True)


def prep_attn1_host(inputs):
    qkvw = np.asarray(inputs["L1_attn_qkv_w"], np.float32)
    projw = np.asarray(inputs["L1_attn_proj_w"], np.float32)
    qkvT = np.ascontiguousarray(
        qkvw.T.reshape(NCH, 128, 48, 128).transpose(2, 1, 0, 3).reshape(48, 128, 2048))
    projT = np.ascontiguousarray(
        projw.T.reshape(NCH, 128, NCH, 128).transpose(2, 1, 0, 3).reshape(NCH, 128, 2048))
    tt = np.arange(128, dtype=np.float32)
    D2 = tt[None, :] - tt[:, None]
    B2 = (D2 >= 0).astype(np.float32)
    return {
        "qkvT_l": qkvT, "projT_l": projT,
        "qnw_d": np.ascontiguousarray(
            np.asarray(inputs["L1_attn_qkv_norm_w"], np.float32).reshape(NCH, 128).T),
        "nw_d": np.ascontiguousarray(
            np.asarray(inputs["L1_attn_norm_w"], np.float32).reshape(NCH, 128).T),
        "pnw_d": np.ascontiguousarray(
            np.asarray(inputs["L1_attn_proj_norm_w"], np.float32).reshape(NCH, 128).T),
        "srow_d": np.asarray(inputs["L1_attn_s"], np.float32).reshape(1, 16),
        "D2_d": np.ascontiguousarray(D2), "B2_d": np.ascontiguousarray(B2),
        "trevcol_d": (128.0 - tt).reshape(128, 1).astype(np.float32),
        "trow_d": np.arange(2048, dtype=np.float32).reshape(1, 2048),
        "dummy1": np.zeros((1, 1), np.float32),
        "ones1_16d": np.ones((1, 128), dtype=BF),
    }


def build_attn1a_launch():
    k = KB()
    xin = k.dram_in("xT", [D, TOK])
    dr = _const_dram_decl(k)
    c = alloc_common(k)
    load_consts(k, c, dr)
    io = {
        "qkvT_l": k.dram_in("qkvT_l", [48, 128, 2048]),
        "projT_l": k.dram_in("projT_l", [16, 128, 2048]),
        "qnw_d": k.dram_in("qnw_d", [128, 16]),
        "srow_d": k.dram_in("srow_d", [1, 16]),
        "D2_d": k.dram_in("D2_d", [128, 128]),
        "B2_d": k.dram_in("B2_d", [128, 128]),
        "trevcol_d": k.dram_in("trevcol_d", [128, 1]),
        "trow_d": k.dram_in("trow_d", [1, 2048]),
        "dummy1": k.dram_in("dummy1", [1, 1]),
        "ones1_16d": k.dram_in("ones1_16d", [1, 128], BF16),
        "wq16d": k.dram_tmp("wq16d", [48, 128, 2048], BF16),
    }
    outs = {
        "oL": k.dram_out("oL", [D, TOK], BF16),
        "qT": k.dram_out("qTo", [D, TOK], BF16),
        "lamg": k.dram_out("lamg", [16, TOK]),
        "send": k.dram_out("send", [128, 2048]),
        "wp16d": k.dram_out("wp16d", [16, 128, 2048], BF16),
        "m2v": k.dram_out("m2v", [1, 1]),
    }
    f = alloc_attn1a(k)
    emit_attn1a(k, c, f, io, xin, outs)
    return k.finish()



def alloc_attn1b(k, ar=None):
    sb = _SbAlloc(k, ar)
    f = {}
    f["Sin16"] = sb("b1_Sin16", [128, 16 * 128], BF16)
    f["Sin32"] = sb("b1_Sin32", [128, 16 * 128], F32)
    f["lgrow"] = sb("b1_lgrow", [1, 2048], F32)
    f["lgb"] = sb("b1_lgb", [128, 512], BF16)
    f["qbuf"] = [sb(f"b1_qb{i}", [128, 512], BF16) for i in range(2)]
    f["obuf"] = [sb(f"b1_ob{i}", [128, 512], BF16) for i in range(2)]
    f["qg"] = [sb(f"b1_qg{i}", [128, 512], BF16) for i in range(2)]
    f["of"] = sb("b1_of", [128, 16 * 512], F32)
    f["h2sb"] = sb("b1_h2", [128, 16 * 512], BF16)
    f["sq16c"] = [sb(f"b1_sq{i}", [128, 512], BF16) for i in range(2)]
    f["tb16"] = [sb(f"b1_tb{i}", [128, 512], BF16) for i in range(2)]
    f["rows"] = sb("b1_rows", [1, 2048], F32)
    f["nw"] = sb("b1_nw", [128, 16], F32)
    f["nwpnw"] = sb("b1_nwpnw", [128, 16], F32)
    f["m2b"] = sb("b1_m2b", [128, 1], F32)
    f["wslot"] = [sb(f"b1_ws{i}", [128, 2048], BF16) for i in range(2)]
    f["xc"] = [sb(f"b1_xc{i}", [128, 512], F32) for i in range(2)]
    f["outc"] = [sb(f"b1_outc{i}", [128, 512], F32) for i in range(2)]
    f["htmp"] = [sb(f"b1_ht{i}", [128, 512], F32) for i in range(2)]
    return f


def emit_attn1b(k, c, f, io, xdram, ydram):
    pb = c["pb"]
    xr = xdram.rearrange("(c p) t -> p c t", p=128)
    k.op("Y", lambda y: y.dma_start(out=f["Sin32"], in_=io["sin_d"]),
         reads=list(io.get("sdeps", ())), writes=["Sin32"], dma=True)
    if "flag_d" in io:
        f["b1fr"] = io["_sb"]("b1_fr", [1, 1], F32)
        f["b1fb"] = io["_sb"]("b1_fb", [128, 1], F32)
        k.op("Y", lambda y: y.dma_start(out=f["b1fr"], in_=io["flag_d"]),
             writes=["b1fr"], dma=True)
        k.op("T", lambda t: t.matmul(c["pb"][1][:, 0:1], c["ones1_32"], f["b1fr"],
                                     start=True, stop=True),
             reads=["b1fr", "ones1_32"], writes=["pb1"])
        k.op("V", lambda v: v.tensor_copy(f["b1fb"], c["pb"][1][:, 0:1]),
             reads=["pb1"], writes=["b1fb"])
        k.op("V", lambda v: v.tensor_scalar(f["Sin32"], f["Sin32"], f["b1fb"],
                                            None, ALU.mult),
             reads=["Sin32", "b1fb"], writes=["Sin32"])
    k.op("V", lambda v: v.tensor_copy(f["Sin16"], f["Sin32"]),
         reads=["Sin32"], writes=["Sin16"])
    k.op("Y", lambda y: y.dma_start(out=f["nw"], in_=io["nw_d"]), writes=["nw"], dma=True)
    k.op("Y", lambda y: y.dma_start(out=f["nwpnw"], in_=io["pnw_d"]),
         writes=["nwpnw"], dma=True)
    k.op("V", lambda v: v.tensor_tensor(f["nwpnw"], f["nwpnw"], f["nw"], ALU.mult),
         reads=["nwpnw", "nw"], writes=["nwpnw"])
    k.op("Y", lambda y: y.dma_start(out=f["rows"][0:1, 0:1], in_=io["m2v_d"]),
         writes=["rows"], dma=True)

    def fm2b(t):
        return t.matmul(pb[0][:, 0:1], c["ones1_32"], f["rows"][0:1, 0:1],
                        start=True, stop=True)
    k.op("T", fm2b, reads=["rows", "ones1_32"], writes=["pb0"])
    k.op("V", lambda v: v.tensor_copy(f["m2b"], pb[0][:, 0:1]), reads=["pb0"],
         writes=["m2b"])

    for tb in range(4):
        th = tb * 512
        for h in range(NCH):
            q = h % 2
            if tb == 0 and True:
                pass
            k.op("Y", lambda y, h=h: y.dma_start(
                out=f["lgrow"], in_=io["lamg_d"][h:h + 1, :]),
                writes=["lgrow"], dma=True) if tb == 0 or True else None

            def flgb(t, th=th):
                return t.matmul(pb[2][:, :], c["ones1_32"], f["lgrow"][0:1, th:th + 512],
                                start=True, stop=True)
            k.op("T", flgb, reads=["lgrow", "ones1_32"], writes=["pb2"])
            k.op("V", lambda v: v.tensor_copy(f["lgb"], pb[2][:, :]),
                 reads=["pb2"], writes=["lgb"])
            k.op("Y", lambda y, h=h, th=th, q=q: y.dma_start(
                out=f["qbuf"][q], in_=io["qT_d"][h * 128:(h + 1) * 128, th:th + 512]),
                writes=[f"qb{q}"], dma=True)
            k.op("V", lambda v, q=q: v.tensor_tensor(f["qg"][q], f["qbuf"][q],
                                                     f["lgb"], ALU.mult),
                 reads=[f"qb{q}", "lgb"], writes=[f"qg{q}"])

            def fcor(t, h=h, q=q):
                return t.matmul(pb[4 + q][:, :], f["Sin16"][:, h * 128:(h + 1) * 128],
                                f["qg"][q], start=True, stop=True)
            k.op("T", fcor, reads=["Sin16", f"qg{q}"], writes=[f"pb{4+q}"])
            k.op("Y", lambda y, h=h, th=th, q=q: y.dma_start(
                out=f["obuf"][q], in_=io["oL_d"][h * 128:(h + 1) * 128, th:th + 512]),
                writes=[f"ob{q}"], dma=True)

            def fof(v, h=h, q=q):
                dst = f["of"][:, h * 512:(h + 1) * 512]
                return v.tensor_tensor(dst, f["obuf"][q], pb[4 + q][:, :], ALU.add)
            k.op("V", fof, reads=[f"ob{q}", f"pb{4+q}"], writes=[f"of{h}"])

        for ch in range(NCH):
            q = ch % 2
            k.op("S", lambda s, ch=ch, q=q: s.activation(
                f["sq16c"][q], f["of"][:, ch * 512:(ch + 1) * 512], AF.Square),
                reads=[f"of{ch}"], writes=[f"sq16c{q}"])
            k.op("T", lambda t, ch=ch, q=q: t.matmul(
                pb[6][0:1, :], c["ones16"], f["sq16c"][q],
                start=(ch == 0), stop=(ch == NCH - 1)),
                reads=[f"sq16c{q}", "ones16"], writes=["pb6"])
            k.op("V", lambda v, ch=ch, q=q: v.tensor_scalar(
                f["tb16"][q], f["of"][:, ch * 512:(ch + 1) * 512],
                f["nw"][:, ch:ch + 1], None, ALU.mult),
                reads=[f"of{ch}", "nw"], writes=[f"tb{q}"])
            k.op("S", lambda s, ch=ch, q=q: s.activation(
                f["sq16c"][q], f["tb16"][q], AF.Square),
                reads=[f"tb{q}"], writes=[f"sq16c{q}"])
            k.op("T", lambda t, ch=ch, q=q: t.matmul(
                pb[7][0:1, :], c["ones16"], f["sq16c"][q],
                start=(ch == 0), stop=(ch == NCH - 1)),
                reads=[f"sq16c{q}", "ones16"], writes=["pb7"])

        rw = f["rows"]
        k.op("S", lambda s: s.activation(rw[0:1, 512:1024], pb[6][0:1, :], AF.Sqrt,
                                         bias=c["kv"][0:1, 2:3], scale=1.0 / D),
             reads=["pb6", "kv"], writes=["rows"])
        k.op("V", lambda v: v.reciprocal(rw[0:1, 512:1024], rw[0:1, 512:1024]),
             reads=["rows"], writes=["rows"])
        k.op("V", lambda v: v.tensor_tensor(rw[0:1, 1024:1536], rw[0:1, 512:1024],
                                            rw[0:1, 512:1024], ALU.mult),
             reads=["rows"], writes=["rows"])
        k.op("V", lambda v: v.tensor_scalar(rw[0:1, 1536:2048], pb[7][0:1, :],
                                            1.0 / D, None, ALU.mult),
             reads=["pb7"], writes=["rows"])
        k.op("V", lambda v: v.tensor_tensor(rw[0:1, 1536:2048], rw[0:1, 1536:2048],
                                            rw[0:1, 1024:1536], ALU.mult),
             reads=["rows"], writes=["rows"])
        k.op("S", lambda s: s.activation(rw[0:1, 1536:2048], rw[0:1, 1536:2048], AF.Sqrt,
                                         bias=c["kv"][0:1, 1:2]),
             reads=["rows", "kv"], writes=["rows"])
        k.op("V", lambda v: v.reciprocal(rw[0:1, 1536:2048], rw[0:1, 1536:2048]),
             reads=["rows"], writes=["rows"])
        k.op("V", lambda v: v.tensor_tensor(rw[0:1, 1024:1536], rw[0:1, 512:1024],
                                            rw[0:1, 1536:2048], ALU.mult),
             reads=["rows"], writes=["rows"])

        def frrb(t):
            return t.matmul(pb[2][:, :], c["ones1_32"], rw[0:1, 1024:1536],
                            start=True, stop=True)
        k.op("T", frrb, reads=["rows", "ones1_32"], writes=["pb2"])

        for ch in range(NCH):
            q = ch % 2
            k.op("V", lambda v, ch=ch, q=q: v.tensor_tensor(
                f["htmp"][q], f["of"][:, ch * 512:(ch + 1) * 512], pb[2][:, :], ALU.mult),
                reads=[f"of{ch}", "pb2"], writes=[f"htmp{q}"])

            def fh2(v, ch=ch, q=q):
                dst = f["h2sb"][:, ch * 512:(ch + 1) * 512]
                return v.tensor_scalar(dst, f["htmp"][q], f["nwpnw"][:, ch:ch + 1],
                                       None, ALU.mult)
            k.op("V", fh2, reads=[f"htmp{q}", "nwpnw"], writes=[f"h2{ch}"])
        h2deps = [f"h2{ch}" for ch in range(NCH)]

        for oc in range(NCH):
            ws = oc % 2
            k.op("G", lambda g, oc=oc, ws=ws: g.dma_start(
                out=f["wslot"][ws], in_=io["wp16_d"][oc]),
                writes=[f"b1ws{ws}"], dma=True)
            q = oc % 2

            def fpj(t, oc=oc, ws=ws, q=q):
                for ch in range(NCH):
                    last = t.matmul(pb[4 + q][:, :],
                                    f["wslot"][ws][:, ch * 128:(ch + 1) * 128],
                                    f["h2sb"][:, ch * 512:(ch + 1) * 512],
                                    start=(ch == 0), stop=(ch == NCH - 1))
                return last
            k.op("T", fpj, reads=[f"b1ws{ws}"] + h2deps, writes=[f"pb{4+q}"])
            k.op("Y", lambda y, oc=oc, th=th, q=q: y.dma_start(
                out=f["xc"][q], in_=xr[:, oc, th:th + 512]),
                writes=[f"xc{q}"], dma=True)

            def fout(v, q=q):
                return v.scalar_tensor_tensor(f["outc"][q], pb[4 + q][:, :], f["m2b"],
                                              f["xc"][q], ALU.mult, ALU.add)
            k.op("V", fout, reads=[f"pb{4+q}", f"xc{q}", "m2b"], writes=[f"outc{q}"])
            k.op("G", lambda g, oc=oc, th=th, q=q: g.dma_start(
                out=ydram[oc * 128:(oc + 1) * 128, th:th + 512], in_=f["outc"][q]),
                reads=[f"outc{q}"], writes=[f"b1y{oc}_{th}"], dma=True)


def build_attn1b_launch():
    k = KB()
    xin = k.dram_in("xT", [D, TOK])
    yout = k.dram_out("yT", [D, TOK])
    dr = _const_dram_decl(k)
    c = alloc_common(k)
    load_consts(k, c, dr)
    io = {
        "sin_d": k.dram_in("sin_d", [128, 2048]),
        "lamg_d": k.dram_in("lamg_d", [16, TOK]),
        "qT_d": k.dram_in("qT_d", [D, TOK], BF16),
        "oL_d": k.dram_in("oL_d", [D, TOK], BF16),
        "wp16_d": k.dram_in("wp16_d", [16, 128, 2048], BF16),
        "m2v_d": k.dram_in("m2v_d", [1, 1]),
        "nw_d": k.dram_in("nw_d", [128, 16]),
        "pnw_d": k.dram_in("pnw_d", [128, 16]),
    }
    f = alloc_attn1b(k)
    emit_attn1b(k, c, f, io, xin, yout)
    return k.finish()



_PROGS = {}


def _prog(name, builder):
    if name not in _PROGS:
        _PROGS[name] = builder()
    return _PROGS[name]


def _split_x(x):
    out = []
    for core in range(NCORE):
        b, h = core // 2, core % 2
        out.append(np.ascontiguousarray(x[b, h * 2048:(h + 1) * 2048, :].T))
    return out


def _join_y(parts):
    y = np.empty((B, SEQ, D), np.float32)
    for core in range(NCORE):
        b, h = core // 2, core % 2
        y[b, h * 2048:(h + 1) * 2048, :] = parts[core].T
    return y


def kernel_unfused(**inputs):
    consts = const_host()
    cores = list(range(NCORE))
    x = np.asarray(inputs["L0_attn_x"], np.float32)
    xs = _split_x(x)

    h0 = prep_ffn0_host(inputs)
    nc = _prog("ffn0", build_ffn0_launch)
    maps = [dict(xT=xs[c], **h0, **consts) for c in cores]
    r = run_bass_kernel_spmd(nc, maps, core_ids=cores)
    xs = [np.asarray(r.results[c]["yT"]) for c in cores]

    ha = prep_attn0_host(inputs)
    nc = _prog("attn0pre", build_attn0pre_launch)
    maps = [dict(xT=xs[c], wql=ha["wql"], **consts) for c in cores]
    r = run_bass_kernel_spmd(nc, maps, core_ids=cores)
    wparts = [np.asarray(r.results[c]["wT"]) for c in cores]

    nc = _prog("attn0core", build_attn0core_launch)
    maps = []
    for c in cores:
        b, h = c // 2, c % 2
        maps.append(dict(w0=wparts[2 * b], w1=wparts[2 * b + 1], xT=xs[c],
                         wpl=ha["wpl"], g1=ha["g1"], g2=ha["g2"], g3=ha["g3"],
                         ones1_16d=np.ones((1, 128), dtype=BF),
                         flag_d=np.full((1, 1), float(h), np.float32), **consts))
    r = run_bass_kernel_spmd(nc, maps, core_ids=cores)
    xs = [np.asarray(r.results[c]["yT"]) for c in cores]

    nc = _prog("ffn0", build_ffn0_launch)
    maps = [dict(xT=xs[c], **h0, **consts) for c in cores]
    r = run_bass_kernel_spmd(nc, maps, core_ids=cores)
    xs = [np.asarray(r.results[c]["yT"]) for c in cores]

    nc = _prog("dequant", build_dequant_launch)
    maps = [prep_dequant_host(inputs, c) for c in cores]
    r = run_bass_kernel_spmd(nc, maps, core_ids=cores)
    gu16 = np.concatenate([np.asarray(r.results[c]["gu_out"]) for c in cores], axis=0)
    dn16 = np.concatenate([np.asarray(r.results[c]["dn_out"]) for c in cores], axis=0)
    h1 = prep_ffn1_host(gu16, dn16)

    nc = _prog("ffn1", build_ffn1_launch)
    maps = [dict(xT=xs[c], **h1, **consts) for c in cores]
    r = run_bass_kernel_spmd(nc, maps, core_ids=cores)
    xs = [np.asarray(r.results[c]["yT"]) for c in cores]

    hb = prep_attn1_host(inputs)
    nc = _prog("attn1a", build_attn1a_launch)
    a_keys = ("qkvT_l", "projT_l", "qnw_d", "srow_d", "D2_d", "B2_d",
              "trevcol_d", "trow_d", "dummy1", "ones1_16d")
    maps = [dict(xT=xs[c], **{kk: hb[kk] for kk in a_keys}, **consts) for c in cores]
    ra = run_bass_kernel_spmd(nc, maps, core_ids=cores)

    nc = _prog("attn1b", build_attn1b_launch)
    maps = []
    for c in cores:
        b, h = c // 2, c % 2
        partner = 2 * b + (1 - h)
        res = ra.results[c]
        sin = (np.asarray(ra.results[partner]["send"], np.float32) if h == 1
               else np.zeros((128, 2048), np.float32))
        maps.append(dict(xT=xs[c], sin_d=sin, lamg_d=np.asarray(res["lamg"]),
                         qT_d=np.asarray(res["qTo"]), oL_d=np.asarray(res["oL"]),
                         wp16_d=np.asarray(res["wp16d"]), m2v_d=np.asarray(res["m2v"]),
                         nw_d=hb["nw_d"], pnw_d=hb["pnw_d"], **consts))
    r = run_bass_kernel_spmd(nc, maps, core_ids=cores)
    xs = [np.asarray(r.results[c]["yT"]) for c in cores]

    nc = _prog("ffn1", build_ffn1_launch)
    maps = [dict(xT=xs[c], **h1, **consts) for c in cores]
    r = run_bass_kernel_spmd(nc, maps, core_ids=cores)
    xs = [np.asarray(r.results[c]["yT"]) for c in cores]

    return _join_y(xs)



def build_fused():
    k = KB()
    x0 = k.dram_in("xT", [D, TOK])
    yout = k.dram_out("yT", [D, TOK])
    t = [k.dram_tmp(f"t{i}", [D, TOK]) for i in range(6)]
    dr = _const_dram_decl(k)
    c = alloc_common(k)
    load_consts(k, c, dr)
    flag_d = k.dram_in("flag_d", [1, 1])

    io0 = decl_ffn0_inputs(k)
    wql = k.dram_in("wql", [NCH, 128, 2048])
    wpl = k.dram_in("wpl", [NCH, 128, 2048])
    g1 = k.dram_in("g1", [128, 1])
    g2 = k.dram_in("g2", [128, 1])
    g3 = k.dram_in("g3", [128, 1])
    ones1_16d = k.dram_in("ones1_16d", [1, 128], BF16)
    gu_pk = k.dram_in("gu_pk", [1408, 1024], U8)
    gu_sc = k.dram_in("gu_sc", [1408, 16], mybir.dt.float16)
    dn_pk = k.dram_in("dn_pk", [256, 2816], U8)
    dn_sc = k.dram_in("dn_sc", [256, 44], mybir.dt.float16)
    a1in = {
        "qkvT_l": k.dram_in("qkvT_l", [48, 128, 2048]),
        "projT_l": k.dram_in("projT_l", [16, 128, 2048]),
        "qnw_d": k.dram_in("qnw_d", [128, 16]),
        "srow_d": k.dram_in("srow_d", [1, 16]),
        "D2_d": k.dram_in("D2_d", [128, 128]),
        "B2_d": k.dram_in("B2_d", [128, 128]),
        "trevcol_d": k.dram_in("trevcol_d", [128, 1]),
        "trow_d": k.dram_in("trow_d", [1, 2048]),
        "dummy1": k.dram_in("dummy1", [1, 1]),
        "ones1_16d": ones1_16d,
        "wq16d": k.dram_tmp("wq16d", [48, 128, 2048], BF16),
    }
    nw_d = k.dram_in("nw_d", [128, 16])
    pnw_d = k.dram_in("pnw_d", [128, 16])

    ccw_in = k.dram_tmp("ccw_in", [D, TOK], BF16)
    ccw_out = k.dram_tmp("ccw_out", [2 * D, TOK], BF16)
    ccs_in = k.dram_tmp("ccs_in", [128, 2048])
    ccs_out = k.dram_tmp("ccs_out", [256, 2048])
    ccgu_in = k.dram_tmp("ccgu_in", [11 * 128, 2048], BF16)
    ccgu_out = k.dram_tmp("ccgu_out", [88 * 128, 2048], BF16)
    ccdn_in = k.dram_tmp("ccdn_in", [2 * 128, 44 * 128], BF16)
    ccdn_out = k.dram_tmp("ccdn_out", [16 * 128, 44 * 128], BF16)
    oL_d = k.dram_tmp("oL_s", [D, TOK], BF16)
    qT_d = k.dram_tmp("qT_s", [D, TOK], BF16)
    lamg_d = k.dram_tmp("lamg_s", [16, TOK])
    wp16d = k.dram_tmp("wp16_s", [16, 128, 2048], BF16)
    m2v_d = k.dram_tmp("m2v_s", [1, 1])
    m1v_d = k.dram_tmp("m1v_s", [1, 1])

    ar = Arena(k, "arena", 200 * 1024)
    PAIRS = [[0, 1], [2, 3], [4, 5], [6, 7]]
    ALL8 = [list(range(8))]

    f0 = alloc_ffn0(k, ar)
    emit_ffn0_init(k, c, f0, io0)
    emit_ffn0(k, c, f0, io0, x0, t[1], uid="fa")
    k.fence(); ar.reset()

    fap = alloc_attn0pre(k, ar)
    fdq = alloc_dequant2(k, ar)
    dq_tiles = []
    for r in range(11):
        dq_tiles.append((gu_pk[r * 128:(r + 1) * 128, :],
                         gu_sc[r * 128:(r + 1) * 128, :],
                         ccgu_in[r * 128:(r + 1) * 128, :], 1024, 16))
    for r in range(2):
        for half in range(2):
            dq_tiles.append((
                dn_pk[r * 128:(r + 1) * 128, half * 1408:(half + 1) * 1408],
                dn_sc[r * 128:(r + 1) * 128, half * 22:(half + 1) * 22],
                ccdn_in[r * 128:(r + 1) * 128, half * 2816:(half + 1) * 2816],
                1408, 22))
    dq_chunks, dq_res = make_dequant_chunks(k, fdq, dq_tiles)
    dq_pacer = Pacer(dq_chunks, 4 * 33)
    emit_attn0pre(k, c, fap, {"wql": wql}, t[1], ccw_in, filler=dq_pacer)
    dq_pacer.flush()
    wres = [f"wout{oc}_{th}" for oc in range(NCH) for th in (0, 512, 1024, 1536)]
    k.cc("AllGather", ALU.bypass, PAIRS, [ccw_in.tensor.ap()], [ccw_out.tensor.ap()],
         reads=wres, writes=["wex"])
    k.cc("AllGather", ALU.bypass, ALL8, [ccgu_in.tensor.ap()], [ccgu_out.tensor.ap()],
         reads=dq_res, writes=["guT"])
    k.cc("AllGather", ALU.bypass, ALL8, [ccdn_in.tensor.ap()], [ccdn_out.tensor.ap()],
         reads=dq_res, writes=["dnT"])
    k.fence(); ar.reset()

    fac = alloc_attn0core(k, ar)
    ioac = {"wpl": wpl, "g1": g1, "g2": g2, "g3": g3, "ones1_16d": ones1_16d,
            "flag_d": flag_d, "wdeps": ("wex",)}
    emit_attn0core(k, c, fac, ioac, ccw_out[0:D, :], ccw_out[D:2 * D, :],
                   t[1], t[2])
    k.fence(); ar.reset()

    f0b = alloc_ffn0(k, ar)
    ftn = alloc_tern(k, ar)
    tn_chunks = (
        make_tern_chunks(k, c, ftn, a1in["qkvT_l"], 48, a1in["wq16d"], "tq",
                         mean_out=m1v_d)
        + make_tern_chunks(k, c, ftn, a1in["projT_l"], 16, wp16d, "tp",
                           mean_out=m2v_d))
    tn_pacer = Pacer(tn_chunks, 2 * (2 + 10 * 27))
    emit_ffn0_init(k, c, f0b, io0)
    emit_ffn0(k, c, f0b, io0, t[2], t[3], uid="fb", filler=tn_pacer)
    tn_pacer.flush()
    k.fence(); ar.reset()

    f1 = alloc_ffn1(k, ar)
    io1 = {"guT_l": ccgu_out.tensor.rearrange("(a p) n -> a p n", p=128),
           "dnT_l": ccdn_out.tensor.rearrange("(a p) n -> a p n", p=128)}
    emit_ffn1(k, c, f1, io1, t[3], t[4], uid="ga", wdeps=("guT", "dnT"))
    k.fence(); ar.reset()

    fa1 = alloc_attn1a(k, ar)
    outs1 = {"oL": oL_d, "qT": qT_d, "lamg": lamg_d, "send": ccs_in,
             "wp16d": wp16d, "m2v": m2v_d}
    a1in["m1v_d"] = m1v_d
    emit_attn1a(k, c, fa1, a1in, t[4], outs1)
    k.cc("AllGather", ALU.bypass, PAIRS, [ccs_in.tensor.ap()], [ccs_out.tensor.ap()],
         reads=["sendo"], writes=["sex"])
    k.fence(); ar.reset()

    fb1 = alloc_attn1b(k, ar)
    iob = {"sin_d": ccs_out[0:128, :], "lamg_d": lamg_d, "qT_d": qT_d,
           "oL_d": oL_d, "wp16_d": wp16d, "m2v_d": m2v_d, "nw_d": nw_d,
           "pnw_d": pnw_d, "sdeps": ("sex",), "flag_d": flag_d,
           "_sb": _SbAlloc(k, ar)}
    emit_attn1b(k, c, fb1, iob, t[4], t[5])
    k.fence(); ar.reset()

    f1b = alloc_ffn1(k, ar)
    emit_ffn1(k, c, f1b, io1, t[5], yout, uid="gb", wdeps=("guT", "dnT"))
    return k.finish()


def kernel(**inputs):
    consts = const_host()
    cores = list(range(NCORE))
    x = np.asarray(inputs["L0_attn_x"], np.float32)
    xs = _split_x(x)

    h0 = prep_ffn0_host(inputs)
    ha = prep_attn0_host(inputs)
    hb = prep_attn1_host(inputs)
    nc = _prog("fused", build_fused)
    maps = []
    for cr in cores:
        b, h = cr // 2, cr % 2
        m = dict(xT=xs[cr], flag_d=np.full((1, 1), float(h), np.float32),
                 wql=ha["wql"], wpl=ha["wpl"], g1=ha["g1"], g2=ha["g2"], g3=ha["g3"],
                 ones1_16d=np.ones((1, 128), dtype=BF))
        m.update(h0)
        m.update(prep_dequant_host(inputs, cr))
        m.update({kk: hb[kk] for kk in ("qkvT_l", "projT_l", "qnw_d", "srow_d",
                                        "D2_d", "B2_d", "trevcol_d", "trow_d",
                                        "dummy1", "nw_d", "pnw_d")})
        m.update(consts)
        maps.append(m)
    r = run_bass_kernel_spmd(nc, maps, core_ids=cores)
    return _join_y([np.asarray(r.results[cr]["yT"]) for cr in cores])


# version 07
# speedup vs baseline: 1.0804x; 1.0804x over previous
import contextlib
import numpy as np
import ml_dtypes
import concourse.bass as bass
import concourse.mybir as mybir
from concourse.bass_utils import run_bass_kernel_spmd

F32 = mybir.dt.float32
BF16 = mybir.dt.bfloat16
I32 = mybir.dt.int32
U8 = mybir.dt.uint8
AF = mybir.ActivationFunctionType
ALU = mybir.AluOpType
AX = mybir.AxisListType

BF = ml_dtypes.bfloat16

NCORE = 8
B, SEQ, D = 4, 4096, 2048
TOK = 2048
NCH = D // 128

LN_EPS = 1e-5
F32_EPS = float(np.finfo(np.float32).eps)

ENGS = ("T", "V", "S", "G", "Y")


class KB:

    def __init__(self):
        self.nc = bass.Bass("TRN2", target_bir_lowering=False, debug=False)
        self.es = contextlib.ExitStack()
        self.es.__enter__()
        self.sems = {}
        dma_keys = tuple(f"GD{i}" for i in range(8)) + tuple(f"YD{i}" for i in range(8))
        for e in ENGS + dma_keys + ("CC",):
            self.sems[e] = self.es.enter_context(self.nc.semaphore("sp_" + e))
        self.cnt = {kk: 0 for kk in self.sems}
        self.dma_rr = {"G": 0, "Y": 0}
        self.hw = {e: {} for e in ENGS}
        self.writer = {}
        self.readers = {}
        self.queue = {e: [] for e in ENGS}

    def sb(self, name, shape, dt=F32):
        return self.es.enter_context(self.nc.sbuf_tensor(name, list(shape), dt)).ap()

    def psum(self, name, shape=(128, 512), dt=F32):
        return self.es.enter_context(self.nc.psum_tensor(name, list(shape), dt)).ap()

    def dram_in(self, name, shape, dt=F32):
        return self.nc.dram_tensor(name, list(shape), dt, kind="ExternalInput").ap()

    def dram_out(self, name, shape, dt=F32):
        return self.nc.dram_tensor(name, list(shape), dt, kind="ExternalOutput").ap()

    def dram_tmp(self, name, shape, dt=F32):
        return self.nc.dram_tensor(name, list(shape), dt, kind="Internal").ap()

    def op(self, eng, fn, reads=(), writes=(), dma=False):
        assert eng in ENGS
        if dma:
            pfx = {"G": "GD", "Y": "YD"}[eng]
            semkey = f"{pfx}{self.dma_rr[eng] % 8}"
            self.dma_rr[eng] += 1
        else:
            semkey = eng
        waits = {}

        def need(wkey, val, wdma):
            if (not dma) and (not wdma) and wkey == eng == "T":
                return
            if dma and (not wdma) and wkey == eng:
                return
            if wdma and (wkey.startswith("GD") or wkey.startswith("YD")):
                val = self.cnt[wkey]
            if waits.get(wkey, 0) < val:
                waits[wkey] = val

        for r in reads:
            w = self.writer.get(r)
            if w is not None:
                need(*w)
        for r in writes:
            w = self.writer.get(r)
            if w is not None:
                need(*w)
            for rk, (rv, rdma) in self.readers.get(r, {}).items():
                need(rk, rv, rdma)

        hw = self.hw[eng]
        final = []
        for kk, v in sorted(waits.items()):
            if hw.get(kk, 0) >= v:
                continue
            hw[kk] = v
            final.append((self.sems[kk], v))

        inc = 16 if dma else 1
        self.cnt[semkey] += inc
        val = self.cnt[semkey]
        semh = self.sems[semkey]

        def run(e, final=final, fn=fn, semh=semh, inc=inc):
            for s, v in final:
                e.wait_ge(s, v)
            fn(e).then_inc(semh, inc)

        self.queue[eng].append(run)

        for r in writes:
            self.writer[r] = (semkey, val, dma)
            self.readers[r] = {}
        for r in reads:
            self.readers.setdefault(r, {})[semkey] = (val, dma)
        return (semkey, val)

    def fence(self):
        snap = {kk: v for kk, v in self.cnt.items() if v and kk != "CC"}
        for e in ENGS:
            hw = self.hw[e]
            waits = []
            for kk, v in sorted(snap.items()):
                if kk == e:
                    continue
                if hw.get(kk, 0) >= v:
                    continue
                hw[kk] = v
                waits.append((self.sems[kk], v))
            if waits:
                def runf(eng, waits=waits):
                    for sh, v in waits:
                        eng.wait_ge(sh, v)
                self.queue[e].append(runf)

    def cc(self, kind, op_, rgroups, ins_aps, outs_aps, reads=(), writes=()):
        waits = {}
        for r in reads:
            w = self.writer.get(r)
            if w is not None and waits.get(w[0], 0) < w[1]:
                waits[w[0]] = w[1]
        for r in writes:
            w = self.writer.get(r)
            if w is not None and waits.get(w[0], 0) < w[1]:
                waits[w[0]] = w[1]
            for rk, (rv, rdma) in self.readers.get(r, {}).items():
                if waits.get(rk, 0) < rv:
                    waits[rk] = rv
        hw = self.hw["G"]
        final = []
        for kk, v in sorted(waits.items()):
            if hw.get(kk, 0) >= v:
                continue
            hw[kk] = v
            final.append((self.sems[kk], v))
        self.cnt["CC"] += 1
        val = self.cnt["CC"]
        semh = self.sems["CC"]

        def run(e, final=final):
            for sh, v in final:
                e.wait_ge(sh, v)
            e.collective_compute(kind, op_, replica_groups=rgroups,
                                 ins=[a.opt() for a in ins_aps],
                                 outs=[a.opt() for a in outs_aps],
                                 ).then_inc(semh, 1)
        self.queue["G"].append(run)
        for r in writes:
            self.writer[r] = ("CC", val, False)
            self.readers[r] = {}
        for r in reads:
            self.readers.setdefault(r, {})["CC"] = (val, False)
        return val

    def finish(self):
        block = self.es.enter_context(self.nc.Block())
        engmap = {"T": block.tensor, "V": block.vector, "S": block.scalar,
                  "G": block.gpsimd, "Y": block.sync}
        sems = self.sems
        dma_finals = [(sems[kk], self.cnt[kk]) for kk in self.cnt
                      if (kk.startswith("GD") or kk.startswith("YD") or kk == "CC")
                      and self.cnt[kk]]

        for e in ENGS:
            q = self.queue[e]
            if e == "G":
                def runall(eng, q=q, dma_finals=dma_finals):
                    for fn in q:
                        fn(eng)
                    for sh, v in dma_finals:
                        eng.wait_ge(sh, v)
                engmap[e](runall)
            elif q:
                def runall(eng, q=q):
                    for fn in q:
                        fn(eng)
                engmap[e](runall)
        self.es.__exit__(None, None, None)
        return self.nc


_ESZ = {F32: 4, BF16: 2, I32: 4, U8: 1, mybir.dt.float16: 2}


class Arena:

    def __init__(self, k, name, nbytes):
        self.nbytes = nbytes
        self.th = k.es.enter_context(k.nc.sbuf_tensor(name, [128, nbytes // 2], BF16))
        self.views = {BF16: self.th}
        self.off = 0
        self.peak = 0

    def get(self, name, shape, dt=F32):
        p, n = shape
        esz = _ESZ[dt]
        off = (self.off + 31) & ~31
        self.off = off + n * esz
        self.peak = max(self.peak, self.off)
        assert self.off <= self.nbytes, f"arena overflow at {name}: {self.off}"
        if dt not in self.views:
            self.views[dt] = self.th.bitcast(dt)
        v = self.views[dt]
        e0 = off // esz
        return v.ap()[0:p, e0:e0 + n]

    def reset(self):
        self.off = 0

    def get2(self, name, shape, dt_a, dt_b):
        p, n = shape
        off = (self.off + 31) & ~31
        self.off = off + n * 4
        self.peak = max(self.peak, self.off)
        assert self.off <= self.nbytes, f"arena overflow at {name}"
        outs = []
        for dt in (dt_a, dt_b):
            if dt not in self.views:
                self.views[dt] = self.th.bitcast(dt)
            outs.append(self.views[dt].ap()[0:p, off // 4: off // 4 + n])
        return outs


class Pacer:

    def __init__(self, chunks, nslots):
        self.chunks = chunks
        self.n = max(1, nslots)
        self.i = 0
        self.done = 0

    def __call__(self):
        self.i += 1
        tgt = min((self.i * len(self.chunks) + self.n - 1) // self.n,
                  len(self.chunks))
        while self.done < tgt:
            self.chunks[self.done]()
            self.done += 1

    def flush(self):
        while self.done < len(self.chunks):
            self.chunks[self.done]()
            self.done += 1


class _SbAlloc:

    def __init__(self, k, ar=None):
        self.k, self.ar = k, ar

    def __call__(self, name, shape, dt=F32):
        if self.ar is not None:
            return self.ar.get(name, shape, dt)
        return self.k.sb(name, shape, dt)



def alloc_common(k):
    c = {}
    c["ones16"] = k.sb("c_ones16", [128, 1], BF16)
    c["ones1_32"] = k.sb("c_ones1_32", [1, 128], F32)
    c["ones_col32"] = k.sb("c_ones_col32", [128, 1], F32)
    c["idn32"] = k.sb("c_idn32", [128, 128], F32)
    c["idn16"] = k.sb("c_idn16", [128, 128], BF16)
    c["kv"] = k.sb("c_kv", [128, 8], F32)
    pbig = k.psum("pbig", (128, 4096))
    c["pbig"] = pbig
    c["pb"] = [pbig[:, i * 512:(i + 1) * 512] for i in range(8)]
    return c


def load_consts(k, c, dr):
    for nm in ("ones16", "ones1_32", "idn32", "idn16", "kv"):
        k.op("Y", lambda y, nm=nm: y.dma_start(out=c[nm], in_=dr[nm]),
             writes=[nm], dma=True)
    k.op("V", lambda v: v.memset(c["ones_col32"], 1.0), writes=["ones_col32"])


KVALS = [LN_EPS, F32_EPS, 1e-6, 0.0, 0.0, 0.0, 0.0, 0.0]


def const_host():
    return {
        "ones16": np.ones((128, 1), dtype=BF),
        "ones1_32": np.ones((1, 128), dtype=np.float32),
        "idn32": np.eye(128, dtype=np.float32),
        "idn16": np.eye(128).astype(BF),
        "kv": np.tile(np.asarray(KVALS, np.float32), (128, 1)),
    }



def emit_ln_half(k, c, f, xr, th, with_gate=False):
    pb = c["pb"]
    for ch in range(NCH):
        q = ch % 2
        k.op("Y", lambda y, ch=ch, q=q: y.dma_start(
            out=f["xc"][q], in_=xr[:, ch, th:th + 512]),
            writes=[f"xc{q}"], dma=True)
        k.op("S", lambda s, ch=ch, q=q: s.activation(
            f["sq16c"][q], f["xc"][q], AF.Square),
            reads=[f"xc{q}"], writes=[f"sq16c{q}"])
        k.op("V", lambda v, ch=ch, q=q: v.tensor_copy(
            f["x16c"][q], f["xc"][q]),
            reads=[f"xc{q}"], writes=[f"x16c{q}"])

        def fmm(t, ch=ch, q=q):
            t.matmul(pb[0][0:1, :], c["ones16"], f["x16c"][q],
                     start=(ch == 0), stop=(ch == NCH - 1))
            last = t.matmul(pb[1][0:1, :], c["ones16"], f["sq16c"][q],
                            start=(ch == 0), stop=(ch == NCH - 1))
            if with_gate:
                last = t.matmul(pb[2][0:8, :], f["gw_sb"][:, ch * 8:(ch + 1) * 8],
                                f["xc"][q],
                                start=(ch == 0), stop=(ch == NCH - 1))
            return last
        wr = ["pb0", "pb1"] + (["pb2"] if with_gate else [])
        k.op("T", fmm, reads=[f"x16c{q}", f"sq16c{q}", f"xc{q}", "ones16"]
             + (["gw_sb"] if with_gate else []), writes=wr)

    sr = f["statrow"]

    def fstat_a(v):
        v.tensor_scalar(sr[0:1, 0:512], pb[0][0:1, :], 1.0 / D, None, ALU.mult)
        return v.tensor_scalar(sr[0:1, 512:1024], pb[1][0:1, :], 1.0 / D, None, ALU.mult)
    k.op("V", fstat_a, reads=["pb0", "pb1"], writes=["statrow"])
    k.op("V", lambda v: v.tensor_tensor(sr[0:1, 1024:1536], sr[0:1, 0:512],
                                        sr[0:1, 0:512], ALU.mult),
         reads=["statrow"], writes=["statrow"])
    k.op("V", lambda v: v.tensor_tensor(sr[0:1, 512:1024], sr[0:1, 512:1024],
                                        sr[0:1, 1024:1536], ALU.subtract),
         reads=["statrow"], writes=["statrow"])

    k.op("S", lambda s: s.activation(sr[0:1, 1024:1536], sr[0:1, 512:1024],
                                     AF.Sqrt, bias=c["kv"][0:1, 0:1]),
         reads=["statrow", "kv"], writes=["statrow"])

    k.op("V", lambda v: v.reciprocal(sr[0:1, 1024:1536], sr[0:1, 1024:1536]),
         reads=["statrow"], writes=["statrow"])
    k.op("V", lambda v: v.tensor_tensor(sr[0:1, 1536:2048], sr[0:1, 0:512],
                                        sr[0:1, 1024:1536], ALU.mult),
         reads=["statrow"], writes=["statrow"])
    k.op("V", lambda v: v.tensor_scalar_mul(sr[0:1, 1536:2048],
                                            sr[0:1, 1536:2048], -1.0),
         reads=["statrow"], writes=["statrow"])

    def fbc(t):
        t.matmul(pb[3][:, :], c["ones1_32"], sr[0:1, 1024:1536], start=True, stop=True)
        return t.matmul(pb[4][:, :], c["ones1_32"], sr[0:1, 1536:2048],
                        start=True, stop=True)
    k.op("T", fbc, reads=["statrow", "ones1_32"], writes=["pb3", "pb4"])



NE = 8
TT = 1024
NTILE = TOK // TT


def alloc_ffn0(k, ar=None):
    sb = _SbAlloc(k, ar)
    f = {}
    f["xc"] = [sb(f"f0_xc{i}", [128, 512], F32) for i in range(2)]
    f["xn16"] = sb("f0_xn16", [128, NCH * TT], BF16)
    f["acc"] = sb("f0_acc", [128, NCH * TT], F32)
    f["hbuf"] = sb("f0_hbuf", [128, 11 * TT], BF16)
    f["wslot"] = [sb(f"f0_ws{i}", [128, 2048], BF16) for i in range(4)]
    f["sg"] = [sb(f"f0_sg{i}", [128, 512], F32) for i in range(4)]
    f["htmp"] = [sb(f"f0_ht{i}", [128, 512], F32) for i in range(2)]
    f["x16c"] = [sb(f"f0_x16c{i}", [128, 512], BF16) for i in range(2)]
    f["sq16c"] = [sb(f"f0_sq16c{i}", [128, 512], BF16) for i in range(2)]
    f["statrow"] = sb("f0_stat", [1, 2048], F32)
    f["outc"] = [sb(f"f0_outc{i}", [128, 512], F32) for i in range(2)]
    f["gw_sb"] = sb("f0_gw", [128, NCH * 8], F32)
    f["gsum"] = sb("f0_gsum", [8, 1], F32)
    f["gbias8"] = sb("f0_gbias8", [128, 8], F32)
    f["grow"] = sb("f0_grow", [8, 512], F32)
    f["rt"] = sb("f0_rt", [128, 256], F32)
    f["weT"] = sb("f0_weT", [8, TOK], F32)
    f["weRow"] = sb("f0_weRow", [1, TT], F32)
    f["Be"] = sb("f0_Be", [128, TT], F32)
    return f


def emit_ffn0_init(k, c, f, io):
    pb = c["pb"]
    k.op("Y", lambda y: y.dma_start(out=f["gw_sb"], in_=io["gw_l"]),
         writes=["gw_sb"], dma=True)
    k.op("Y", lambda y: y.dma_start(out=f["rt"][0:1, 0:8], in_=io["gbias_row"]),
         writes=["rt"], dma=True)

    def fgs(t):
        for ch in range(NCH):
            last = t.matmul(pb[7][0:8, 0:1], f["gw_sb"][:, ch * 8:(ch + 1) * 8],
                            c["ones_col32"], start=(ch == 0), stop=(ch == NCH - 1))
        return last
    k.op("T", fgs, reads=["gw_sb", "ones_col32"], writes=["pb7"])
    k.op("V", lambda v: v.tensor_copy(f["gsum"], pb[7][0:8, 0:1]),
         reads=["pb7"], writes=["gsum"])

    k.op("T", lambda t: t.matmul(pb[7][:, 0:8], c["ones1_32"], f["rt"][0:1, 0:8],
                                 start=True, stop=True),
         reads=["rt", "ones1_32"], writes=["pb7"])
    k.op("V", lambda v: v.tensor_copy(f["gbias8"], pb[7][:, 0:8]),
         reads=["pb7"], writes=["gbias8"])


def emit_routing_ffn0(k, c, f, toff):
    pb = c["pb"]
    rt = f["rt"]

    def ftr(t):
        for sbk in range(4):
            last = t.transpose(pb[5][:, sbk * 8:(sbk + 1) * 8],
                               f["grow"][0:8, sbk * 128:(sbk + 1) * 128],
                               c["idn32"][0:8, 0:8])
        return last
    k.op("T", ftr, reads=["grow", "idn32"], writes=["pb5"])

    O = 64
    SL = {"s": 0, "tb": 8, "m1": 16, "m2": 17, "dn": 18, "mk1": 24, "mk2": 32,
          "t2": 40, "mkf": 48, "w": 56}

    def sl(nm, sbk, wid=8):
        b = sbk * O + SL[nm]
        return rt[:, b:b + wid]

    def step(fn):
        def g(v):
            for sbk in range(4):
                last = fn(v, sbk)
            return last
        k.op("V", g, reads=["pb5", "gbias8", "rt"], writes=["rt"])

    step(lambda v, i: v.tensor_copy(sl("s", i), pb[5][:, i * 8:(i + 1) * 8]))
    step(lambda v, i: v.tensor_tensor(sl("tb", i), sl("s", i), f["gbias8"], ALU.add))
    step(lambda v, i: v.tensor_reduce(sl("m1", i, 1), sl("tb", i), AX.X, ALU.max))
    step(lambda v, i: v.tensor_scalar(sl("mk1", i), sl("tb", i), sl("m1", i, 1), None, ALU.is_equal))
    step(lambda v, i: v.scalar_tensor_tensor(sl("t2", i), sl("mk1", i), -1e30, sl("tb", i), ALU.mult, ALU.add))
    step(lambda v, i: v.tensor_reduce(sl("m2", i, 1), sl("t2", i), AX.X, ALU.max))
    step(lambda v, i: v.tensor_scalar(sl("mk2", i), sl("t2", i), sl("m2", i, 1), None, ALU.is_equal))
    step(lambda v, i: v.tensor_tensor(sl("mkf", i), sl("mk1", i), sl("mk2", i), ALU.add))
    step(lambda v, i: v.tensor_tensor(sl("w", i), sl("s", i), sl("mkf", i), ALU.mult))
    step(lambda v, i: v.tensor_reduce(sl("dn", i, 1), sl("w", i), AX.X, ALU.add))
    step(lambda v, i: v.reciprocal(sl("dn", i, 1), sl("dn", i, 1)))
    step(lambda v, i: v.tensor_scalar(sl("w", i), sl("w", i), sl("dn", i, 1), None, ALU.mult))

    def ftb(t):
        for sbk in range(4):
            last = t.transpose(pb[6][0:8, sbk * 128:(sbk + 1) * 128],
                               rt[:, sbk * 64 + 56:sbk * 64 + 64], c["idn32"])
        return last
    k.op("T", ftb, reads=["rt", "idn32"], writes=["pb6"])

    k.op("V", lambda v: v.tensor_copy(f["weT"][:, toff:toff + 512], pb[6][0:8, 0:512]),
         reads=["pb6"], writes=["weT"])


def emit_ffn0(k, c, f, io, xdram, ydram, uid):
    pb = c["pb"]
    sgw, sdw, upw, dww = io["sgw_l"], io["sdw_l"], io["upw_l"], io["dww_l"]
    xr = xdram.rearrange("(c p) t -> p c t", p=128)

    for it in range(NTILE):
        t0 = it * TT
        for hf in range(2):
            th = t0 + hf * 512
            emit_ln_half(k, c, f, xr, th, with_gate=True)

            for ch in range(NCH):
                q = ch % 2
                k.op("Y", lambda y, ch=ch, q=q, th=th: y.dma_start(
                    out=f["xc"][q], in_=xr[:, ch, th:th + 512]),
                    writes=[f"xc{q}"], dma=True)

                k.op("V", lambda v, ch=ch, q=q: v.tensor_tensor(
                    f["htmp"][q], f["xc"][q], pb[3][:, :], ALU.mult),
                    reads=[f"xc{q}", "pb3"], writes=[f"htmp{q}"])

                def fxn(v, ch=ch, q=q, hf=hf):
                    dst = f["xn16"][:, ch * TT + hf * 512: ch * TT + hf * 512 + 512]
                    return v.tensor_tensor(dst, f["htmp"][q], pb[4][:, :], ALU.add)
                k.op("V", fxn, reads=[f"htmp{q}", "pb4"],
                     writes=[f"xn16h{hf}c{ch}"])

            k.op("V", lambda v: v.tensor_copy(f["grow"], pb[2][0:8, :]),
                 reads=["pb2"], writes=["grow"])
            k.op("V", lambda v: v.tensor_tensor(f["grow"], f["grow"],
                                                pb[3][0:8, :], ALU.mult),
                 reads=["grow", "pb3"], writes=["grow"])
            k.op("V", lambda v: v.scalar_tensor_tensor(
                f["grow"], pb[4][0:8, :], f["gsum"], f["grow"], ALU.mult, ALU.add),
                 reads=["pb4", "gsum", "grow"], writes=["grow"])
            k.op("S", lambda s: s.activation(f["grow"], f["grow"], AF.Sigmoid),
                 reads=["grow"], writes=["grow"])

            emit_routing_ffn0(k, c, f, toff=t0 + hf * 512)

        groups = [("shA", None), ("shB", None)] + [("ex", e) for e in range(NE)]
        for gi, (kind, e) in enumerate(groups):
            if kind == "ex":
                def fwr(g, e=e, t0=t0):
                    return g.dma_start(out=f["weRow"],
                                       in_=f["weT"][e:e + 1, t0:t0 + TT])
                k.op("G", fwr, reads=["weT"], writes=["weRow"], dma=True)

                def fbe(t):
                    t.matmul(pb[5][:, :], c["ones1_32"],
                             f["weRow"][0:1, 0:512], start=True, stop=True)
                    return t.matmul(pb[6][:, :], c["ones1_32"],
                                    f["weRow"][0:1, 512:TT],
                                    start=True, stop=True)
                k.op("T", fbe, reads=["weRow", "ones1_32"], writes=["pb5", "pb6"])

                def fbe2(v):
                    v.tensor_copy(f["Be"][:, 0:512], pb[5][:, :])
                    return v.tensor_copy(f["Be"][:, 512:TT], pb[6][:, :])
                k.op("V", fbe2, reads=["pb5", "pb6"], writes=["Be"])

            for j in range(11):
                q = j % 2
                bb = 4 * q
                for half, slot in ((0, 2 * q), (1, 2 * q + 1)):
                    if kind == "shA":
                        src = sgw[j + 22 * half]
                    elif kind == "shB":
                        src = sgw[11 + j + 22 * half]
                    else:
                        src = upw[e, j + 11 * half]
                    k.op("G", lambda g, src=src, slot=slot: g.dma_start(
                        out=f["wslot"][slot], in_=src),
                        writes=[f"ws{slot}"], dma=True)

                def fup(t, q=q, bb=bb):
                    for ts in range(2):
                        for ch in range(NCH):
                            t.matmul(pb[bb + ts][:, :],
                                     f["wslot"][2 * q][:, ch * 128:(ch + 1) * 128],
                                     f["xn16"][:, ch * TT + ts * 512:ch * TT + (ts + 1) * 512],
                                     start=(ch == 0), stop=(ch == NCH - 1))
                    for ts in range(2):
                        for ch in range(NCH):
                            last = t.matmul(pb[bb + 2 + ts][:, :],
                                            f["wslot"][2 * q + 1][:, ch * 128:(ch + 1) * 128],
                                            f["xn16"][:, ch * TT + ts * 512:ch * TT + (ts + 1) * 512],
                                            start=(ch == 0), stop=(ch == NCH - 1))
                    return last
                xn_deps = [f"xn16h{h}c{ch}" for h in range(2) for ch in range(NCH)]
                k.op("T", fup, reads=[f"ws{2*q}", f"ws{2*q+1}"] + xn_deps,
                     writes=[f"pb{bb}", f"pb{bb+1}", f"pb{bb+2}", f"pb{bb+3}"])

                silu_off = 2 if kind in ("shA", "shB") else 0
                lin_off = 2 - silu_off
                for ts in range(2):
                    k.op("S", lambda s, ts=ts, q=q, bb=bb, so=silu_off: s.activation(
                        f["sg"][2 * q + ts], pb[bb + so + ts][:, :], AF.Silu),
                        reads=[f"pb{bb + silu_off + ts}"], writes=[f"sg{2*q+ts}"])

                for ts in range(2):
                    if kind == "ex":
                        k.op("V", lambda v, ts=ts, q=q, bb=bb, lo=lin_off: v.tensor_tensor(
                            f["htmp"][ts], f["sg"][2 * q + ts],
                            pb[bb + lo + ts][:, :], ALU.mult),
                            reads=[f"sg{2*q+ts}", f"pb{bb + lin_off + ts}"],
                            writes=[f"htmp{ts}"])

                        def fh2(v, ts=ts, j=j):
                            dst = f["hbuf"][:, j * TT + ts * 512: j * TT + (ts + 1) * 512]
                            return v.tensor_tensor(dst, f["htmp"][ts],
                                                   f["Be"][:, ts * 512:(ts + 1) * 512],
                                                   ALU.mult)
                        k.op("V", fh2, reads=[f"htmp{ts}", "Be"], writes=[f"hb{j}t{ts}"])
                    else:
                        def fh(v, ts=ts, j=j, q=q, bb=bb, lo=lin_off):
                            dst = f["hbuf"][:, j * TT + ts * 512: j * TT + (ts + 1) * 512]
                            return v.tensor_tensor(dst, f["sg"][2 * q + ts],
                                                   pb[bb + lo + ts][:, :], ALU.mult)
                        k.op("V", fh, reads=[f"sg{2*q+ts}", f"pb{bb + lin_off + ts}"],
                             writes=[f"hb{j}t{ts}"])

            for o2 in range(16):
                slot = o2 % 4
                if kind == "shA":
                    src = sdw[o2][:, 0:1408]
                elif kind == "shB":
                    src = sdw[o2][:, 1408:2816]
                else:
                    src = dww[e, o2]
                k.op("G", lambda g, src=src, slot=slot: g.dma_start(
                    out=f["wslot"][slot][:, 0:1408], in_=src),
                    writes=[f"ws{slot}"], dma=True)

                for ts in range(2):
                    bank = 5 + (o2 * 2 + ts) % 2

                    def fdn(t, ts=ts, slot=slot, bank=bank):
                        for ch in range(11):
                            last = t.matmul(pb[bank][:, :],
                                            f["wslot"][slot][:, ch * 128:(ch + 1) * 128],
                                            f["hbuf"][:, ch * TT + ts * 512:ch * TT + (ts + 1) * 512],
                                            start=(ch == 0), stop=(ch == 10))
                        return last
                    hdeps = [f"hb{jj}t{ts}" for jj in range(11)]
                    k.op("T", fdn, reads=[f"ws{slot}"] + hdeps, writes=[f"pb{bank}"])

                    def facc(v, o2=o2, ts=ts, bank=bank, gi=gi):
                        dst = f["acc"][:, o2 * TT + ts * 512: o2 * TT + (ts + 1) * 512]
                        if gi == 0:
                            return v.tensor_copy(dst, pb[bank][:, :])
                        return v.tensor_tensor(dst, dst, pb[bank][:, :], ALU.add)
                    k.op("V", facc, reads=[f"pb{bank}"] + ([f"acc{o2}t{ts}"] if gi else []),
                         writes=[f"acc{o2}t{ts}"])

        for hf in range(2):
            th = t0 + hf * 512
            for ch in range(NCH):
                q = ch % 2
                k.op("Y", lambda y, ch=ch, q=q, th=th: y.dma_start(
                    out=f["xc"][q], in_=xr[:, ch, th:th + 512]),
                    writes=[f"xc{q}"], dma=True)

                def fo(v, ch=ch, q=q, hf=hf):
                    src = f["acc"][:, ch * TT + hf * 512: ch * TT + (hf + 1) * 512]
                    return v.scalar_tensor_tensor(f["outc"][q], src, 0.5,
                                                  f["xc"][q],
                                                  ALU.mult, ALU.add)
                k.op("V", fo, reads=[f"xc{q}", f"acc{ch}t{hf}"], writes=[f"outc{q}"])

                k.op("G", lambda g, ch=ch, q=q, th=th: g.dma_start(
                    out=ydram[ch * 128:(ch + 1) * 128, th:th + 512], in_=f["outc"][q]),
                    reads=[f"outc{q}"], writes=[f"{uid}y{ch}_{th}"], dma=True)



def prep_ffn0_host(inputs):
    gw = np.asarray(inputs["L0_ffn_gate_w"], dtype=np.float32)
    sgwt = np.ascontiguousarray(np.asarray(inputs["L0_ffn_shared_gw"], np.float32).T)
    sdwt = np.ascontiguousarray(np.asarray(inputs["L0_ffn_shared_dw"], np.float32).T)
    upw = np.asarray(inputs["L0_ffn_up_w"], np.float32)
    dww = np.asarray(inputs["L0_ffn_down_w"], np.float32)

    gw_l = np.ascontiguousarray(
        gw.T.reshape(NCH, 128, 8).transpose(1, 0, 2).reshape(128, NCH * 8))
    sgw_l = np.ascontiguousarray(
        sgwt.reshape(NCH, 128, 44, 128).transpose(2, 1, 0, 3).reshape(44, 128, 2048))
    sdw_l = np.ascontiguousarray(
        sdwt.reshape(22, 128, 16, 128).transpose(2, 1, 0, 3).reshape(16, 128, 22 * 128))
    upw_l = np.ascontiguousarray(
        upw.transpose(0, 2, 1).reshape(8, NCH, 128, 22, 128)
        .transpose(0, 3, 2, 1, 4).reshape(8, 22, 128, 2048))
    dww_l = np.ascontiguousarray(
        dww.transpose(0, 2, 1).reshape(8, 11, 128, 16, 128)
        .transpose(0, 3, 2, 1, 4).reshape(8, 16, 128, 11 * 128))
    gbias_row = np.asarray(inputs["L0_ffn_gate_bias"], np.float32).reshape(1, 8)
    return {"gw_l": gw_l, "sgw_l": sgw_l, "sdw_l": sdw_l, "upw_l": upw_l,
            "dww_l": dww_l, "gbias_row": gbias_row}


def decl_ffn0_inputs(k):
    io = {}
    io["gw_l"] = k.dram_in("gw_l", [128, NCH * 8])
    io["gbias_row"] = k.dram_in("gbias_row", [1, 8])
    io["sgw_l"] = k.dram_in("sgw_l", [44, 128, 2048])
    io["sdw_l"] = k.dram_in("sdw_l", [16, 128, 22 * 128])
    io["upw_l"] = k.dram_in("upw_l", [8, 22, 128, 2048])
    io["dww_l"] = k.dram_in("dww_l", [8, 16, 128, 11 * 128])
    return io



def _const_dram_decl(k):
    dr = {}
    dr["ones16"] = k.dram_in("ones16", [128, 1], BF16)
    dr["ones1_32"] = k.dram_in("ones1_32", [1, 128], F32)
    dr["idn32"] = k.dram_in("idn32", [128, 128], F32)
    dr["idn16"] = k.dram_in("idn16", [128, 128], BF16)
    dr["kv"] = k.dram_in("kv", [128, 8], F32)
    return dr


def build_ffn0_launch():
    k = KB()
    xin = k.dram_in("xT", [D, TOK])
    yout = k.dram_out("yT", [D, TOK])
    dr = _const_dram_decl(k)
    c = alloc_common(k)
    load_consts(k, c, dr)
    io = decl_ffn0_inputs(k)
    f = alloc_ffn0(k)
    emit_ffn0_init(k, c, f, io)
    emit_ffn0(k, c, f, io, xin, yout, uid="a")
    return k.finish()





def alloc_attn0pre(k, ar=None):
    sb = _SbAlloc(k, ar)
    f = {}
    f["xc"] = [sb(f"ap_xc{i}", [128, 512], F32) for i in range(2)]
    f["x16c"] = [sb(f"ap_x16c{i}", [128, 512], BF16) for i in range(2)]
    f["sq16c"] = [sb(f"ap_sq16c{i}", [128, 512], BF16) for i in range(2)]
    f["statrow"] = sb("ap_stat", [1, 2048], F32)
    f["htmp"] = [sb(f"ap_ht{i}", [128, 512], F32) for i in range(2)]
    f["xn16"] = sb("ap_xn16", [128, NCH * 512], BF16)
    f["wq16"] = sb("ap_wq16", [128, NCH * 2048], BF16)
    f["wstg"] = [sb(f"ap_wstg{i}", [128, 512], BF16) for i in range(2)]
    return f


def emit_attn0pre(k, c, f, io, xdram, wdram, filler=None):
    pb = c["pb"]
    xr = xdram.rearrange("(c p) t -> p c t", p=128)
    for oc in range(NCH):
        k.op("G", lambda g, oc=oc: g.dma_start(
            out=f["wq16"][:, oc * 2048:(oc + 1) * 2048], in_=io["wql"][oc]),
            writes=[f"wq{oc}"], dma=True)

    for hf in range(4):
        th = hf * 512
        if filler:
            filler()
        emit_ln_half(k, c, f, xr, th, with_gate=False)
        for ch in range(NCH):
            q = ch % 2
            k.op("Y", lambda y, ch=ch, q=q, th=th: y.dma_start(
                out=f["xc"][q], in_=xr[:, ch, th:th + 512]),
                writes=[f"xc{q}"], dma=True)
            k.op("V", lambda v, ch=ch, q=q: v.tensor_tensor(
                f["htmp"][q], f["xc"][q], pb[3][:, :], ALU.mult),
                reads=[f"xc{q}", "pb3"], writes=[f"htmp{q}"])

            def fxn(v, ch=ch, q=q):
                dst = f["xn16"][:, ch * 512:(ch + 1) * 512]
                return v.tensor_tensor(dst, f["htmp"][q], pb[4][:, :], ALU.add)
            k.op("V", fxn, reads=[f"htmp{q}", "pb4"], writes=[f"apxn{ch}"])
            if filler:
                filler()

        for oc in range(NCH):
            q = oc % 2
            if filler:
                filler()

            def fw(t, oc=oc, q=q):
                for ch in range(NCH):
                    last = t.matmul(pb[q][:, :],
                                    f["wq16"][:, oc * 2048 + ch * 128:oc * 2048 + (ch + 1) * 128],
                                    f["xn16"][:, ch * 512:(ch + 1) * 512],
                                    start=(ch == 0), stop=(ch == NCH - 1))
                return last
            k.op("T", fw, reads=[f"wq{oc}"] + [f"apxn{ch}" for ch in range(NCH)],
                 writes=[f"pb{q}"])
            k.op("V", lambda v, q=q: v.tensor_copy(f["wstg"][q], pb[q][:, :]),
                 reads=[f"pb{q}"], writes=[f"wstg{q}"])
            k.op("G", lambda g, oc=oc, q=q, th=th: g.dma_start(
                out=wdram[oc * 128:(oc + 1) * 128, th:th + 512], in_=f["wstg"][q]),
                reads=[f"wstg{q}"], writes=[f"wout{oc}_{th}"], dma=True)


def prep_attn0_host(inputs):
    W = np.asarray(inputs["L0_attn_W"], np.float32)
    WT = np.ascontiguousarray(W.T)
    wql = np.ascontiguousarray(
        WT.reshape(NCH, 128, NCH, 128).transpose(2, 1, 0, 3).reshape(NCH, 128, 2048))
    wpl = np.ascontiguousarray(
        W.reshape(NCH, 128, NCH, 128).transpose(2, 1, 0, 3).reshape(NCH, 128, 2048))
    g1 = np.asarray(inputs["L0_attn_g1"], np.float32).reshape(128, 1)
    g2 = np.asarray(inputs["L0_attn_g2"], np.float32).reshape(128, 1)
    g3 = np.asarray(inputs["L0_attn_g3"], np.float32).reshape(128, 1)
    return {"wql": wql, "wpl": wpl, "g1": g1, "g2": g2, "g3": g3}


def build_attn0pre_launch():
    k = KB()
    xin = k.dram_in("xT", [D, TOK])
    wout = k.dram_out("wT", [D, TOK], BF16)
    dr = _const_dram_decl(k)
    c = alloc_common(k)
    load_consts(k, c, dr)
    io = {"wql": k.dram_in("wql", [NCH, 128, 2048])}
    f = alloc_attn0pre(k)
    emit_attn0pre(k, c, f, io, xin, wout)
    return k.finish()



SCL = float(1.0 / np.sqrt(128.0))


def alloc_attn0core(k, ar=None):
    sb = _SbAlloc(k, ar)
    f = {}
    f["wh"] = sb("ac_wh", [128, SEQ], BF16)
    f["R"] = sb("ac_R", [128, SEQ], BF16)
    f["qh"] = sb("ac_qh", [128, SEQ], BF16)
    f["vh"] = sb("ac_vh", [128, SEQ], BF16)
    f["vtm"] = sb("ac_vtm", [128, SEQ], BF16)
    f["Estrip"] = [sb(f"ac_Es{i}", [128, 1024], BF16) for i in range(2)]
    f["Es2"] = [sb(f"ac_Es2{i}", [128, 1024], BF16) for i in range(2)]
    f["Eu"] = [sb(f"ac_Eu{i}", [128, 1024], BF16) for i in range(2)]
    f["Ew"] = [sb(f"ac_Ew{i}", [128, 1024], BF16) for i in range(2)]
    f["Ub"] = sb("ac_Ub", [128, 512], BF16)
    f["csb"] = sb("ac_csb", [128, 128], F32)
    f["cs"] = sb("ac_cs", [128, 32], F32)
    f["colinv"] = sb("ac_colinv", [128, 32], F32)
    f["s2r"] = sb("ac_s2r", [1, SEQ], BF16)
    f["rstdr"] = sb("ac_rstdr", [1, SEQ], BF16)
    f["rsr"] = sb("ac_rsr", [1, SEQ], F32)
    f["srow"] = sb("ac_srow", [1, 64], F32)
    f["negM"] = sb("ac_negM", [128, 1], F32)
    f["g12"] = sb("ac_g12", [128, 1], F32)
    f["g3c"] = sb("ac_g3c", [128, 1], F32)
    f["gtmp"] = sb("ac_gtmp", [128, 3], F32)
    f["fb"] = sb("ac_fb", [128, 1], F32)
    f["flagrow"] = sb("ac_flagrow", [1, 1], F32)
    f["rowsel"] = sb("ac_rowsel", [1, 512], F32)
    f["qsel"] = [sb(f"ac_qsel{i}", [128, 512], BF16) for i in range(2)]
    f["qst"] = [sb(f"ac_qst{i}", [128, 512], BF16) for i in range(2)]
    f["yatt"] = sb("ac_yatt", [128, NCH * TOK], BF16)
    f["wslot"] = [sb(f"ac_ws{i}", [128, 2048], BF16) for i in range(2)]
    f["xc"] = [sb(f"ac_xc{i}", [128, 512], F32) for i in range(2)]
    f["outc"] = [sb(f"ac_outc{i}", [128, 512], F32) for i in range(2)]
    f["ones1_16"] = sb("ac_ones1_16", [1, 128], BF16)
    return f


def emit_attn0core(k, c, f, io, w0, w1, xdram, ydram, filler=None):
    pb = c["pb"]
    k.op("Y", lambda y: y.dma_start(out=f["ones1_16"], in_=io["ones1_16d"]),
         writes=["ones1_16"], dma=True)
    for nm, dn in (("g12", "g1"), ("g3c", "g3")):
        pass
    k.op("Y", lambda y: y.dma_start(out=f["gtmp"][:, 0:1], in_=io["g1"]),
         writes=["gtmp"], dma=True)
    k.op("Y", lambda y: y.dma_start(out=f["gtmp"][:, 1:2], in_=io["g2"]),
         writes=["gtmp"], dma=True)
    k.op("Y", lambda y: y.dma_start(out=f["gtmp"][:, 2:3], in_=io["g3"]),
         writes=["gtmp"], dma=True)
    k.op("V", lambda v: v.tensor_tensor(f["g12"], f["gtmp"][:, 0:1],
                                        f["gtmp"][:, 1:2], ALU.mult),
         reads=["gtmp"], writes=["g12"])
    k.op("V", lambda v: v.tensor_copy(f["g3c"], f["gtmp"][:, 2:3]),
         reads=["gtmp"], writes=["g3c"])
    k.op("Y", lambda y: y.dma_start(out=f["flagrow"], in_=io["flag_d"]),
         writes=["flagrow"], dma=True)

    def ffb(t):
        return t.matmul(pb[0][:, 0:1], c["ones1_32"], f["flagrow"], start=True, stop=True)
    k.op("T", ffb, reads=["flagrow", "ones1_32"], writes=["pb0"])
    k.op("V", lambda v: v.tensor_copy(f["fb"], pb[0][:, 0:1]), reads=["pb0"],
         writes=["fb"])

    xr = xdram.rearrange("(c p) t -> p c t", p=128)

    for h in range(NCH):
        if filler:
            filler()
        k.op("Y", lambda y, h=h: y.dma_start(
            out=f["wh"][:, 0:2048], in_=w0[h * 128:(h + 1) * 128, :]),
            reads=list(io.get("wdeps", ())), writes=["wh"], dma=True)
        k.op("Y", lambda y, h=h: y.dma_start(
            out=f["wh"][:, 2048:4096], in_=w1[h * 128:(h + 1) * 128, :]),
            reads=list(io.get("wdeps", ())), writes=["wh"], dma=True)
        k.op("S", lambda s: s.activation(f["R"], f["wh"], AF.Square),
             reads=["wh"], writes=["R"])
        for ib in range(8):
            q = ib % 2

            def fs2(t, ib=ib, q=q):
                return t.matmul(pb[2 + q][0:1, :], c["ones16"],
                                f["R"][:, ib * 512:(ib + 1) * 512], start=True, stop=True)
            k.op("T", fs2, reads=["R", "ones16"], writes=[f"pb{2+q}"])
            k.op("V", lambda v, ib=ib, q=q: v.tensor_copy(
                f["s2r"][0:1, ib * 512:(ib + 1) * 512], pb[2 + q][0:1, :]),
                reads=[f"pb{2+q}"], writes=["s2r"])
        k.op("S", lambda s: s.activation(f["rstdr"], f["s2r"], AF.Sqrt,
                                         bias=c["kv"][0:1, 1:2], scale=1.0 / 128.0),
             reads=["s2r", "kv"], writes=["rstdr"])
        def frcp(v):
            with k.nc.allow_low_precision(reason="rms row kept in bf16"):
                return v.reciprocal(f["rstdr"], f["rstdr"])
        k.op("V", frcp, reads=["rstdr"], writes=["rstdr"])

        for ib in range(8):
            q = ib % 2

            def fR(t, ib=ib, q=q):
                return t.matmul(pb[2 + q][:, :], f["ones1_16"],
                                f["rstdr"][0:1, ib * 512:(ib + 1) * 512], start=True, stop=True)
            k.op("T", fR, reads=["rstdr", "ones1_16"], writes=[f"pb{2+q}"])
            k.op("V", lambda v, ib=ib, q=q: v.tensor_copy(
                f["R"][:, ib * 512:(ib + 1) * 512], pb[2 + q][:, :]),
                reads=[f"pb{2+q}"], writes=["R"])
        k.op("V", lambda v: v.tensor_tensor(f["wh"], f["wh"], f["R"], ALU.mult),
             reads=["wh", "R"], writes=["wh"])
        k.op("V", lambda v: v.tensor_scalar(f["qh"], f["wh"], f["g12"], None, ALU.mult),
             reads=["wh", "g12"], writes=["qh"])
        k.op("V", lambda v: v.tensor_scalar(f["vh"], f["wh"], f["g3c"], None, ALU.mult),
             reads=["wh", "g3c"], writes=["vh"])
        for tc in range(32):
            k.op("Y", lambda y, tc=tc: y.dma_start(
                out=f["vtm"][:, tc * 128:(tc + 1) * 128],
                in_=f["vh"][:, tc * 128:(tc + 1) * 128], transpose=True),
                reads=["vh"], writes=["vtm"], dma=True)
        k.op("S", lambda s: s.activation(f["R"], f["qh"], AF.Square),
             reads=["qh"], writes=["R"])
        k.op("V", lambda v: v.memset(f["srow"][0:1, 0:1], 0.0), writes=["srow"])
        for ib in range(8):
            q = ib % 2

            def fnq(t, ib=ib, q=q):
                return t.matmul(pb[2 + q][0:1, :], c["ones16"],
                                f["R"][:, ib * 512:(ib + 1) * 512], start=True, stop=True)
            k.op("T", fnq, reads=["R", "ones16"], writes=[f"pb{2+q}"])
            k.op("V", lambda v, ib=ib, q=q: v.tensor_reduce(
                f["srow"][0:1, 1:2], pb[2 + q][0:1, :], AX.X, ALU.max),
                reads=[f"pb{2+q}", "srow"], writes=["srow"])
            k.op("V", lambda v: v.tensor_tensor(
                f["srow"][0:1, 0:1], f["srow"][0:1, 0:1], f["srow"][0:1, 1:2], ALU.max),
                reads=["srow"], writes=["srow"])
        k.op("S", lambda s: s.activation(f["srow"][0:1, 3:4], f["srow"][0:1, 0:1], AF.Sqrt),
             reads=["srow"], writes=["srow"])
        k.op("V", lambda v: v.tensor_scalar_mul(f["srow"][0:1, 4:5], f["srow"][0:1, 3:4],
                                                -SCL * float(np.sqrt(128.0))),
             reads=["srow"], writes=["srow"])

        def fnM(t):
            return t.matmul(pb[6][:, 0:1], c["ones1_32"], f["srow"][0:1, 4:5],
                            start=True, stop=True)
        k.op("T", fnM, reads=["srow", "ones1_32"], writes=["pb6"])
        k.op("V", lambda v: v.tensor_copy(f["negM"], pb[6][:, 0:1]),
             reads=["pb6"], writes=["negM"])

        for ibp in range(4):
            if filler:
                filler()

            def mk_frs(jt):
                q = jt % 2

                def frs(t, jt=jt, q=q):
                    for d in range(2):
                        last = t.matmul(pb[4 + d][0:1, :], c["ones16"],
                                        f["Estrip"][q][:, d * 512:(d + 1) * 512],
                                        start=(jt == 0), stop=(jt == 31))
                    return last
                k.op("T", frs, reads=[f"E{q}", "ones16"], writes=["pb4", "pb5"])

            for jt in range(32):
                q = jt % 2

                def fat(t, jt=jt, ibp=ibp, q=q):
                    for d in range(2):
                        ib = ibp * 2 + d
                        last = t.matmul(pb[2 * q + d][:, :],
                                        f["wh"][:, jt * 128:(jt + 1) * 128],
                                        f["qh"][:, ib * 512:(ib + 1) * 512],
                                        start=True, stop=True)
                    return last
                k.op("T", fat, reads=["wh", "qh"], writes=[f"pb{2*q}", f"pb{2*q+1}"])

                def fex(s, jt=jt, ibp=ibp, q=q):
                    return s.activation(f["Estrip"][q],
                                        c["pbig"][:, q * 1024:(q + 1) * 1024], AF.Exp,
                                        bias=f["negM"], scale=SCL,
                                        accum_out=f["csb"][:, jt * 4 + ibp:jt * 4 + ibp + 1])
                csl = ["csb"] if (jt == 0 and ibp == 0) or (jt == 31 and ibp == 3) else []
                k.op("S", fex, reads=[f"pb{2*q}", f"pb{2*q+1}", "negM"],
                     writes=[f"E{q}"] + csl)
                if jt > 0:
                    mk_frs(jt - 1)
            mk_frs(31)

            def frse(v, ibp=ibp):
                for d in range(2):
                    ib = ibp * 2 + d
                    last = v.tensor_copy(f["rsr"][0:1, ib * 512:(ib + 1) * 512],
                                         pb[4 + d][0:1, :])
                return last
            k.op("V", frse, reads=["pb4", "pb5"], writes=["rsr"])

        k.op("V", lambda v: v.tensor_reduce(
            f["cs"], f["csb"].rearrange("p (jt g) -> p jt g", g=4), AX.X, ALU.add),
            reads=["csb"], writes=["cs"])
        k.op("V", lambda v: v.reciprocal(f["colinv"], f["cs"]),
             reads=["cs"], writes=["colinv"])

        for q4 in range(4):
            if filler:
                filler()
            qs = q4 % 2

            def fqs1(v, q4=q4, qs=qs):
                a = f["qh"][:, q4 * 512:(q4 + 1) * 512]
                b = f["qh"][:, (4 + q4) * 512:(5 + q4) * 512]
                return v.tensor_tensor(f["qst"][qs], b, a, ALU.subtract)
            k.op("V", fqs1, reads=["qh"], writes=[f"qst{qs}"])
            k.op("V", lambda v, qs=qs: v.tensor_scalar(
                f["qst"][qs], f["qst"][qs], f["fb"], None, ALU.mult),
                reads=[f"qst{qs}", "fb"], writes=[f"qst{qs}"])

            def fqs2(v, q4=q4, qs=qs):
                a = f["qh"][:, q4 * 512:(q4 + 1) * 512]
                return v.tensor_tensor(f["qsel"][qs], f["qst"][qs], a, ALU.add)
            k.op("V", fqs2, reads=[f"qst{qs}", "qh"], writes=[f"qsel{qs}"])

            rs = f["rowsel"]

            def frsel1(v, q4=q4):
                a = f["rsr"][0:1, q4 * 512:(q4 + 1) * 512]
                b = f["rsr"][0:1, (4 + q4) * 512:(5 + q4) * 512]
                return v.tensor_tensor(rs, b, a, ALU.subtract)
            k.op("V", frsel1, reads=["rsr"], writes=["rowsel"])
            k.op("V", lambda v: v.tensor_scalar(rs, rs, f["flagrow"], None, ALU.mult),
                 reads=["rowsel", "flagrow"], writes=["rowsel"])

            def frsel2(v, q4=q4):
                a = f["rsr"][0:1, q4 * 512:(q4 + 1) * 512]
                return v.tensor_tensor(rs, rs, a, ALU.add)
            k.op("V", frsel2, reads=["rowsel", "rsr"], writes=["rowsel"])
            k.op("V", lambda v: v.reciprocal(rs, rs), reads=["rowsel"],
                 writes=["rowsel"])
            k.op("T", lambda t: t.matmul(pb[6][:, :], c["ones1_32"], f["rowsel"],
                                         start=True, stop=True),
                 reads=["rowsel", "ones1_32"], writes=["pb6"])
            k.op("V", lambda v: v.tensor_copy(f["Ub"], pb[6][:, :]),
                 reads=["pb6"], writes=["Ub"])

            def mk_fy(sp):
                q = sp % 2
                jts = (2 * sp, 2 * sp + 1)

                def fy(t, jts=jts, q=q, sp=sp):
                    for d, jt in enumerate(jts):
                        t.matmul(pb[7][:, :], f["vtm"][:, jt * 128:(jt + 1) * 128],
                                 f["Eu"][q][:, d * 512:(d + 1) * 512],
                                 start=(sp == 0 and d == 0), stop=False)
                        last = t.matmul(pb[7][:, :], f["vtm"][:, jt * 128:(jt + 1) * 128],
                                        f["Ew"][q][:, d * 512:(d + 1) * 512], start=False,
                                        stop=(sp == 15 and d == 1))
                    return last
                k.op("T", fy, reads=[f"Eu{q}", f"Ew{q}", "vtm"], writes=["pb7"])

            for sp in range(16):
                q = sp % 2
                jts = (2 * sp, 2 * sp + 1)

                def fat(t, jts=jts, qs=qs, q=q):
                    for d, jt in enumerate(jts):
                        last = t.matmul(pb[2 * q + d][:, :],
                                        f["wh"][:, jt * 128:(jt + 1) * 128],
                                        f["qsel"][qs], start=True, stop=True)
                    return last
                k.op("T", fat, reads=["wh", f"qsel{qs}"], writes=[f"pb{2*q}", f"pb{2*q+1}"])

                def fex2(s, q=q):
                    return s.activation(f["Es2"][q],
                                        c["pbig"][:, q * 1024:(q + 1) * 1024], AF.Exp,
                                        bias=f["negM"], scale=SCL)
                k.op("S", fex2, reads=[f"pb{2*q}", f"pb{2*q+1}", "negM"],
                     writes=[f"E2{q}"])

                def fuw(v, jts=jts, q=q):
                    for d, jt in enumerate(jts):
                        v.tensor_tensor(f["Eu"][q][:, d * 512:(d + 1) * 512],
                                        f["Es2"][q][:, d * 512:(d + 1) * 512],
                                        f["Ub"], ALU.mult)
                        last = v.tensor_scalar(f["Ew"][q][:, d * 512:(d + 1) * 512],
                                               f["Es2"][q][:, d * 512:(d + 1) * 512],
                                               f["colinv"][:, jt:jt + 1], None, ALU.mult)
                    return last
                k.op("V", fuw, reads=[f"E2{q}", "Ub", "colinv"],
                     writes=[f"Eu{q}", f"Ew{q}"])
                if sp > 0:
                    mk_fy(sp - 1)
            mk_fy(15)

            def fye(v, h=h, q4=q4):
                dst = f["yatt"][:, h * TOK + q4 * 512: h * TOK + (q4 + 1) * 512]
                return v.tensor_copy(dst, pb[7][:, :])
            k.op("V", fye, reads=["pb7"], writes=[f"ya{h}"])

    for oc in range(NCH):
        ws = oc % 2
        k.op("G", lambda g, oc=oc, ws=ws: g.dma_start(
            out=f["wslot"][ws], in_=io["wpl"][oc]),
            writes=[f"aws{ws}"], dma=True)
        for tb in range(4):
            q = tb % 2

            def fpj(t, oc=oc, ws=ws, tb=tb, q=q):
                for hh in range(NCH):
                    last = t.matmul(pb[q][:, :], f["wslot"][ws][:, hh * 128:(hh + 1) * 128],
                                    f["yatt"][:, hh * TOK + tb * 512:hh * TOK + (tb + 1) * 512],
                                    start=(hh == 0), stop=(hh == NCH - 1))
                return last
            k.op("T", fpj, reads=[f"aws{ws}"] + [f"ya{hh}" for hh in range(NCH)],
                 writes=[f"pb{q}"])
            k.op("Y", lambda y, oc=oc, tb=tb, q=q: y.dma_start(
                out=f["xc"][q], in_=xr[:, oc, tb * 512:(tb + 1) * 512]),
                writes=[f"xc{q}"], dma=True)

            def fout(v, q=q):
                return v.scalar_tensor_tensor(f["outc"][q], pb[q][:, :], 1.0,
                                              f["xc"][q], ALU.mult, ALU.add)
            k.op("V", fout, reads=[f"pb{q}", f"xc{q}"], writes=[f"outc{q}"])
            k.op("G", lambda g, oc=oc, tb=tb, q=q: g.dma_start(
                out=ydram[oc * 128:(oc + 1) * 128, tb * 512:(tb + 1) * 512],
                in_=f["outc"][q]),
                reads=[f"outc{q}"], writes=[f"acy{oc}_{tb}"], dma=True)


def build_attn0core_launch():
    k = KB()
    w0in = k.dram_in("w0", [D, TOK], BF16)
    w1in = k.dram_in("w1", [D, TOK], BF16)
    xin = k.dram_in("xT", [D, TOK])
    yout = k.dram_out("yT", [D, TOK])
    dr = _const_dram_decl(k)
    c = alloc_common(k)
    load_consts(k, c, dr)
    io = {
        "wpl": k.dram_in("wpl", [NCH, 128, 2048]),
        "g1": k.dram_in("g1", [128, 1]),
        "g2": k.dram_in("g2", [128, 1]),
        "g3": k.dram_in("g3", [128, 1]),
        "ones1_16d": k.dram_in("ones1_16d", [1, 128], BF16),
        "flag_d": k.dram_in("flag_d", [1, 1]),
    }
    f = alloc_attn0core(k)
    emit_attn0core(k, c, f, io, w0in, w1in, xin, yout)
    return k.finish()





def alloc_dequant(k, ar=None):
    sb = _SbAlloc(k, ar)
    f = {}
    f["bt"] = [sb(f"dq_bt{i}", [128, 1408], F32) for i in range(2)]
    f["h1"] = sb("dq_h1", [128, 1408], F32)
    f["hi"] = sb("dq_hi", [128, 1408], I32)
    f["hif"] = sb("dq_hif", [128, 1408], F32)
    f["lo"] = sb("dq_lo", [128, 1408], F32)
    f["s1"] = sb("dq_s1", [128, 1408], F32)
    f["m"] = sb("dq_m", [128, 1408], F32)
    f["r4"] = sb("dq_r4", [128, 1408], F32)
    f["t2"] = sb("dq_t2", [128, 1408], F32)
    f["sgn"] = sb("dq_sgn", [128, 1408], F32)
    f["out"] = [sb(f"dq_out{i}", [128, 2816], F32) for i in range(2)]
    f["o16"] = [sb(f"dq_o16{i}", [128, 2816], BF16) for i in range(2)]
    f["sct"] = sb("dq_sct", [128, 44], F32)
    f["sc16"] = sb("dq_sc16", [128, 44], mybir.dt.float16)
    f["trtmp"] = sb("dq_trtmp", [128, 256], F32)
    f["tstage"] = [sb(f"dq_tst{i}", [128, 2816], BF16) for i in range(2)]
    return f


def emit_dequant_tile(k, f, uid, pk_u8, sct_f16, w16_out, nby, ngrp):
    q = uid % 2
    k.op("G", lambda g: g.dma_start(out=f["bt"][q][:, 0:nby], in_=pk_u8),
         writes=[f"dqbt{q}"], dma=True)
    k.op("G", lambda g: g.dma_start(out=f["sc16"][:, 0:ngrp], in_=sct_f16),
         writes=["dqsc16"], dma=True)
    k.op("V", lambda v: v.tensor_copy(f["sct"][:, 0:ngrp], f["sc16"][:, 0:ngrp]),
         reads=["dqsc16"], writes=["dqsct"])

    bt = f["bt"][q][:, 0:nby]
    B_ = [f[nm] for nm in ("h1", "hi", "hif", "lo", "s1", "m", "r4", "t2", "sgn")]
    h1, hi, hif, lo, s1, m, r4, t2, sgn = [b[:, 0:nby] for b in B_]
    out = f["out"][q][:, 0:2 * nby]
    o16 = f["o16"][q][:, 0:2 * nby]

    def vo(fn, rd, wr):
        k.op("V", fn, reads=rd, writes=wr)

    R = f"dqbt{q}"
    vo(lambda v: v.tensor_scalar(h1, bt, 1.0 / 16.0, -0.46875, ALU.mult, ALU.add), [R], ["dqh1"])
    vo(lambda v: v.tensor_copy(hi, h1), ["dqh1"], ["dqhi"])
    vo(lambda v: v.tensor_copy(hif, hi), ["dqhi"], ["dqhif"])
    vo(lambda v: v.scalar_tensor_tensor(lo, hif, -16.0, bt, ALU.mult, ALU.add),
       ["dqhif", R], ["dqlo"])
    for par, n in ((0, lo), (1, hif)):
        dst = out.rearrange("p (t two) -> p t two", two=2)[:, :, par]
        vo(lambda v, n=n: v.tensor_scalar(s1, n, 8.0, None, ALU.is_ge),
           ["dqlo" if par == 0 else "dqhif"], ["dqs1"])
        vo(lambda v, n=n: v.scalar_tensor_tensor(m, s1, -8.0, n, ALU.mult, ALU.add),
           ["dqs1", "dqlo" if par == 0 else "dqhif"], ["dqm"])
        vo(lambda v: v.tensor_scalar(r4, m, -4.0, 0.0, ALU.add, ALU.max), ["dqm"], ["dqr4"])
        vo(lambda v: v.tensor_scalar(t2, m, -6.0, 0.0, ALU.add, ALU.max), ["dqm"], ["dqt2"])
        vo(lambda v: v.scalar_tensor_tensor(t2, r4, 0.5, t2, ALU.mult, ALU.add),
           ["dqr4", "dqt2"], ["dqt2"])
        vo(lambda v: v.scalar_tensor_tensor(t2, m, 0.5, t2, ALU.mult, ALU.add),
           ["dqm", "dqt2"], ["dqt2"])
        vo(lambda v: v.tensor_scalar(sgn, s1, -2.0, 1.0, ALU.mult, ALU.add), ["dqs1"], ["dqsgn"])
        vo(lambda v, dst=dst: v.tensor_tensor(dst, t2, sgn, ALU.mult),
           ["dqt2", "dqsgn"], [f"dqoutp{par}"])
    def fsc(v):
        for g in range(ngrp):
            last = v.tensor_scalar(o16[:, g * 128:(g + 1) * 128],
                                   out[:, g * 128:(g + 1) * 128],
                                   f["sct"][:, g:g + 1], None, ALU.mult)
        return last
    k.op("V", fsc, reads=["dqoutp0", "dqoutp1", "dqsct"], writes=[f"dqo16{q}"])
    if w16_out is not None:
        k.op("G", lambda g: g.dma_start(out=w16_out, in_=o16), reads=[f"dqo16{q}"],
             writes=[f"dqw{uid}"], dma=True)
        return
    slab, c, trtmp, tstage = f["_slab"], f["_c"], f["trtmp"], f["tstage"][q]
    nblk = ngrp
    for blk in range(nblk):
        bq = blk % 2
        k.op("V", lambda v, blk=blk, bq=bq: v.tensor_copy(
            trtmp[:, bq * 128:(bq + 1) * 128], o16[:, blk * 128:(blk + 1) * 128]),
            reads=[f"dqo16{q}"], writes=[f"dqtr{bq}"])
        k.op("T", lambda t, blk=blk, bq=bq: t.transpose(
            c["pb"][4 + bq][:, 0:128], trtmp[:, bq * 128:(bq + 1) * 128], c["idn32"]),
            reads=[f"dqtr{bq}", "idn32"], writes=[f"pb{4+bq}"])
        k.op("V", lambda v, blk=blk, bq=bq: v.tensor_copy(
            tstage[:, blk * 128:(blk + 1) * 128], c["pb"][4 + bq][:, 0:128]),
            reads=[f"pb{4+bq}"], writes=[f"dqst{q}"])
    k.op("G", lambda g: g.dma_start(out=slab, in_=tstage[:, 0:nblk * 128]),
         reads=[f"dqst{q}"], writes=[f"dqw{uid}"], dma=True)


def build_dequant_launch():
    k = KB()
    gu_pk = k.dram_in("gu_pk", [1408, 1024], U8)
    gu_sc = k.dram_in("gu_sc", [1408, 16], mybir.dt.float16)
    dn_pk = k.dram_in("dn_pk", [256, 2816], U8)
    dn_sc = k.dram_in("dn_sc", [256, 44], mybir.dt.float16)
    gu_out = k.dram_out("gu_out", [1408, 2048], BF16)
    dn_out = k.dram_out("dn_out", [256, 5632], BF16)
    f = alloc_dequant(k)
    uid = 0
    for r in range(11):
        emit_dequant_tile(k, f, uid, gu_pk[r * 128:(r + 1) * 128, :],
                          gu_sc[r * 128:(r + 1) * 128, :],
                          gu_out[r * 128:(r + 1) * 128, :], 1024, 16)
        uid += 1
    for r in range(2):
        emit_dequant_tile(k, f, uid, dn_pk[r * 128:(r + 1) * 128, 0:1408],
                          dn_sc[r * 128:(r + 1) * 128, 0:22],
                          dn_out[r * 128:(r + 1) * 128, 0:2816], 1408, 22)
        uid += 1
        emit_dequant_tile(k, f, uid, dn_pk[r * 128:(r + 1) * 128, 1408:2816],
                          dn_sc[r * 128:(r + 1) * 128, 22:44],
                          dn_out[r * 128:(r + 1) * 128, 2816:5632], 1408, 22)
        uid += 1
    return k.finish()


def alloc_dequant2(k, ar):
    f = {}
    f["bt"] = [ar.get(f"d2_bt{i}", (128, 1408), F32) for i in range(2)]
    for nm in ("A", "C", "D", "E", "F"):
        f[nm] = ar.get("d2_" + nm, (128, 1408), F32)
    f["Bf"], f["Bi"] = ar.get2("d2_B", (128, 1408), F32, I32)
    f["out"] = [ar.get(f"d2_out{i}", (128, 2816), F32) for i in range(2)]
    f["o16"] = [ar.get(f"d2_o16{i}", (128, 2816), BF16) for i in range(2)]
    f["sct"] = [ar.get(f"d2_sct{i}", (128, 44), F32) for i in range(2)]
    f["sc16"] = [ar.get(f"d2_sc16{i}", (128, 44), mybir.dt.float16)
                 for i in range(2)]
    f["tstage"] = [ar.get(f"d2_tst{i}", (128, 2816), BF16) for i in range(2)]
    return f


def make_dequant_chunks(k, f, tiles):
    chunks = []
    res = []
    for idx, (pk, sc, slab, nby, ngrp) in enumerate(tiles):
        q = idx % 2
        bt = f["bt"][q][:, 0:nby]
        A, C = f["A"][:, 0:nby], f["C"][:, 0:nby]
        Dt, E, Ff = f["D"][:, 0:nby], f["E"][:, 0:nby], f["F"][:, 0:nby]
        Bf, Bi = f["Bf"][:, 0:nby], f["Bi"][:, 0:nby]
        out = f["out"][q][:, 0:2 * nby]
        o16 = f["o16"][q][:, 0:2 * nby]
        sct = f["sct"][q][:, 0:ngrp]
        sc16 = f["sc16"][q][:, 0:ngrp]
        tst = f["tstage"][q]
        R = f"dqbt{q}"

        def c_load(pk=pk, sc=sc, bt=bt, A=A, C=C, Bi=Bi, sct=sct, sc16=sc16,
                   q=q, R=R):
            k.op("G", lambda g: g.dma_start(out=bt, in_=pk),
                 writes=[R], dma=True)
            k.op("G", lambda g: g.dma_start(out=sc16, in_=sc),
                 writes=[f"dqsc16{q}"], dma=True)
            k.op("V", lambda v: v.tensor_copy(sct, sc16),
                 reads=[f"dqsc16{q}"], writes=[f"dqsct{q}"])
            k.op("V", lambda v: v.tensor_scalar(A, bt, 1.0 / 16.0, -0.46875,
                                                ALU.mult, ALU.add),
                 reads=[R], writes=["dqA"])
            k.op("V", lambda v: v.tensor_copy(Bi, A), reads=["dqA"], writes=["dqB"])
            k.op("V", lambda v: v.tensor_copy(C, Bi), reads=["dqB"], writes=["dqC"])
            k.op("V", lambda v: v.scalar_tensor_tensor(A, C, -16.0, bt,
                                                       ALU.mult, ALU.add),
                 reads=["dqC", R], writes=["dqA"])
        chunks.append(c_load)

        for par in range(2):
            def c_par(par=par, A=A, C=C, Bf=Bf, Dt=Dt, E=E, Ff=Ff, out=out, q=q):
                n = A if par == 0 else C
                nn = "dqA" if par == 0 else "dqC"
                k.op("V", lambda v: v.tensor_scalar(Bf, n, 8.0, None, ALU.is_ge),
                     reads=[nn], writes=["dqB"])
                k.op("V", lambda v: v.scalar_tensor_tensor(Dt, Bf, -8.0, n,
                                                           ALU.mult, ALU.add),
                     reads=["dqB", nn], writes=["dqD"])
                k.op("V", lambda v: v.tensor_scalar(E, Dt, -4.0, 0.0,
                                                    ALU.add, ALU.max),
                     reads=["dqD"], writes=["dqE"])
                k.op("V", lambda v: v.tensor_scalar(Ff, Dt, -6.0, 0.0,
                                                    ALU.add, ALU.max),
                     reads=["dqD"], writes=["dqF"])
                k.op("V", lambda v: v.scalar_tensor_tensor(Ff, E, 0.5, Ff,
                                                           ALU.mult, ALU.add),
                     reads=["dqE", "dqF"], writes=["dqF"])
                k.op("V", lambda v: v.scalar_tensor_tensor(Ff, Dt, 0.5, Ff,
                                                           ALU.mult, ALU.add),
                     reads=["dqD", "dqF"], writes=["dqF"])
                k.op("V", lambda v: v.tensor_scalar(Bf, Bf, -2.0, 1.0,
                                                    ALU.mult, ALU.add),
                     reads=["dqB"], writes=["dqB"])
                dst = out.rearrange("p (t two) -> p t two", two=2)[:, :, par]
                k.op("V", lambda v: v.tensor_tensor(dst, Ff, Bf, ALU.mult),
                     reads=["dqF", "dqB"], writes=[f"dqoutp{par}"])
            chunks.append(c_par)

        def c_scale(out=out, o16=o16, sct=sct, ngrp=ngrp, q=q):
            def fsc(v):
                for g in range(ngrp):
                    last = v.tensor_scalar(o16[:, g * 128:(g + 1) * 128],
                                           out[:, g * 128:(g + 1) * 128],
                                           sct[:, g:g + 1], None, ALU.mult)
                return last
            k.op("V", fsc, reads=["dqoutp0", "dqoutp1", f"dqsct{q}"],
                 writes=[f"dqo16{q}"])
        chunks.append(c_scale)

        def c_tr(o16=o16, tst=tst, slab=slab, nby=nby, q=q, idx=idx):
            for blk in range(2 * nby // 128):
                k.op("Y", lambda y, blk=blk: y.dma_start(
                    out=tst[:, blk * 128:(blk + 1) * 128],
                    in_=o16[:, blk * 128:(blk + 1) * 128], transpose=True),
                    reads=[f"dqo16{q}"], writes=[f"dqts{q}"], dma=True)
            k.op("G", lambda g: g.dma_start(out=slab, in_=tst[:, 0:2 * nby]),
                 reads=[f"dqts{q}"], writes=[f"dqw{idx}"], dma=True)
        chunks.append(c_tr)
        res.append(f"dqw{idx}")
    return chunks, res


def alloc_tern(k, ar):
    ft = {}
    ft["wbuf"] = [ar.get(f"tn_wb{i}", (128, 1024), F32) for i in range(2)]
    ft["wt16"] = [ar.get(f"tn_wt{i}", (128, 1024), BF16) for i in range(2)]
    ft["absacc"] = ar.get("tn_absacc", (128, 2), F32)
    ft["sv"] = ar.get("tn_sv", (1, 4), F32)
    ft["s1b"] = ar.get("tn_s1b", (128, 1), F32)
    return ft


def make_tern_chunks(k, c, ft, wsrc_l, n_oc, w16dst, uid, mean_out):
    pb = c["pb"]
    nel = float(n_oc * 128 * 2048)
    chunks = []

    def c_init():
        k.op("V", lambda v: v.memset(ft["absacc"][:, 0:1], 0.0),
             writes=["tnaa"])
    chunks.append(c_init)

    for oc in range(n_oc):
        def c_mean(oc=oc):
            for hh in range(2):
                q = hh
                k.op("G", lambda g, oc=oc, hh=hh, q=q: g.dma_start(
                    out=ft["wbuf"][q], in_=wsrc_l[oc][:, hh * 1024:(hh + 1) * 1024]),
                    writes=[f"tnwb{q}"], dma=True)
                k.op("V", lambda v, q=q: v.tensor_reduce(
                    ft["absacc"][:, 1:2], ft["wbuf"][q], AX.X, ALU.add,
                    apply_absolute_value=True),
                    reads=[f"tnwb{q}"], writes=["tnap"])
                k.op("V", lambda v: v.tensor_tensor(
                    ft["absacc"][:, 0:1], ft["absacc"][:, 0:1],
                    ft["absacc"][:, 1:2], ALU.add),
                    reads=["tnap", "tnaa"], writes=["tnaa"])
        chunks.append(c_mean)

    def c_mid():
        def ftot(t):
            return t.matmul(pb[6][0:1, 0:1], ft["absacc"][:, 0:1],
                            c["ones_col32"], start=True, stop=True)
        k.op("T", ftot, reads=["tnaa", "ones_col32"], writes=["pb6"])
        k.op("V", lambda v: v.tensor_scalar(ft["sv"][0:1, 0:1], pb[6][0:1, 0:1],
                                            1.0 / nel, 1e-5, ALU.mult, ALU.max),
             reads=["pb6"], writes=["tnsv"])
        k.op("V", lambda v: v.reciprocal(ft["sv"][0:1, 1:2], ft["sv"][0:1, 0:1]),
             reads=["tnsv"], writes=["tnsv"])
        k.op("G", lambda g: g.dma_start(out=mean_out, in_=ft["sv"][0:1, 0:1]),
             reads=["tnsv"], writes=[uid + "mo"], dma=True)

        def fbc(t):
            return t.matmul(pb[6][:, 0:1], c["ones1_32"], ft["sv"][0:1, 1:2],
                            start=True, stop=True)
        k.op("T", fbc, reads=["tnsv", "ones1_32"], writes=["pb6"])
        k.op("V", lambda v: v.tensor_copy(ft["s1b"], pb[6][:, 0:1]),
             reads=["pb6"], writes=["tns1b"])
    chunks.append(c_mid)

    for oc in range(n_oc):
        def c_quant(oc=oc):
            for hh in range(2):
                q = hh
                dst = w16dst[oc][:, hh * 1024:(hh + 1) * 1024]
                k.op("G", lambda g, oc=oc, hh=hh, q=q: g.dma_start(
                    out=ft["wbuf"][q], in_=wsrc_l[oc][:, hh * 1024:(hh + 1) * 1024]),
                    writes=[f"tnwb{q}"], dma=True)
                k.op("V", lambda v, q=q: v.tensor_scalar(
                    ft["wbuf"][q], ft["wbuf"][q], ft["s1b"], None, ALU.mult),
                    reads=[f"tnwb{q}", "tns1b"], writes=[f"tnwb{q}"])
                k.op("V", lambda v, q=q: v.tensor_scalar(
                    ft["wt16"][q], ft["wbuf"][q], 0.5, None, ALU.is_gt),
                    reads=[f"tnwb{q}"], writes=[f"tnwa{q}"])
                k.op("V", lambda v, q=q: v.tensor_scalar(
                    ft["wbuf"][q], ft["wbuf"][q], -0.5, None, ALU.is_lt),
                    reads=[f"tnwb{q}"], writes=[f"tnwb{q}"])
                k.op("V", lambda v, q=q: v.tensor_tensor(
                    ft["wt16"][q], ft["wt16"][q], ft["wbuf"][q], ALU.subtract),
                    reads=[f"tnwa{q}", f"tnwb{q}"], writes=[f"tnwt{q}"])
                k.op("G", lambda g, dst=dst, q=q: g.dma_start(
                    out=dst, in_=ft["wt16"][q]),
                    reads=[f"tnwt{q}"], writes=[f"{uid}wo{oc}"], dma=True)
        chunks.append(c_quant)
    return chunks


def prep_dequant_host(inputs, core):
    gu_pk = np.ascontiguousarray(np.asarray(inputs["L1_ffn_gate_up_packed"]))
    dn_pk = np.ascontiguousarray(np.asarray(inputs["L1_ffn_down_packed"]))
    gu_b = gu_pk.view(np.uint8).reshape(11264, 1024)
    dn_b = dn_pk.view(np.uint8).reshape(2048, 2816)
    gu_sc = np.ascontiguousarray(np.asarray(inputs["L1_ffn_gate_up_scales"]).T)
    dn_sc = np.ascontiguousarray(np.asarray(inputs["L1_ffn_down_scales"]).T)
    return {
        "gu_pk": np.ascontiguousarray(gu_b[core * 1408:(core + 1) * 1408]),
        "gu_sc": np.ascontiguousarray(gu_sc[core * 1408:(core + 1) * 1408]),
        "dn_pk": np.ascontiguousarray(dn_b[core * 256:(core + 1) * 256]),
        "dn_sc": np.ascontiguousarray(dn_sc[core * 256:(core + 1) * 256]),
    }



def alloc_ffn1(k, ar=None):
    sb = _SbAlloc(k, ar)
    f = {}
    f["xc"] = [sb(f"f1_xc{i}", [128, 512], F32) for i in range(2)]
    f["x16c"] = [sb(f"f1_x16c{i}", [128, 512], BF16) for i in range(2)]
    f["sq16c"] = [sb(f"f1_sq16c{i}", [128, 512], BF16) for i in range(2)]
    f["statrow"] = sb("f1_stat", [1, 2048], F32)
    f["htmp"] = [sb(f"f1_ht{i}", [128, 512], F32) for i in range(2)]
    f["xn16"] = sb("f1_xn16", [128, NCH * 512], BF16)
    f["hbuf"] = sb("f1_hbuf", [128, 44 * 512], BF16)
    f["ybuf"] = sb("f1_ybuf", [128, NCH * 512], F32)
    f["wslot"] = [sb(f"f1_ws{i}", [128, 2048], BF16) for i in range(4)]
    f["dslot"] = [sb(f"f1_ds{i}", [128, 5632], BF16) for i in range(2)]
    f["sg"] = [sb(f"f1_sg{i}", [128, 512], F32) for i in range(4)]
    f["yrow"] = sb("f1_yrow", [1, 512], F32)
    f["outc"] = [sb(f"f1_outc{i}", [128, 512], F32) for i in range(2)]
    return f


def emit_ffn1(k, c, f, io, xdram, ydram, uid, wdeps=()):
    pb = c["pb"]
    xr = xdram.rearrange("(c p) t -> p c t", p=128)
    guT, dnT = io["guT_l"], io["dnT_l"]

    for it in range(4):
        th = it * 512
        emit_ln_half(k, c, f, xr, th, with_gate=False)
        for ch in range(NCH):
            q = ch % 2
            k.op("Y", lambda y, ch=ch, q=q, th=th: y.dma_start(
                out=f["xc"][q], in_=xr[:, ch, th:th + 512]),
                writes=[f"xc{q}"], dma=True)
            k.op("V", lambda v, ch=ch, q=q: v.tensor_tensor(
                f["htmp"][q], f["xc"][q], pb[3][:, :], ALU.mult),
                reads=[f"xc{q}", "pb3"], writes=[f"htmp{q}"])

            def fxn(v, ch=ch, q=q):
                dst = f["xn16"][:, ch * 512:(ch + 1) * 512]
                return v.tensor_tensor(dst, f["htmp"][q], pb[4][:, :], ALU.add)
            k.op("V", fxn, reads=[f"htmp{q}", "pb4"], writes=[f"f1xn{ch}"])
        xn_deps = [f"f1xn{ch}" for ch in range(NCH)]

        for j in range(44):
            q = j % 2
            bb = 4 * q
            for half, slot in ((0, 2 * q), (1, 2 * q + 1)):
                src = guT[j + 44 * half]
                k.op("G", lambda g, src=src, slot=slot: g.dma_start(
                    out=f["wslot"][slot], in_=src),
                    reads=list(wdeps), writes=[f"f1ws{slot}"], dma=True)

            def fup(t, q=q, bb=bb):
                for ch in range(NCH):
                    t.matmul(pb[bb][:, :], f["wslot"][2 * q][:, ch * 128:(ch + 1) * 128],
                             f["xn16"][:, ch * 512:(ch + 1) * 512],
                             start=(ch == 0), stop=(ch == NCH - 1))
                for ch in range(NCH):
                    last = t.matmul(pb[bb + 2][:, :],
                                    f["wslot"][2 * q + 1][:, ch * 128:(ch + 1) * 128],
                                    f["xn16"][:, ch * 512:(ch + 1) * 512],
                                    start=(ch == 0), stop=(ch == NCH - 1))
                return last
            k.op("T", fup, reads=[f"f1ws{2*q}", f"f1ws{2*q+1}"] + xn_deps,
                 writes=[f"pb{bb}", f"pb{bb+2}"])

            k.op("S", lambda s, q=q, bb=bb: s.activation(
                f["sg"][2 * q], pb[bb][:, :], AF.Silu),
                reads=[f"pb{bb}"], writes=[f"f1sg{2*q}"])

            def fh(v, j=j, q=q, bb=bb):
                dst = f["hbuf"][:, j * 512:(j + 1) * 512]
                return v.tensor_tensor(dst, f["sg"][2 * q], pb[bb + 2][:, :], ALU.mult)
            k.op("V", fh, reads=[f"f1sg{2*q}", f"pb{bb+2}"], writes=[f"f1hb{j}"])

        for oc in range(NCH):
            slot = oc % 2
            k.op("G", lambda g, oc=oc, slot=slot: g.dma_start(
                out=f["dslot"][slot], in_=dnT[oc]),
                reads=list(wdeps), writes=[f"f1ds{slot}"], dma=True)
            bank = 5 + oc % 2

            def fdn(t, oc=oc, slot=slot, bank=bank):
                for ci in range(44):
                    last = t.matmul(pb[bank][:, :],
                                    f["dslot"][slot][:, ci * 128:(ci + 1) * 128],
                                    f["hbuf"][:, ci * 512:(ci + 1) * 512],
                                    start=(ci == 0), stop=(ci == 43))
                return last
            k.op("T", fdn, reads=[f"f1ds{slot}"] + [f"f1hb{j}" for j in range(44)],
                 writes=[f"pb{bank}"])
            k.op("V", lambda v, oc=oc, bank=bank: v.tensor_copy(
                f["ybuf"][:, oc * 512:(oc + 1) * 512], pb[bank][:, :]),
                reads=[f"pb{bank}"], writes=[f"f1y{oc}"])
            k.op("S", lambda s, oc=oc: s.activation(
                f["sq16c"][oc % 2], f["ybuf"][:, oc * 512:(oc + 1) * 512], AF.Square),
                reads=[f"f1y{oc}"], writes=[f"sq16c{oc%2}"])

            def fst(t, oc=oc):
                return t.matmul(pb[7][0:1, :], c["ones16"], f["sq16c"][oc % 2],
                                start=(oc == 0), stop=(oc == NCH - 1))
            k.op("T", fst, reads=[f"sq16c{oc%2}", "ones16"], writes=["pb7"])

        k.op("S", lambda s: s.activation(f["yrow"], pb[7][0:1, :], AF.Sqrt,
                                         bias=c["kv"][0:1, 2:3], scale=1.0 / D),
             reads=["pb7", "kv"], writes=["yrow"])
        k.op("V", lambda v: v.reciprocal(f["yrow"], f["yrow"]),
             reads=["yrow"], writes=["yrow"])

        def fyb(t):
            return t.matmul(pb[7][:, :], c["ones1_32"], f["yrow"], start=True, stop=True)
        k.op("T", fyb, reads=["yrow", "ones1_32"], writes=["pb7"])

        for ch in range(NCH):
            q = ch % 2
            k.op("Y", lambda y, ch=ch, q=q, th=th: y.dma_start(
                out=f["xc"][q], in_=xr[:, ch, th:th + 512]),
                writes=[f"xc{q}"], dma=True)
            k.op("V", lambda v, ch=ch, q=q: v.tensor_tensor(
                f["htmp"][q], f["ybuf"][:, ch * 512:(ch + 1) * 512], pb[7][:, :], ALU.mult),
                reads=[f"f1y{ch}", "pb7"], writes=[f"htmp{q}"])

            def fo(v, ch=ch, q=q):
                return v.scalar_tensor_tensor(f["outc"][q], f["htmp"][q], 0.5,
                                              f["xc"][q], ALU.mult, ALU.add)
            k.op("V", fo, reads=[f"htmp{q}", f"xc{q}"], writes=[f"outc{q}"])
            k.op("G", lambda g, ch=ch, q=q, th=th: g.dma_start(
                out=ydram[ch * 128:(ch + 1) * 128, th:th + 512], in_=f["outc"][q]),
                reads=[f"outc{q}"], writes=[f"{uid}f1y{ch}_{th}"], dma=True)


def prep_ffn1_host(gu16, dn16):
    guT = np.ascontiguousarray(
        gu16.T.reshape(NCH, 128, 88, 128).transpose(2, 1, 0, 3).reshape(88, 128, 2048))
    dnT = np.ascontiguousarray(
        dn16.T.reshape(44, 128, 16, 128).transpose(2, 1, 0, 3).reshape(16, 128, 44 * 128))
    return {"guT_l": guT, "dnT_l": dnT}


def build_ffn1_launch():
    k = KB()
    xin = k.dram_in("xT", [D, TOK])
    yout = k.dram_out("yT", [D, TOK])
    dr = _const_dram_decl(k)
    c = alloc_common(k)
    load_consts(k, c, dr)
    io = {"guT_l": k.dram_in("guT_l", [88, 128, 2048], BF16),
          "dnT_l": k.dram_in("dnT_l", [16, 128, 44 * 128], BF16)}
    f = alloc_ffn1(k)
    emit_ffn1(k, c, f, io, xin, yout, uid="a")
    return k.finish()





SCL128 = float(1.0 / np.sqrt(128.0))
RMS1_EPS = 1e-6


def alloc_attn1a(k, ar=None):
    sb = _SbAlloc(k, ar)
    f = {}
    f["wbuf"] = [sb(f"a1_wb{i}", [128, 2048], F32) for i in range(2)]
    f["wt16"] = [sb(f"a1_wt{i}", [128, 2048], BF16) for i in range(2)]
    f["absacc"] = sb("a1_absacc", [128, 2], F32)
    f["sv"] = sb("a1_sv", [1, 16], F32)
    f["s1b"] = sb("a1_s1b", [128, 1], F32)
    f["m1b"] = sb("a1_m1b", [128, 1], F32)
    f["xc"] = [sb(f"a1_xc{i}", [128, 512], F32) for i in range(2)]
    f["x16c"] = [sb(f"a1_x16c{i}", [128, 512], BF16) for i in range(2)]
    f["sq16c"] = [sb(f"a1_sq16c{i}", [128, 512], BF16) for i in range(2)]
    f["statrow"] = sb("a1_stat", [1, 2048], F32)
    f["htmp"] = [sb(f"a1_ht{i}", [128, 512], F32) for i in range(2)]
    f["zn16"] = sb("a1_zn16", [128, NCH * 512], BF16)
    f["r2row"] = sb("a1_r2row", [1, 512], F32)
    f["qnw"] = sb("a1_qnw", [128, 16], F32)
    f["qkvsb"] = sb("a1_qkvsb", [128, 48 * 512], BF16)
    f["wslot"] = [sb(f"a1_ws{i}", [128, 2048], BF16) for i in range(4)]
    f["negs"] = sb("a1_negs", [128, 16], F32)
    f["lamrev"] = sb("a1_lamrev", [128, 16], F32)
    f["lamT"] = sb("a1_lamT", [128, 16], F32)
    f["D2"] = sb("a1_D2", [128, 128], F32)
    f["B2"] = sb("a1_B2", [128, 128], F32)
    f["trevcol"] = sb("a1_trev", [128, 1], F32)
    f["trow"] = sb("a1_trow", [1, 2048], F32)
    f["MT"] = sb("a1_MT", [128, 16 * 128], BF16)
    f["lambc"] = sb("a1_lambc", [128, 16 * 128], BF16)
    f["mtmp"] = sb("a1_mtmp", [128, 128], F32)
    f["rtmp"] = sb("a1_rtmp", [1, 2048], F32)
    f["Ssb"] = sb("a1_S", [128, 16 * 128], F32)
    f["S16"] = sb("a1_S16", [128, 16 * 128], BF16)
    f["vct"] = [sb(f"a1_vct{i}", [128, 128], BF16) for i in range(2)]
    f["kct"] = [sb(f"a1_kct{i}", [128, 128], BF16) for i in range(2)]
    f["kct2"] = [sb(f"a1_kct2{i}", [128, 128], BF16) for i in range(2)]
    f["sc16"] = [sb(f"a1_sc{i}", [128, 128], BF16) for i in range(2)]
    f["qlb"] = [sb(f"a1_qlb{i}", [128, 128], BF16) for i in range(2)]
    f["ost"] = sb("a1_ost", [128, 16 * 512], BF16)
    f["m1r"] = sb("a1_m1r", [1, 1], F32)
    return f


def emit_ternarize(k, c, f, wsrc_l, n_oc, w16dst, uid, mean_out=None):
    pb = c["pb"]
    nel = float(n_oc * 128 * 2048)
    k.op("V", lambda v: v.memset(f["absacc"][:, 0:1], 0.0), writes=[uid + "aa"])
    for oc in range(n_oc):
        q = oc % 2
        k.op("G", lambda g, oc=oc, q=q: g.dma_start(out=f["wbuf"][q], in_=wsrc_l[oc]),
             writes=[f"{uid}wb{q}"], dma=True)
        k.op("V", lambda v, q=q: v.tensor_reduce(
            f["absacc"][:, 1:2], f["wbuf"][q], AX.X, ALU.add, apply_absolute_value=True),
            reads=[f"{uid}wb{q}"], writes=[uid + "ap"])
        k.op("V", lambda v: v.tensor_tensor(f["absacc"][:, 0:1], f["absacc"][:, 0:1],
                                            f["absacc"][:, 1:2], ALU.add),
             reads=[uid + "ap", uid + "aa"], writes=[uid + "aa"])

    def ftot(t):
        return t.matmul(pb[0][0:1, 0:1], f["absacc"][:, 0:1], c["ones_col32"],
                        start=True, stop=True)
    k.op("T", ftot, reads=[uid + "aa", "ones_col32"], writes=["pb0"])
    k.op("V", lambda v: v.tensor_scalar(f["sv"][0:1, 0:1], pb[0][0:1, 0:1],
                                        1.0 / nel, 1e-5, ALU.mult, ALU.max),
         reads=["pb0"], writes=[uid + "sv"])
    k.op("V", lambda v: v.reciprocal(f["sv"][0:1, 1:2], f["sv"][0:1, 0:1]),
         reads=[uid + "sv"], writes=[uid + "sv"])

    def fbc(t):
        t.matmul(pb[0][:, 0:1], c["ones1_32"], f["sv"][0:1, 1:2], start=True, stop=True)
        return t.matmul(pb[1][:, 0:1], c["ones1_32"], f["sv"][0:1, 0:1],
                        start=True, stop=True)
    k.op("T", fbc, reads=[uid + "sv", "ones1_32"], writes=["pb0", "pb1"])
    k.op("V", lambda v: v.tensor_copy(f["s1b"], pb[0][:, 0:1]), reads=["pb0"],
         writes=[uid + "s1b"])
    k.op("V", lambda v: v.tensor_copy(f["m1b"], pb[1][:, 0:1]), reads=["pb1"],
         writes=[uid + "m1b"])
    if mean_out is not None:
        k.op("G", lambda g: g.dma_start(out=mean_out, in_=f["sv"][0:1, 0:1]),
             reads=[uid + "sv"], writes=[uid + "mo"], dma=True)

    for oc in range(n_oc):
        q = oc % 2
        k.op("G", lambda g, oc=oc, q=q: g.dma_start(out=f["wbuf"][q], in_=wsrc_l[oc]),
             writes=[f"{uid}wb{q}"], dma=True)
        k.op("V", lambda v, q=q: v.tensor_scalar(
            f["wbuf"][q], f["wbuf"][q], f["s1b"], None, ALU.mult),
            reads=[f"{uid}wb{q}", uid + "s1b"], writes=[f"{uid}wb{q}"])
        k.op("V", lambda v, q=q: v.tensor_scalar(
            f["htmp"][0 if q == 0 else 1][:, 0:512], f["wbuf"][q][:, 0:512], 0.0, None,
            ALU.bypass), reads=[], writes=["_dummy"]) if False else None
        k.op("V", lambda v, q=q: v.tensor_scalar(
            f["wt16"][q], f["wbuf"][q], 0.5, None, ALU.is_gt),
            reads=[f"{uid}wb{q}"], writes=[f"{uid}wa{q}"])
        k.op("V", lambda v, q=q: v.tensor_scalar(
            f["wbuf"][q], f["wbuf"][q], -0.5, None, ALU.is_lt),
            reads=[f"{uid}wb{q}"], writes=[f"{uid}wb{q}"])
        k.op("V", lambda v, q=q: v.tensor_tensor(
            f["wt16"][q], f["wt16"][q], f["wbuf"][q], ALU.subtract),
            reads=[f"{uid}wa{q}", f"{uid}wb{q}"], writes=[f"{uid}wt{q}"])
        k.op("G", lambda g, oc=oc, q=q: g.dma_start(out=w16dst[oc], in_=f["wt16"][q]),
             reads=[f"{uid}wt{q}"], writes=[f"{uid}wo{oc}"], dma=True)


def emit_attn1a(k, c, f, io, xdram, outs):
    pb = c["pb"]
    xr = xdram.rearrange("(c p) t -> p c t", p=128)
    oL, qT, lamg_d, send_d = outs["oL"], outs["qT"], outs["lamg"], outs["send"]

    for nm in ("D2", "B2", "trevcol", "trow", "qnw"):
        k.op("Y", lambda y, nm=nm: y.dma_start(out=f[nm], in_=io[nm + "_d"]),
             writes=[nm], dma=True)
    k.op("Y", lambda y: y.dma_start(out=f["sv"][0:1, 2:3], in_=io["dummy1"]),
         writes=["svd"], dma=True)
    k.op("Y", lambda y: y.dma_start(out=f["sv"][0:1, 0:16], in_=io["srow_d"]),
         writes=["svs"], dma=True)
    k.op("V", lambda v: v.tensor_scalar_mul(f["rtmp"][0:1, 0:16], f["sv"][0:1, 0:16], -1.0),
         reads=["svs"], writes=["rtmp"])

    def fnegs(t):
        return t.matmul(pb[0][:, 0:16], c["ones1_32"], f["rtmp"][0:1, 0:16],
                        start=True, stop=True)
    k.op("T", fnegs, reads=["rtmp", "ones1_32"], writes=["pb0"])
    k.op("V", lambda v: v.tensor_copy(f["negs"], pb[0][:, 0:16]), reads=["pb0"],
         writes=["negs"])

    for h in range(NCH):
        k.op("V", lambda v, h=h: v.tensor_tensor(
            f["lamrev"][:, h:h + 1], f["trevcol"], f["negs"][:, h:h + 1], ALU.mult),
            reads=["trevcol", "negs"], writes=["lamrev"])
    k.op("S", lambda s: s.activation(f["lamrev"], f["lamrev"], AF.Exp),
         reads=["lamrev"], writes=["lamrev"])
    k.op("V", lambda v: v.tensor_scalar_mul(f["lamT"], f["negs"], 128.0),
         reads=["negs"], writes=["lamT"])
    k.op("S", lambda s: s.activation(f["lamT"], f["lamT"], AF.Exp),
         reads=["lamT"], writes=["lamT"])

    for h in range(NCH):
        k.op("V", lambda v, h=h: v.tensor_scalar(
            f["mtmp"], f["D2"], f["negs"][:, h:h + 1], None, ALU.mult),
            reads=["D2", "negs"], writes=["mtmp"])
        k.op("S", lambda s: s.activation(f["mtmp"], f["mtmp"], AF.Exp),
             reads=["mtmp"], writes=["mtmp"])
        k.op("V", lambda v, h=h: v.tensor_tensor(
            f["MT"][:, h * 128:(h + 1) * 128], f["mtmp"], f["B2"], ALU.mult),
            reads=["mtmp", "B2"], writes=["MT"])
        k.op("V", lambda v, h=h: v.tensor_scalar(
            f["rtmp"][0:1, 0:128], f["D2"][0:1, :], f["negs"][0:1, h:h + 1], None, ALU.mult),
            reads=["D2", "negs"], writes=["rtmp"])
        k.op("S", lambda s: s.activation(f["rtmp"][0:1, 128:256], f["rtmp"][0:1, 0:128],
                                         AF.Exp), reads=["rtmp"], writes=["rtmp"])
        k.op("V", lambda v: v.tensor_copy(f["sq16c"][0][0:1, 0:128],
                                          f["rtmp"][0:1, 128:256]),
             reads=["rtmp"], writes=["sq16c0"])

        def flb(t):
            return t.matmul(pb[1][:, 0:128], c["ones16"][0:1, 0:1].tensor.ap()[0:1, 0:1]
                            if False else f["sq16c"][0][0:1, 0:128], f["sq16c"][0][0:1, 0:128],
                            start=True, stop=True)
        k.op("Y", lambda y: y.dma_start(out=f["x16c"][0][0:1, 0:128],
                                        in_=io["ones1_16d"]),
             writes=["x16c0"], dma=True) if h == 0 else None

        def flb2(t, h=h):
            return t.matmul(pb[1][:, 0:128], f["x16c"][0][0:1, 0:128],
                            f["sq16c"][0][0:1, 0:128], start=True, stop=True)
        k.op("T", flb2, reads=["sq16c0", "x16c0"], writes=["pb1"])
        k.op("V", lambda v, h=h: v.tensor_copy(
            f["lambc"][:, h * 128:(h + 1) * 128], pb[1][:, 0:128]),
            reads=["pb1"], writes=["lambc"])
        k.op("V", lambda v, h=h: v.tensor_scalar(
            f["rtmp"], f["trow"], f["negs"][0:1, h:h + 1], None, ALU.mult),
            reads=["trow", "negs"], writes=["rtmp"])
        k.op("S", lambda s: s.activation(f["rtmp"], f["rtmp"], AF.Exp),
             reads=["rtmp"], writes=["rtmp"])
        k.op("V", lambda v: v.tensor_scalar_mul(f["rtmp"], f["rtmp"], SCL128),
             reads=["rtmp"], writes=["rtmp"])
        k.op("G", lambda g, h=h: g.dma_start(out=lamg_d[h:h + 1, :], in_=f["rtmp"]),
             reads=["rtmp"], writes=[f"lamg{h}"], dma=True)

    if "m1v_d" in io:
        k.op("Y", lambda y: y.dma_start(out=f["m1r"], in_=io["m1v_d"]),
             reads=list(io.get("m1deps", ())), writes=["m1r"], dma=True)
        k.op("T", lambda t: t.matmul(pb[1][:, 0:1], c["ones1_32"], f["m1r"],
                                     start=True, stop=True),
             reads=["m1r", "ones1_32"], writes=["pb1"])
        k.op("V", lambda v: v.tensor_copy(f["absacc"][:, 0:1], pb[1][:, 0:1]),
             reads=["pb1"], writes=["qkvscale"])
    else:
        emit_ternarize(k, c, f, io["qkvT_l"], 48, io["wq16d"], "tq")
        k.op("V", lambda v: v.tensor_copy(f["absacc"][:, 0:1], f["m1b"]),
             reads=["tqm1b"], writes=["qkvscale"])
        emit_ternarize(k, c, f, io["projT_l"], 16, outs["wp16d"], "tp",
                       mean_out=outs["m2v"])

    k.op("V", lambda v: v.memset(f["Ssb"], 0.0), writes=["Sall"])
    k.op("V", lambda v: v.memset(f["S16"], 0.0), writes=["S16all"])

    for it in range(4):
        th = it * 512
        emit_ln_half(k, c, f, xr, th, with_gate=False)
        for ch in range(NCH):
            q = ch % 2
            k.op("Y", lambda y, ch=ch, q=q, th=th: y.dma_start(
                out=f["xc"][q], in_=xr[:, ch, th:th + 512]),
                writes=[f"xc{q}"], dma=True)
            k.op("V", lambda v, ch=ch, q=q: v.tensor_tensor(
                f["htmp"][q], f["xc"][q], pb[3][:, :], ALU.mult),
                reads=[f"xc{q}", "pb3"], writes=[f"htmp{q}"])

            def fz(v, ch=ch, q=q):
                dst = f["zn16"][:, ch * 512:(ch + 1) * 512]
                return v.tensor_tensor(dst, f["htmp"][q], pb[4][:, :], ALU.add)
            k.op("V", fz, reads=[f"htmp{q}", "pb4"], writes=[f"zn{ch}"])
            k.op("S", lambda s, ch=ch, q=q: s.activation(
                f["sq16c"][q], f["zn16"][:, ch * 512:(ch + 1) * 512], AF.Square),
                reads=[f"zn{ch}"], writes=[f"sq16c{q}"])

            def fzs(t, ch=ch, q=q):
                return t.matmul(pb[5][0:1, :], c["ones16"], f["sq16c"][q],
                                start=(ch == 0), stop=(ch == NCH - 1))
            k.op("T", fzs, reads=[f"sq16c{q}", "ones16"], writes=["pb5"])
        k.op("S", lambda s: s.activation(f["r2row"], pb[5][0:1, :], AF.Sqrt,
                                         bias=c["kv"][0:1, 1:2], scale=1.0 / D),
             reads=["pb5", "kv"], writes=["r2row"])
        k.op("V", lambda v: v.reciprocal(f["r2row"], f["r2row"]),
             reads=["r2row"], writes=["r2row"])

        def fr2b(t):
            return t.matmul(pb[6][:, :], c["ones1_32"], f["r2row"], start=True, stop=True)
        k.op("T", fr2b, reads=["r2row", "ones1_32"], writes=["pb6"])
        for ch in range(NCH):
            q = ch % 2
            k.op("V", lambda v, ch=ch, q=q: v.tensor_tensor(
                f["htmp"][q], f["zn16"][:, ch * 512:(ch + 1) * 512], pb[6][:, :], ALU.mult),
                reads=[f"zn{ch}", "pb6"], writes=[f"htmp{q}"])

            def fh2(v, ch=ch, q=q):
                dst = f["zn16"][:, ch * 512:(ch + 1) * 512]
                return v.tensor_scalar(dst, f["htmp"][q], f["qnw"][:, ch:ch + 1],
                                       None, ALU.mult)
            k.op("V", fh2, reads=[f"htmp{q}", "qnw"], writes=[f"zn{ch}"])
        zdeps = [f"zn{ch}" for ch in range(NCH)]

        for oc in range(48):
            slot = oc % 4
            q = oc % 2
            k.op("G", lambda g, oc=oc, slot=slot: g.dma_start(
                out=f["wslot"][slot], in_=io["wq16d"][oc]),
                writes=[f"a1ws{slot}"], dma=True)

            def fqkv(t, oc=oc, slot=slot, q=q):
                for ch in range(NCH):
                    last = t.matmul(pb[q][:, :],
                                    f["wslot"][slot][:, ch * 128:(ch + 1) * 128],
                                    f["zn16"][:, ch * 512:(ch + 1) * 512],
                                    start=(ch == 0), stop=(ch == NCH - 1))
                return last
            k.op("T", fqkv, reads=[f"a1ws{slot}"] + zdeps, writes=[f"pb{q}"])

            def fqe(v, oc=oc, q=q):
                dst = f["qkvsb"][:, oc * 512:(oc + 1) * 512]
                return v.tensor_scalar(dst, pb[q][:, :], f["absacc"][:, 0:1],
                                       None, ALU.mult)
            k.op("V", fqe, reads=[f"pb{q}", "qkvscale"], writes=[f"qk{oc}"])
        for h in range(NCH):
            k.op("G", lambda g, h=h, th=th: g.dma_start(
                out=qT[h * 128:(h + 1) * 128, th:th + 512],
                in_=f["qkvsb"][:, h * 512:(h + 1) * 512]),
                reads=[f"qk{h}"], writes=[f"qTo{h}_{it}"], dma=True)

        for cc in range(4):
            for h in range(NCH):
                q = (cc * NCH + h) % 2
                qsl = f["qkvsb"][:, h * 512 + cc * 128: h * 512 + (cc + 1) * 128]
                ksl = f["qkvsb"][:, (16 + h) * 512 + cc * 128: (16 + h) * 512 + (cc + 1) * 128]
                vsl = f["qkvsb"][:, (32 + h) * 512 + cc * 128: (32 + h) * 512 + (cc + 1) * 128]
                rq, rk, rv = f"qk{h}", f"qk{16+h}", f"qk{32+h}"

                k.op("Y", lambda y, vsl=vsl, q=q: y.dma_start(
                    out=f["vct"][q], in_=vsl, transpose=True),
                    reads=[rv], writes=[f"vct{q}"], dma=True)
                k.op("Y", lambda y, ksl=ksl, q=q: y.dma_start(
                    out=f["kct"][q], in_=ksl, transpose=True),
                    reads=[rk], writes=[f"kct{q}"], dma=True)
                k.op("T", lambda t, ksl=ksl, qsl=qsl, q=q: t.matmul(
                    pb[2 + q][:, 0:128], ksl, qsl, start=True, stop=True),
                    reads=[rk, rq], writes=[f"pb{2+q}"])
                k.op("V", lambda v, h=h, q=q: v.tensor_tensor(
                    f["sc16"][q], pb[2 + q][:, 0:128], f["MT"][:, h * 128:(h + 1) * 128],
                    ALU.mult), reads=[f"pb{2+q}", "MT"], writes=[f"sc16{q}"])
                k.op("V", lambda v, h=h, q=q, qsl=qsl: v.tensor_tensor(
                    f["qlb"][q], qsl, f["lambc"][:, h * 128:(h + 1) * 128], ALU.mult),
                    reads=[rq, "lambc"], writes=[f"qlb{q}"])

                def fic(t, h=h, q=q):
                    t.matmul(pb[4 + q][:, 0:128], f["vct"][q], f["sc16"][q],
                             start=True, stop=False)
                    return t.matmul(pb[4 + q][:, 0:128],
                                    f["S16"][:, h * 128:(h + 1) * 128], f["qlb"][q],
                                    start=False, stop=True)
                k.op("T", fic, reads=[f"vct{q}", f"sc16{q}", f"S16h{h}", f"qlb{q}"],
                     writes=[f"pb{4+q}"])
                k.op("V", lambda v, h=h, cc=cc, q=q: v.tensor_scalar(
                    f["ost"][:, h * 512 + cc * 128: h * 512 + (cc + 1) * 128],
                    pb[4 + q][:, 0:128], SCL128, None, ALU.mult),
                    reads=[f"pb{4+q}"], writes=[f"ost{h}"])
                k.op("V", lambda v, h=h, q=q: v.tensor_scalar(
                    f["kct2"][q], f["kct"][q], f["lamrev"][:, h:h + 1], None, ALU.mult),
                    reads=[f"kct{q}", "lamrev"], writes=[f"kct2{q}"])
                k.op("T", lambda t, q=q: t.matmul(
                    pb[6 + q][:, 0:128], f["kct2"][q], f["vct"][q], start=True, stop=True),
                    reads=[f"kct2{q}", f"vct{q}"], writes=[f"pb{6+q}"])
                k.op("V", lambda v, h=h: v.tensor_scalar(
                    f["Ssb"][:, h * 128:(h + 1) * 128],
                    f["Ssb"][:, h * 128:(h + 1) * 128],
                    f["lamT"][:, h:h + 1], None, ALU.mult),
                    reads=[f"Sh{h}", "lamT"], writes=[f"Sh{h}"])
                k.op("V", lambda v, h=h, q=q: v.tensor_tensor(
                    f["Ssb"][:, h * 128:(h + 1) * 128],
                    f["Ssb"][:, h * 128:(h + 1) * 128],
                    pb[6 + q][:, 0:128], ALU.add),
                    reads=[f"Sh{h}", f"pb{6+q}"], writes=[f"Sh{h}"])
                k.op("V", lambda v, h=h: v.tensor_copy(
                    f["S16"][:, h * 128:(h + 1) * 128],
                    f["Ssb"][:, h * 128:(h + 1) * 128]),
                    reads=[f"Sh{h}"], writes=[f"S16h{h}"])
        for h in range(NCH):
            k.op("G", lambda g, h=h, th=th: g.dma_start(
                out=oL[h * 128:(h + 1) * 128, th:th + 512],
                in_=f["ost"][:, h * 512:(h + 1) * 512]),
                reads=[f"ost{h}"], writes=[f"oLo{h}_{it}"], dma=True)

    k.op("G", lambda g: g.dma_start(out=send_d, in_=f["Ssb"]),
         reads=[f"Sh{h}" for h in range(NCH)] + ["Sall"], writes=["sendo"], dma=True)


def prep_attn1_host(inputs):
    qkvw = np.asarray(inputs["L1_attn_qkv_w"], np.float32)
    projw = np.asarray(inputs["L1_attn_proj_w"], np.float32)
    qkvT = np.ascontiguousarray(
        qkvw.T.reshape(NCH, 128, 48, 128).transpose(2, 1, 0, 3).reshape(48, 128, 2048))
    projT = np.ascontiguousarray(
        projw.T.reshape(NCH, 128, NCH, 128).transpose(2, 1, 0, 3).reshape(NCH, 128, 2048))
    tt = np.arange(128, dtype=np.float32)
    D2 = tt[None, :] - tt[:, None]
    B2 = (D2 >= 0).astype(np.float32)
    return {
        "qkvT_l": qkvT, "projT_l": projT,
        "qnw_d": np.ascontiguousarray(
            np.asarray(inputs["L1_attn_qkv_norm_w"], np.float32).reshape(NCH, 128).T),
        "nw_d": np.ascontiguousarray(
            np.asarray(inputs["L1_attn_norm_w"], np.float32).reshape(NCH, 128).T),
        "pnw_d": np.ascontiguousarray(
            np.asarray(inputs["L1_attn_proj_norm_w"], np.float32).reshape(NCH, 128).T),
        "srow_d": np.asarray(inputs["L1_attn_s"], np.float32).reshape(1, 16),
        "D2_d": np.ascontiguousarray(D2), "B2_d": np.ascontiguousarray(B2),
        "trevcol_d": (128.0 - tt).reshape(128, 1).astype(np.float32),
        "trow_d": np.arange(2048, dtype=np.float32).reshape(1, 2048),
        "dummy1": np.zeros((1, 1), np.float32),
        "ones1_16d": np.ones((1, 128), dtype=BF),
    }


def build_attn1a_launch():
    k = KB()
    xin = k.dram_in("xT", [D, TOK])
    dr = _const_dram_decl(k)
    c = alloc_common(k)
    load_consts(k, c, dr)
    io = {
        "qkvT_l": k.dram_in("qkvT_l", [48, 128, 2048]),
        "projT_l": k.dram_in("projT_l", [16, 128, 2048]),
        "qnw_d": k.dram_in("qnw_d", [128, 16]),
        "srow_d": k.dram_in("srow_d", [1, 16]),
        "D2_d": k.dram_in("D2_d", [128, 128]),
        "B2_d": k.dram_in("B2_d", [128, 128]),
        "trevcol_d": k.dram_in("trevcol_d", [128, 1]),
        "trow_d": k.dram_in("trow_d", [1, 2048]),
        "dummy1": k.dram_in("dummy1", [1, 1]),
        "ones1_16d": k.dram_in("ones1_16d", [1, 128], BF16),
        "wq16d": k.dram_tmp("wq16d", [48, 128, 2048], BF16),
    }
    outs = {
        "oL": k.dram_out("oL", [D, TOK], BF16),
        "qT": k.dram_out("qTo", [D, TOK], BF16),
        "lamg": k.dram_out("lamg", [16, TOK]),
        "send": k.dram_out("send", [128, 2048]),
        "wp16d": k.dram_out("wp16d", [16, 128, 2048], BF16),
        "m2v": k.dram_out("m2v", [1, 1]),
    }
    f = alloc_attn1a(k)
    emit_attn1a(k, c, f, io, xin, outs)
    return k.finish()



def alloc_attn1b(k, ar=None):
    sb = _SbAlloc(k, ar)
    f = {}
    f["Sin16"] = sb("b1_Sin16", [128, 16 * 128], BF16)
    f["Sin32"] = sb("b1_Sin32", [128, 16 * 128], F32)
    f["lgrow"] = sb("b1_lgrow", [1, 2048], F32)
    f["lgb"] = sb("b1_lgb", [128, 512], BF16)
    f["qbuf"] = [sb(f"b1_qb{i}", [128, 512], BF16) for i in range(2)]
    f["obuf"] = [sb(f"b1_ob{i}", [128, 512], BF16) for i in range(2)]
    f["qg"] = [sb(f"b1_qg{i}", [128, 512], BF16) for i in range(2)]
    f["of"] = sb("b1_of", [128, 16 * 512], F32)
    f["h2sb"] = sb("b1_h2", [128, 16 * 512], BF16)
    f["sq16c"] = [sb(f"b1_sq{i}", [128, 512], BF16) for i in range(2)]
    f["tb16"] = [sb(f"b1_tb{i}", [128, 512], BF16) for i in range(2)]
    f["rows"] = sb("b1_rows", [1, 2048], F32)
    f["nw"] = sb("b1_nw", [128, 16], F32)
    f["nwpnw"] = sb("b1_nwpnw", [128, 16], F32)
    f["m2b"] = sb("b1_m2b", [128, 1], F32)
    f["wslot"] = [sb(f"b1_ws{i}", [128, 2048], BF16) for i in range(2)]
    f["xc"] = [sb(f"b1_xc{i}", [128, 512], F32) for i in range(2)]
    f["outc"] = [sb(f"b1_outc{i}", [128, 512], F32) for i in range(2)]
    f["htmp"] = [sb(f"b1_ht{i}", [128, 512], F32) for i in range(2)]
    return f


def emit_attn1b(k, c, f, io, xdram, ydram):
    pb = c["pb"]
    xr = xdram.rearrange("(c p) t -> p c t", p=128)
    k.op("Y", lambda y: y.dma_start(out=f["Sin32"], in_=io["sin_d"]),
         reads=list(io.get("sdeps", ())), writes=["Sin32"], dma=True)
    if "flag_d" in io:
        f["b1fr"] = io["_sb"]("b1_fr", [1, 1], F32)
        f["b1fb"] = io["_sb"]("b1_fb", [128, 1], F32)
        k.op("Y", lambda y: y.dma_start(out=f["b1fr"], in_=io["flag_d"]),
             writes=["b1fr"], dma=True)
        k.op("T", lambda t: t.matmul(c["pb"][1][:, 0:1], c["ones1_32"], f["b1fr"],
                                     start=True, stop=True),
             reads=["b1fr", "ones1_32"], writes=["pb1"])
        k.op("V", lambda v: v.tensor_copy(f["b1fb"], c["pb"][1][:, 0:1]),
             reads=["pb1"], writes=["b1fb"])
        k.op("V", lambda v: v.tensor_scalar(f["Sin32"], f["Sin32"], f["b1fb"],
                                            None, ALU.mult),
             reads=["Sin32", "b1fb"], writes=["Sin32"])
    k.op("V", lambda v: v.tensor_copy(f["Sin16"], f["Sin32"]),
         reads=["Sin32"], writes=["Sin16"])
    k.op("Y", lambda y: y.dma_start(out=f["nw"], in_=io["nw_d"]), writes=["nw"], dma=True)
    k.op("Y", lambda y: y.dma_start(out=f["nwpnw"], in_=io["pnw_d"]),
         writes=["nwpnw"], dma=True)
    k.op("V", lambda v: v.tensor_tensor(f["nwpnw"], f["nwpnw"], f["nw"], ALU.mult),
         reads=["nwpnw", "nw"], writes=["nwpnw"])
    k.op("Y", lambda y: y.dma_start(out=f["rows"][0:1, 0:1], in_=io["m2v_d"]),
         writes=["rows"], dma=True)

    def fm2b(t):
        return t.matmul(pb[0][:, 0:1], c["ones1_32"], f["rows"][0:1, 0:1],
                        start=True, stop=True)
    k.op("T", fm2b, reads=["rows", "ones1_32"], writes=["pb0"])
    k.op("V", lambda v: v.tensor_copy(f["m2b"], pb[0][:, 0:1]), reads=["pb0"],
         writes=["m2b"])

    for tb in range(4):
        th = tb * 512
        for h in range(NCH):
            q = h % 2
            if tb == 0 and True:
                pass
            k.op("Y", lambda y, h=h: y.dma_start(
                out=f["lgrow"], in_=io["lamg_d"][h:h + 1, :]),
                writes=["lgrow"], dma=True) if tb == 0 or True else None

            def flgb(t, th=th):
                return t.matmul(pb[2][:, :], c["ones1_32"], f["lgrow"][0:1, th:th + 512],
                                start=True, stop=True)
            k.op("T", flgb, reads=["lgrow", "ones1_32"], writes=["pb2"])
            k.op("V", lambda v: v.tensor_copy(f["lgb"], pb[2][:, :]),
                 reads=["pb2"], writes=["lgb"])
            k.op("Y", lambda y, h=h, th=th, q=q: y.dma_start(
                out=f["qbuf"][q], in_=io["qT_d"][h * 128:(h + 1) * 128, th:th + 512]),
                writes=[f"qb{q}"], dma=True)
            k.op("V", lambda v, q=q: v.tensor_tensor(f["qg"][q], f["qbuf"][q],
                                                     f["lgb"], ALU.mult),
                 reads=[f"qb{q}", "lgb"], writes=[f"qg{q}"])

            def fcor(t, h=h, q=q):
                return t.matmul(pb[4 + q][:, :], f["Sin16"][:, h * 128:(h + 1) * 128],
                                f["qg"][q], start=True, stop=True)
            k.op("T", fcor, reads=["Sin16", f"qg{q}"], writes=[f"pb{4+q}"])
            k.op("Y", lambda y, h=h, th=th, q=q: y.dma_start(
                out=f["obuf"][q], in_=io["oL_d"][h * 128:(h + 1) * 128, th:th + 512]),
                writes=[f"ob{q}"], dma=True)

            def fof(v, h=h, q=q):
                dst = f["of"][:, h * 512:(h + 1) * 512]
                return v.tensor_tensor(dst, f["obuf"][q], pb[4 + q][:, :], ALU.add)
            k.op("V", fof, reads=[f"ob{q}", f"pb{4+q}"], writes=[f"of{h}"])

        for ch in range(NCH):
            q = ch % 2
            k.op("S", lambda s, ch=ch, q=q: s.activation(
                f["sq16c"][q], f["of"][:, ch * 512:(ch + 1) * 512], AF.Square),
                reads=[f"of{ch}"], writes=[f"sq16c{q}"])
            k.op("T", lambda t, ch=ch, q=q: t.matmul(
                pb[6][0:1, :], c["ones16"], f["sq16c"][q],
                start=(ch == 0), stop=(ch == NCH - 1)),
                reads=[f"sq16c{q}", "ones16"], writes=["pb6"])
            k.op("V", lambda v, ch=ch, q=q: v.tensor_scalar(
                f["tb16"][q], f["of"][:, ch * 512:(ch + 1) * 512],
                f["nw"][:, ch:ch + 1], None, ALU.mult),
                reads=[f"of{ch}", "nw"], writes=[f"tb{q}"])
            k.op("S", lambda s, ch=ch, q=q: s.activation(
                f["sq16c"][q], f["tb16"][q], AF.Square),
                reads=[f"tb{q}"], writes=[f"sq16c{q}"])
            k.op("T", lambda t, ch=ch, q=q: t.matmul(
                pb[7][0:1, :], c["ones16"], f["sq16c"][q],
                start=(ch == 0), stop=(ch == NCH - 1)),
                reads=[f"sq16c{q}", "ones16"], writes=["pb7"])

        rw = f["rows"]
        k.op("S", lambda s: s.activation(rw[0:1, 512:1024], pb[6][0:1, :], AF.Sqrt,
                                         bias=c["kv"][0:1, 2:3], scale=1.0 / D),
             reads=["pb6", "kv"], writes=["rows"])
        k.op("V", lambda v: v.reciprocal(rw[0:1, 512:1024], rw[0:1, 512:1024]),
             reads=["rows"], writes=["rows"])
        k.op("V", lambda v: v.tensor_tensor(rw[0:1, 1024:1536], rw[0:1, 512:1024],
                                            rw[0:1, 512:1024], ALU.mult),
             reads=["rows"], writes=["rows"])
        k.op("V", lambda v: v.tensor_scalar(rw[0:1, 1536:2048], pb[7][0:1, :],
                                            1.0 / D, None, ALU.mult),
             reads=["pb7"], writes=["rows"])
        k.op("V", lambda v: v.tensor_tensor(rw[0:1, 1536:2048], rw[0:1, 1536:2048],
                                            rw[0:1, 1024:1536], ALU.mult),
             reads=["rows"], writes=["rows"])
        k.op("S", lambda s: s.activation(rw[0:1, 1536:2048], rw[0:1, 1536:2048], AF.Sqrt,
                                         bias=c["kv"][0:1, 1:2]),
             reads=["rows", "kv"], writes=["rows"])
        k.op("V", lambda v: v.reciprocal(rw[0:1, 1536:2048], rw[0:1, 1536:2048]),
             reads=["rows"], writes=["rows"])
        k.op("V", lambda v: v.tensor_tensor(rw[0:1, 1024:1536], rw[0:1, 512:1024],
                                            rw[0:1, 1536:2048], ALU.mult),
             reads=["rows"], writes=["rows"])

        def frrb(t):
            return t.matmul(pb[2][:, :], c["ones1_32"], rw[0:1, 1024:1536],
                            start=True, stop=True)
        k.op("T", frrb, reads=["rows", "ones1_32"], writes=["pb2"])

        for ch in range(NCH):
            q = ch % 2
            k.op("V", lambda v, ch=ch, q=q: v.tensor_tensor(
                f["htmp"][q], f["of"][:, ch * 512:(ch + 1) * 512], pb[2][:, :], ALU.mult),
                reads=[f"of{ch}", "pb2"], writes=[f"htmp{q}"])

            def fh2(v, ch=ch, q=q):
                dst = f["h2sb"][:, ch * 512:(ch + 1) * 512]
                return v.tensor_scalar(dst, f["htmp"][q], f["nwpnw"][:, ch:ch + 1],
                                       None, ALU.mult)
            k.op("V", fh2, reads=[f"htmp{q}", "nwpnw"], writes=[f"h2{ch}"])
        h2deps = [f"h2{ch}" for ch in range(NCH)]

        for oc in range(NCH):
            ws = oc % 2
            k.op("G", lambda g, oc=oc, ws=ws: g.dma_start(
                out=f["wslot"][ws], in_=io["wp16_d"][oc]),
                writes=[f"b1ws{ws}"], dma=True)
            q = oc % 2

            def fpj(t, oc=oc, ws=ws, q=q):
                for ch in range(NCH):
                    last = t.matmul(pb[4 + q][:, :],
                                    f["wslot"][ws][:, ch * 128:(ch + 1) * 128],
                                    f["h2sb"][:, ch * 512:(ch + 1) * 512],
                                    start=(ch == 0), stop=(ch == NCH - 1))
                return last
            k.op("T", fpj, reads=[f"b1ws{ws}"] + h2deps, writes=[f"pb{4+q}"])
            k.op("Y", lambda y, oc=oc, th=th, q=q: y.dma_start(
                out=f["xc"][q], in_=xr[:, oc, th:th + 512]),
                writes=[f"xc{q}"], dma=True)

            def fout(v, q=q):
                return v.scalar_tensor_tensor(f["outc"][q], pb[4 + q][:, :], f["m2b"],
                                              f["xc"][q], ALU.mult, ALU.add)
            k.op("V", fout, reads=[f"pb{4+q}", f"xc{q}", "m2b"], writes=[f"outc{q}"])
            k.op("G", lambda g, oc=oc, th=th, q=q: g.dma_start(
                out=ydram[oc * 128:(oc + 1) * 128, th:th + 512], in_=f["outc"][q]),
                reads=[f"outc{q}"], writes=[f"b1y{oc}_{th}"], dma=True)


def build_attn1b_launch():
    k = KB()
    xin = k.dram_in("xT", [D, TOK])
    yout = k.dram_out("yT", [D, TOK])
    dr = _const_dram_decl(k)
    c = alloc_common(k)
    load_consts(k, c, dr)
    io = {
        "sin_d": k.dram_in("sin_d", [128, 2048]),
        "lamg_d": k.dram_in("lamg_d", [16, TOK]),
        "qT_d": k.dram_in("qT_d", [D, TOK], BF16),
        "oL_d": k.dram_in("oL_d", [D, TOK], BF16),
        "wp16_d": k.dram_in("wp16_d", [16, 128, 2048], BF16),
        "m2v_d": k.dram_in("m2v_d", [1, 1]),
        "nw_d": k.dram_in("nw_d", [128, 16]),
        "pnw_d": k.dram_in("pnw_d", [128, 16]),
    }
    f = alloc_attn1b(k)
    emit_attn1b(k, c, f, io, xin, yout)
    return k.finish()



_PROGS = {}


def _prog(name, builder):
    if name not in _PROGS:
        _PROGS[name] = builder()
    return _PROGS[name]


def _split_x(x):
    out = []
    for core in range(NCORE):
        b, h = core // 2, core % 2
        out.append(np.ascontiguousarray(x[b, h * 2048:(h + 1) * 2048, :].T))
    return out


def _join_y(parts):
    y = np.empty((B, SEQ, D), np.float32)
    for core in range(NCORE):
        b, h = core // 2, core % 2
        y[b, h * 2048:(h + 1) * 2048, :] = parts[core].T
    return y


def kernel_unfused(**inputs):
    consts = const_host()
    cores = list(range(NCORE))
    x = np.asarray(inputs["L0_attn_x"], np.float32)
    xs = _split_x(x)

    h0 = prep_ffn0_host(inputs)
    nc = _prog("ffn0", build_ffn0_launch)
    maps = [dict(xT=xs[c], **h0, **consts) for c in cores]
    r = run_bass_kernel_spmd(nc, maps, core_ids=cores)
    xs = [np.asarray(r.results[c]["yT"]) for c in cores]

    ha = prep_attn0_host(inputs)
    nc = _prog("attn0pre", build_attn0pre_launch)
    maps = [dict(xT=xs[c], wql=ha["wql"], **consts) for c in cores]
    r = run_bass_kernel_spmd(nc, maps, core_ids=cores)
    wparts = [np.asarray(r.results[c]["wT"]) for c in cores]

    nc = _prog("attn0core", build_attn0core_launch)
    maps = []
    for c in cores:
        b, h = c // 2, c % 2
        maps.append(dict(w0=wparts[2 * b], w1=wparts[2 * b + 1], xT=xs[c],
                         wpl=ha["wpl"], g1=ha["g1"], g2=ha["g2"], g3=ha["g3"],
                         ones1_16d=np.ones((1, 128), dtype=BF),
                         flag_d=np.full((1, 1), float(h), np.float32), **consts))
    r = run_bass_kernel_spmd(nc, maps, core_ids=cores)
    xs = [np.asarray(r.results[c]["yT"]) for c in cores]

    nc = _prog("ffn0", build_ffn0_launch)
    maps = [dict(xT=xs[c], **h0, **consts) for c in cores]
    r = run_bass_kernel_spmd(nc, maps, core_ids=cores)
    xs = [np.asarray(r.results[c]["yT"]) for c in cores]

    nc = _prog("dequant", build_dequant_launch)
    maps = [prep_dequant_host(inputs, c) for c in cores]
    r = run_bass_kernel_spmd(nc, maps, core_ids=cores)
    gu16 = np.concatenate([np.asarray(r.results[c]["gu_out"]) for c in cores], axis=0)
    dn16 = np.concatenate([np.asarray(r.results[c]["dn_out"]) for c in cores], axis=0)
    h1 = prep_ffn1_host(gu16, dn16)

    nc = _prog("ffn1", build_ffn1_launch)
    maps = [dict(xT=xs[c], **h1, **consts) for c in cores]
    r = run_bass_kernel_spmd(nc, maps, core_ids=cores)
    xs = [np.asarray(r.results[c]["yT"]) for c in cores]

    hb = prep_attn1_host(inputs)
    nc = _prog("attn1a", build_attn1a_launch)
    a_keys = ("qkvT_l", "projT_l", "qnw_d", "srow_d", "D2_d", "B2_d",
              "trevcol_d", "trow_d", "dummy1", "ones1_16d")
    maps = [dict(xT=xs[c], **{kk: hb[kk] for kk in a_keys}, **consts) for c in cores]
    ra = run_bass_kernel_spmd(nc, maps, core_ids=cores)

    nc = _prog("attn1b", build_attn1b_launch)
    maps = []
    for c in cores:
        b, h = c // 2, c % 2
        partner = 2 * b + (1 - h)
        res = ra.results[c]
        sin = (np.asarray(ra.results[partner]["send"], np.float32) if h == 1
               else np.zeros((128, 2048), np.float32))
        maps.append(dict(xT=xs[c], sin_d=sin, lamg_d=np.asarray(res["lamg"]),
                         qT_d=np.asarray(res["qTo"]), oL_d=np.asarray(res["oL"]),
                         wp16_d=np.asarray(res["wp16d"]), m2v_d=np.asarray(res["m2v"]),
                         nw_d=hb["nw_d"], pnw_d=hb["pnw_d"], **consts))
    r = run_bass_kernel_spmd(nc, maps, core_ids=cores)
    xs = [np.asarray(r.results[c]["yT"]) for c in cores]

    nc = _prog("ffn1", build_ffn1_launch)
    maps = [dict(xT=xs[c], **h1, **consts) for c in cores]
    r = run_bass_kernel_spmd(nc, maps, core_ids=cores)
    xs = [np.asarray(r.results[c]["yT"]) for c in cores]

    return _join_y(xs)



def build_fused():
    k = KB()
    x0 = k.dram_in("xT", [D, TOK])
    yout = k.dram_out("yT", [D, TOK])
    t = [k.dram_tmp(f"t{i}", [D, TOK]) for i in range(6)]
    dr = _const_dram_decl(k)
    c = alloc_common(k)
    load_consts(k, c, dr)
    flag_d = k.dram_in("flag_d", [1, 1])

    io0 = decl_ffn0_inputs(k)
    wql = k.dram_in("wql", [NCH, 128, 2048])
    wpl = k.dram_in("wpl", [NCH, 128, 2048])
    g1 = k.dram_in("g1", [128, 1])
    g2 = k.dram_in("g2", [128, 1])
    g3 = k.dram_in("g3", [128, 1])
    ones1_16d = k.dram_in("ones1_16d", [1, 128], BF16)
    gu_pk = k.dram_in("gu_pk", [1408, 1024], U8)
    gu_sc = k.dram_in("gu_sc", [1408, 16], mybir.dt.float16)
    dn_pk = k.dram_in("dn_pk", [256, 2816], U8)
    dn_sc = k.dram_in("dn_sc", [256, 44], mybir.dt.float16)
    a1in = {
        "qkvT_l": k.dram_in("qkvT_l", [48, 128, 2048]),
        "projT_l": k.dram_in("projT_l", [16, 128, 2048]),
        "qnw_d": k.dram_in("qnw_d", [128, 16]),
        "srow_d": k.dram_in("srow_d", [1, 16]),
        "D2_d": k.dram_in("D2_d", [128, 128]),
        "B2_d": k.dram_in("B2_d", [128, 128]),
        "trevcol_d": k.dram_in("trevcol_d", [128, 1]),
        "trow_d": k.dram_in("trow_d", [1, 2048]),
        "dummy1": k.dram_in("dummy1", [1, 1]),
        "ones1_16d": ones1_16d,
        "wq16d": k.dram_tmp("wq16d", [48, 128, 2048], BF16),
    }
    nw_d = k.dram_in("nw_d", [128, 16])
    pnw_d = k.dram_in("pnw_d", [128, 16])

    ccw_in = k.dram_tmp("ccw_in", [D, TOK], BF16)
    ccw_out = k.dram_tmp("ccw_out", [2 * D, TOK], BF16)
    ccs_in = k.dram_tmp("ccs_in", [128, 2048])
    ccs_out = k.dram_tmp("ccs_out", [256, 2048])
    ccgu_in = k.dram_tmp("ccgu_in", [11 * 128, 2048], BF16)
    ccgu_out = k.dram_tmp("ccgu_out", [88 * 128, 2048], BF16)
    ccdn_in = k.dram_tmp("ccdn_in", [2 * 128, 44 * 128], BF16)
    ccdn_out = k.dram_tmp("ccdn_out", [16 * 128, 44 * 128], BF16)
    oL_d = k.dram_tmp("oL_s", [D, TOK], BF16)
    qT_d = k.dram_tmp("qT_s", [D, TOK], BF16)
    lamg_d = k.dram_tmp("lamg_s", [16, TOK])
    wp16d = k.dram_tmp("wp16_s", [16, 128, 2048], BF16)
    m2v_d = k.dram_tmp("m2v_s", [1, 1])
    m1v_d = k.dram_tmp("m1v_s", [1, 1])

    ar = Arena(k, "arena", 200 * 1024)
    PAIRS = [[0, 1], [2, 3], [4, 5], [6, 7]]
    ALL8 = [list(range(8))]

    f0 = alloc_ffn0(k, ar)
    emit_ffn0_init(k, c, f0, io0)
    emit_ffn0(k, c, f0, io0, x0, t[1], uid="fa")
    k.fence(); ar.reset()

    fap = alloc_attn0pre(k, ar)
    fdq = alloc_dequant2(k, ar)
    dq_tiles = []
    for r in range(11):
        dq_tiles.append((gu_pk[r * 128:(r + 1) * 128, :],
                         gu_sc[r * 128:(r + 1) * 128, :],
                         ccgu_in[r * 128:(r + 1) * 128, :], 1024, 16))
    for r in range(2):
        for half in range(2):
            dq_tiles.append((
                dn_pk[r * 128:(r + 1) * 128, half * 1408:(half + 1) * 1408],
                dn_sc[r * 128:(r + 1) * 128, half * 22:(half + 1) * 22],
                ccdn_in[r * 128:(r + 1) * 128, half * 2816:(half + 1) * 2816],
                1408, 22))
    dq_chunks, dq_res = make_dequant_chunks(k, fdq, dq_tiles)
    dq_pacer = Pacer(dq_chunks, 4 * 33)
    emit_attn0pre(k, c, fap, {"wql": wql}, t[1], ccw_in, filler=dq_pacer)
    dq_pacer.flush()
    wres = [f"wout{oc}_{th}" for oc in range(NCH) for th in (0, 512, 1024, 1536)]
    k.cc("AllGather", ALU.bypass, PAIRS, [ccw_in.tensor.ap()], [ccw_out.tensor.ap()],
         reads=wres, writes=["wex"])
    k.cc("AllGather", ALU.bypass, ALL8, [ccgu_in.tensor.ap()], [ccgu_out.tensor.ap()],
         reads=dq_res, writes=["guT"])
    k.cc("AllGather", ALU.bypass, ALL8, [ccdn_in.tensor.ap()], [ccdn_out.tensor.ap()],
         reads=dq_res, writes=["dnT"])
    k.fence(); ar.reset()

    fac = alloc_attn0core(k, ar)
    ftn = alloc_tern(k, ar)
    tn_chunks = (
        make_tern_chunks(k, c, ftn, a1in["qkvT_l"], 48, a1in["wq16d"], "tq",
                         mean_out=m1v_d)
        + make_tern_chunks(k, c, ftn, a1in["projT_l"], 16, wp16d, "tp",
                           mean_out=m2v_d))
    tn_pacer = Pacer(tn_chunks, 16 * 9)
    ioac = {"wpl": wpl, "g1": g1, "g2": g2, "g3": g3, "ones1_16d": ones1_16d,
            "flag_d": flag_d, "wdeps": ("wex",)}
    emit_attn0core(k, c, fac, ioac, ccw_out[0:D, :], ccw_out[D:2 * D, :],
                   t[1], t[2], filler=tn_pacer)
    tn_pacer.flush()
    k.fence(); ar.reset()

    f0b = alloc_ffn0(k, ar)
    emit_ffn0_init(k, c, f0b, io0)
    emit_ffn0(k, c, f0b, io0, t[2], t[3], uid="fb")
    k.fence(); ar.reset()

    f1 = alloc_ffn1(k, ar)
    io1 = {"guT_l": ccgu_out.tensor.rearrange("(a p) n -> a p n", p=128),
           "dnT_l": ccdn_out.tensor.rearrange("(a p) n -> a p n", p=128)}
    emit_ffn1(k, c, f1, io1, t[3], t[4], uid="ga", wdeps=("guT", "dnT"))
    k.fence(); ar.reset()

    fa1 = alloc_attn1a(k, ar)
    outs1 = {"oL": oL_d, "qT": qT_d, "lamg": lamg_d, "send": ccs_in,
             "wp16d": wp16d, "m2v": m2v_d}
    a1in["m1v_d"] = m1v_d
    emit_attn1a(k, c, fa1, a1in, t[4], outs1)
    k.cc("AllGather", ALU.bypass, PAIRS, [ccs_in.tensor.ap()], [ccs_out.tensor.ap()],
         reads=["sendo"], writes=["sex"])
    k.fence(); ar.reset()

    fb1 = alloc_attn1b(k, ar)
    iob = {"sin_d": ccs_out[0:128, :], "lamg_d": lamg_d, "qT_d": qT_d,
           "oL_d": oL_d, "wp16_d": wp16d, "m2v_d": m2v_d, "nw_d": nw_d,
           "pnw_d": pnw_d, "sdeps": ("sex",), "flag_d": flag_d,
           "_sb": _SbAlloc(k, ar)}
    emit_attn1b(k, c, fb1, iob, t[4], t[5])
    k.fence(); ar.reset()

    f1b = alloc_ffn1(k, ar)
    emit_ffn1(k, c, f1b, io1, t[5], yout, uid="gb", wdeps=("guT", "dnT"))
    return k.finish()


def kernel(**inputs):
    consts = const_host()
    cores = list(range(NCORE))
    x = np.asarray(inputs["L0_attn_x"], np.float32)
    xs = _split_x(x)

    h0 = prep_ffn0_host(inputs)
    ha = prep_attn0_host(inputs)
    hb = prep_attn1_host(inputs)
    nc = _prog("fused", build_fused)
    maps = []
    for cr in cores:
        b, h = cr // 2, cr % 2
        m = dict(xT=xs[cr], flag_d=np.full((1, 1), float(h), np.float32),
                 wql=ha["wql"], wpl=ha["wpl"], g1=ha["g1"], g2=ha["g2"], g3=ha["g3"],
                 ones1_16d=np.ones((1, 128), dtype=BF))
        m.update(h0)
        m.update(prep_dequant_host(inputs, cr))
        m.update({kk: hb[kk] for kk in ("qkvT_l", "projT_l", "qnw_d", "srow_d",
                                        "D2_d", "B2_d", "trevcol_d", "trow_d",
                                        "dummy1", "nw_d", "pnw_d")})
        m.update(consts)
        maps.append(m)
    r = run_bass_kernel_spmd(nc, maps, core_ids=cores)
    return _join_y([np.asarray(r.results[cr]["yT"]) for cr in cores])
